# version 06
# speedup vs baseline: 1.1802x; 1.0679x over previous
import contextlib
import numpy as np
import ml_dtypes
import concourse.bass as bass
import concourse.mybir as mybir
from concourse.bass_utils import run_bass_kernel_spmd

F32 = mybir.dt.float32
BF16 = mybir.dt.bfloat16
I32 = mybir.dt.int32
AF = mybir.ActivationFunctionType
ALU = mybir.AluOpType
AX = mybir.AxisListType.X
BF = ml_dtypes.bfloat16

NCORES = 8
T = 2048
DM = 1024
DC = DM // 128
NTB = T // 512
NLAYERS = 4
L2STOP = 0
DBG = False
EPS = 1e-5
LN2 = float(np.log(2.0))
RG = [list(range(NCORES))]


class Sem:
    __slots__ = ("h", "v")
    def __init__(self, h):
        self.h = h
        self.v = 0


class Rec:
    def __init__(self, q, own_sem=None):
        object.__setattr__(self, "_q", q)
        object.__setattr__(self, "_own", own_sem)

    def set_own(self, sem):
        object.__setattr__(self, "_own", sem)

    def wait(self, sem, val=None):
        v = sem.v if val is None else val
        assert v <= sem.v, "wait on a future semaphore value"
        if v <= 0:
            return
        self._q.append(lambda e, s=sem.h, w=v: e.wait_ge(s, w))

    def __getattr__(self, meth):
        def call(*args, inc=None, n=1, **kw):
            if meth == "matmul":
                kw.setdefault("skip_group_check", True)
            own = self._own
            if own is not None:
                if own.v > 0:
                    self._q.append(
                        lambda e, s=own.h, w=own.v: e.wait_ge(s, w))
                if inc is None:
                    inc, n = own, 1
            if inc is not None:
                inc.v += n
                self._q.append(lambda e, a=args, k=kw, s=inc.h, m=n:
                               getattr(e, meth)(*a, **k).then_inc(s, m))
            else:
                self._q.append(lambda e, a=args, k=kw: getattr(e, meth)(*a, **k))
        return call


def build(nlayers=NLAYERS, dbg=None):
    nc = bass.Bass("TRN2", target_bir_lowering=False)
    es = contextlib.ExitStack()
    if dbg is None:
        dbg = DBG
    Q = {k: [] for k in ("te", "ve", "sc", "gp", "sy")}
    TE, VE, SC, GP, SY = (Rec(Q[k]) for k in ("te", "ve", "sc", "gp", "sy"))

    def inp(name, shape, dt=F32):
        return nc.dram_tensor(name, list(shape), dt, kind="ExternalInput").ap()

    def idram(name, shape, dt=F32):
        return nc.dram_tensor(name, list(shape), dt, kind="Internal")

    x0 = inp("x0_fm", (DM, T))
    out_fm = nc.dram_tensor("out_fm", [DM, T], F32, kind="ExternalOutput").ap()
    dbg_d = nc.dram_tensor("dbg", [128, 8192], F32, kind="ExternalOutput").ap()

    wd_d = inp("L0_wd", (DM, DM))
    bd_d = inp("L0_bd", (DM, 1))
    wb_d = inp("L0_wb", (DM, 16))
    bb_d = inp("L0_bb", (16, 1))
    wc_d = inp("L0_wc", (DM, 16))
    bc_d = inp("L0_bc", (16, 1))
    wo_d = inp("L0_wo", (DM, DM))
    bo_d = inp("L0_bo", (DM, 1))
    at_d = inp("L0_at", (16, DM))
    clipx_d = inp("L0_clipx", (1, 1))
    clipd_d = inp("L0_clipd", (1, 1))
    wg_d = inp("L0_wg", (2816, DM))
    wu_d = inp("L0_wu", (2816, DM))
    wdn_d = inp("L0_wdn", (DM, 2816))
    sel_d = inp("sel_bc", (16, 8))

    l1 = {}
    if True:
        l1["wq"] = inp("L1_wq", (DM, 128))
        l1["wk"] = inp("L1_wk", (DM, 128))
        l1["wv"] = inp("L1_wv", (DM, 128))
        l1["bq"] = inp("L1_bq", (128, 1))
        l1["bk"] = inp("L1_bk", (128, 1))
        l1["bv"] = inp("L1_bv", (1, 128))
        l1["wo"] = inp("L1_wo", (64, 2 * DM))
        l1["bo"] = inp("L1_bo", (128, 8))
        l1["w1T"] = inp("L1_w1T", (DM, 4096))
        l1["b1"] = inp("L1_b1", (128, 32))
        l1["w2T"] = inp("L1_w2T", (4096, DM))
        l1["b2"] = inp("L1_b2", (128, 8))

    l2 = {}
    l2["tk"] = inp("L2_tk", (DM, DM))
    l2["tkT"] = inp("L2_tkT", (DM, DM))
    l2["tvT"] = inp("L2_tvT", (DM, DM))
    l2["tqT"] = inp("L2_tqT", (DM, DM))
    l2["w1T"] = inp("L2_w1T", (DM, 256))
    l2["b1"] = inp("L2_b1", (1, 256))
    l2["w2T"] = inp("L2_w2T", (256, DM))
    l2["b2"] = inp("L2_b2", (1, DM))
    l2["flc"] = inp("L2_flc", (128, 4))
    l2["fl1"] = inp("L2_fl1", (128, 4))
    l2["wq"] = inp("L2f_wq", (DM, 128))
    l2["wk"] = inp("L2f_wk", (DM, 128))
    l2["wv"] = inp("L2f_wv", (DM, 128))
    l2["wproj"] = inp("L2f_wproj", (64, 2 * DM))
    l2["bproj"] = inp("L2f_bproj", (128, 8))
    l2["qg"] = inp("L2f_qg", (128, 4))

    l3 = {}
    l3["wq"] = inp("L3_wq", (DM, 128))
    l3["wk"] = inp("L3_wk", (DM, 128))
    l3["wv"] = inp("L3_wv", (DM, 128))
    l3["wout"] = inp("L3_wout", (64, 2 * DM))
    l3["bout"] = inp("L3_bout", (128, 8))
    l3["rmbd"] = inp("L3_rmbd", (32 * 128, 128))
    l3["rubd"] = inp("L3_rubd", (32 * 128, 512))
    l3["wcT"] = inp("L3_wcT", (16384, DM))

    cc_pub1 = idram("cc_pub1", (64, 1))
    cc_ag1 = idram("cc_ag1", (NCORES * 64, 1))
    cc_pub2 = idram("cc_pub2", (8, 1))
    cc_ag2 = idram("cc_ag2", (NCORES * 8, 1))
    h_hbm = idram("h_hbm", (2816, T), BF16)
    wdqT_hbm = idram("wdqT_hbm", (2816, DM), BF16)
    xn_pub = idram("xn_pub", (DM, T), BF16)
    xn_ag = idram("xn_ag", (NCORES * DM, T), BF16)
    po_hbm = idram("po_hbm", (NCORES * DM, T), BF16)
    rs_out = idram("rs_out", (DM, T), BF16)
    w2q_hbm = idram("w2q_hbm", (4096, DM), BF16)
    hq_hbm = idram("hq_hbm", (4096, T), BF16)
    z_hbm = idram("z_hbm", (T, DM), BF16)
    zy_hbm = idram("zy_hbm", (T, DM), BF16)
    q2_hbm = idram("q2_hbm", (DM, T), BF16)
    g_hbm = idram("g_hbm", (T, DM), BF16)
    g_pub = idram("g_pub", (514, 1024))
    g_ar = idram("g_ar", (514, 1024))
    sim_pub = idram("sim_pub", (1, 16384))
    sim_ar = idram("sim_ar", (1, 16384))
    mq_hbm = idram("mq_hbm", (1, 16384))
    gg_hbm = idram("gg_hbm", (32 * 128, 512), BF16)
    h3_hbm = idram("h3_hbm", (16384, T), BF16)
    wc_bh = idram("wc_bh", (16384, DM), BF16)

    c_ones_f = nc.inline_tensor(np.ones((128, 128), np.float32), name="c1f")
    c_zeros = nc.inline_tensor(np.zeros((128, 256), np.float32), name="c0f")
    c_id_f = nc.inline_tensor(np.eye(128, dtype=np.float32), name="cidf")
    c_od_bf = nc.inline_tensor(
        np.full((128, 128), 1.0 / DM, np.float32).astype(BF), name="codb")
    c_id_bf = nc.inline_tensor(
        np.eye(128, dtype=np.float32).astype(BF), name="cidb")
    _msk = np.zeros((128, 4 * 512), np.float32)
    for m in range(4):
        i = np.arange(128)[:, None]
        j = np.arange(512)[None, :]
        _msk[:, m * 512:(m + 1) * 512] = (128 * m + i <= j)
    c_masks = nc.inline_tensor(_msk.astype(BF), name="cmsk")
    _ho = np.zeros((128, 2), np.float32)
    _ho[0:64, 0] = 1.0
    _ho[64:128, 1] = 1.0
    c_halfones = nc.inline_tensor(_ho, name="chalf")
    _reg = np.zeros((128, 128), np.float32)
    for _b in range(4):
        for _r in range(5):
            _t = 4096 * _b + _r
            _reg[_t // 128, _t % 128] = 1.0
    c_reg = nc.inline_tensor(_reg, name="creg")
    c_ones_bf = nc.inline_tensor(np.ones((128, 1), np.float32).astype(BF),
                                 name="c1bf")
    _ln2 = float(np.log(2.0))
    _qc = np.zeros((128, 10), np.float32)
    for _i, _v in enumerate([_ln2, -_ln2, 3 * _ln2, -3 * _ln2,
                             10 * _ln2, -10 * _ln2, 0.125, 2.0 ** -40,
                             -0.5, 0.0]):
        _qc[:, _i] = _v
    c_qc = nc.inline_tensor(_qc, name="cqc")


    with es:
        def sb(name, shape, dt):
            return es.enter_context(nc.sbuf_tensor(name, shape, dt))
        def psum(name, shape):
            return es.enter_context(nc.psum_tensor(name, shape, F32))
        def sem(name):
            return Sem(es.enter_context(nc.semaphore(name)))

        x32 = sb("x32", [128, DC * T], F32)
        bfc = sb("bfc", [128, DC * T], BF16)
        bfa = sb("bfa", [128, 12288], BF16)
        bfb = sb("bfb", [128, 6144], BF16)
        fpw = sb("fpw", [128, 4096], F32)
        s16 = sb("s16", [16, 6144], F32)
        bfo = sb("bfo", [64, 2048], BF16)
        mskb = sb("mskb", [128, 2048], BF16)
        half = sb("half", [128, 2], F32)
        ctb = sb("ctb", [16, 2048], BF16)
        vecs = sb("vecs", [128, 512], F32)
        cons = sb("cons", [128, 256], F32)
        consb = sb("consb", [128, 256], BF16)
        tmpA = sb("tmpA", [128, 1088], F32)
        tmpB = sb("tmpB", [128, 1536], F32)
        tmpC = sb("tmpC", [128, 512], F32)

        psA = psum("psA", [128, 2048])
        psB = psum("psB", [128, 512])
        psC = psum("psC", [128, 512])
        psD = psum("psD", [128, 512])
        psE = psum("psE", [128, 512])

        s_ld = sem("s_ld"); s_gp = sem("s_gp"); s_cc = sem("s_cc")
        s_mm = sem("s_mm"); s_ve = sem("s_ve"); s_sc = sem("s_sc")
        s_st = sem("s_st")
        s_pc = sem("s_pc"); s_lo = sem("s_lo"); s_go = sem("s_go")
        VE.set_own(s_ve)
        SC.set_own(s_sc)

        _vs = [0]
        def vslot():
            i = _vs[0]; _vs[0] += 1
            assert i < 256
            return vecs[:, i:i + 1]
        def vslot1():
            return vslot()[0:1, :]

        ONES_COL = cons[:, 0:1]
        ONES_ROW = cons[0:1, 0:128]
        ID_F = cons[:, 128:256]
        OD_BF = consb[:, 0:128]
        ID_BF = consb[:, 128:256]

        SY.dma_start(out=cons[:, 0:128], in_=c_ones_f.ap(), inc=s_ld, n=16)
        SY.dma_start(out=cons[:, 128:256], in_=c_id_f.ap(), inc=s_ld, n=16)
        SY.dma_start(out=consb[:, 0:128], in_=c_od_bf.ap(), inc=s_ld, n=16)
        SY.dma_start(out=consb[:, 128:256], in_=c_id_bf.ap(), inc=s_ld, n=16)
        SY.dma_start(out=mskb[:, :], in_=c_masks.ap(), inc=s_ld, n=16)
        SY.dma_start(out=half[:, :], in_=c_halfones.ap(), inc=s_ld, n=16)
        qcn = sb("qcn", [128, 10], F32)
        SY.dma_start(out=qcn[:, :], in_=c_qc.ap(), inc=s_ld, n=16)
        regt = sb("regt", [128, 128], F32)
        SY.dma_start(out=regt[:, :], in_=c_reg.ap(), inc=s_ld, n=16)
        onebf = sb("onebf", [128, 1], BF16)
        SY.dma_start(out=onebf[:, :], in_=c_ones_bf.ap(), inc=s_ld, n=16)
        zstr = sb("zstr", [128, 2048], BF16)
        for dc in range(DC):
            SY.dma_start(out=x32[:, dc * T:(dc + 1) * T],
                         in_=x0[dc * 128:(dc + 1) * 128, :], inc=s_ld, n=16)

        _tap = [0]
        tapst = sb("tapst", [128, 512], F32) if dbg else None
        def tap(ap, rows=128, cols=512):
            i = _tap[0]
            if i >= 16:
                return
            _tap[0] += 1
            VE.tensor_copy(tapst[0:rows, 0:cols], ap, inc=s_ve)
            SY.wait(s_ve)
            SY.dma_start(out=dbg_d[0:rows, i * 512:i * 512 + cols],
                         in_=tapst[0:rows, 0:cols], inc=s_st, n=16)
            VE.wait(s_st)

        def fm(tsr, dc, tb=None):
            if tb is None:
                return tsr[:, dc * T:(dc + 1) * T]
            return tsr[:, dc * T + tb * 512: dc * T + (tb + 1) * 512]

        def resadd(pss, dc, tb, bias_ap=None, scale_ap=None):
            t = tmpB[:, 1024:1536]
            if scale_ap is not None:
                VE.tensor_scalar(t, pss, scale_ap, None, ALU.mult, inc=s_ve)
            elif bias_ap is not None:
                VE.tensor_scalar(t, pss, bias_ap, None, ALU.add, inc=s_ve)
            else:
                VE.tensor_copy(t, pss, inc=s_ve)
            VE.tensor_add(fm(x32, dc, tb), fm(x32, dc, tb), t, inc=s_ve)

        def bcast_col(src11, dst_col):
            TE.wait(s_ve)
            TE.matmul(psE[:, 0:1], ONES_ROW, src11, start=True, stop=True,
                      inc=s_mm)
            VE.wait(s_mm)
            VE.tensor_copy(dst_col, psE[:, 0:1], inc=s_ve)

        def pmax_to_scalar(col128, dst11):
            TE.wait(s_ve)
            TE.transpose(psE[0:1, 0:128], col128, ID_F, inc=s_mm)
            VE.wait(s_mm)
            VE.reduce_max(dst11, psE[0:1, 0:128], AX, inc=s_ve)

        def emit_layernorm(src32, dstbf, absmax=None):
            TE.wait(s_ve)
            SC.wait(s_ve)
            m_ = tmpA[0:1, 0:512]
            t_ = tmpA[0:1, 512:1024]
            q_ = tmpB[0:1, 0:512]
            r_ = tmpB[0:1, 512:1024]
            for tb in range(NTB):
                for dc in range(DC):
                    TE.matmul(psB[0:1, :], ONES_COL, fm(src32, dc, tb),
                              start=(dc == 0), stop=(dc == DC - 1), inc=s_mm)
                for dc in range(DC):
                    SC.wait(s_mm)
                    SC.activation(tmpC[:, 0:512], fm(src32, dc, tb), AF.Square,
                                  inc=s_sc)
                    TE.wait(s_sc)
                    TE.matmul(psC[0:1, :], ONES_COL, tmpC[:, 0:512],
                              start=(dc == 0), stop=(dc == DC - 1), inc=s_mm)
                VE.wait(s_mm)
                VE.tensor_scalar(m_, psB[0:1, :], 1.0 / DM, None, ALU.mult,
                                 inc=s_ve)
                VE.tensor_scalar(t_, psC[0:1, :], 1.0 / DM, None, ALU.mult,
                                 inc=s_ve)
                VE.tensor_mul(q_, m_, m_, inc=s_ve)
                VE.tensor_sub(t_, t_, q_, inc=s_ve)
                VE.tensor_scalar(t_, t_, EPS, None, ALU.add, inc=s_ve)
                VE.reciprocal(t_, t_, inc=s_ve)
                SC.wait(s_ve)
                SC.activation(r_, t_, AF.Sqrt, inc=s_sc)
                VE.wait(s_sc)
                VE.tensor_mul(m_, m_, r_, inc=s_ve)
                TE.wait(s_ve)
                TE.matmul(psD[:, :], ONES_ROW, r_, start=True, stop=True,
                          inc=s_mm)
                TE.matmul(psE[:, :], ONES_ROW, m_, start=True, stop=True,
                          inc=s_mm)
                VE.wait(s_mm)
                for dc in range(DC):
                    VE.tensor_mul(tmpB[:, 0:512], fm(src32, dc, tb), psD[:, :],
                                  inc=s_ve)
                    VE.tensor_sub(fm(dstbf, dc, tb), tmpB[:, 0:512], psE[:, :],
                                  inc=s_ve)
                TE.wait(s_ve)
            if absmax is not None:
                t = vslot(); acc = vslot()
                VE.memset(acc, 0.0, inc=s_ve)
                for dc in range(DC):
                    VE.tensor_reduce(t, fm(dstbf, dc), AX, ALU.max,
                                     apply_absolute_value=True, inc=s_ve)
                    VE.tensor_tensor(acc, acc, t, ALU.max, inc=s_ve)
                pmax_to_scalar(acc, absmax)

        RNE_C = 12582912.0

        def quant_int8(dst_bf, src, inv_ap, w):
            VE.tensor_scalar(tmpA[:, 0:w], src, inv_ap, RNE_C, ALU.mult,
                             ALU.add, inc=s_ve)
            VE.tensor_scalar(tmpA[:, 0:w], tmpA[:, 0:w], -RNE_C, -128.0,
                             ALU.add, ALU.max, inc=s_ve)
            VE.tensor_scalar(dst_bf, tmpA[:, 0:w], 127.0, None, ALU.min,
                             inc=s_ve)

        def st_barrier():
            VE.wait(s_st)
            VE.tensor_copy(tmpA[0:1, 1087:1088], ONES_COL[0:1, :])
            return s_ve.v

        def emit_l0():
            xnb = bfc
            a_sb = s16[:, 0:2048]
            b_sb = s16[:, 2048:4096]
            hs_sb = s16[:, 4096:6144]
            pp_sb = b_sb
            bd_ap = [vslot() for _ in range(DC)]
            bb_ap = vecs[0:16, 30:31]
            bc_ap = vecs[0:16, 31:32]
            sa_ap = vecs[0:16, 32:33]
            hin_ap = vecs[0:16, 33:34]
            xnmax = vecs[0:1, 34:35]
            _vs[0] = 35
            hmax_acc = vslot()
            bo_ap = [vslot() for _ in range(DC)]
            invx_bc = vslot()
            sxc_bc = vslot()
            inv2_bc = vslot()
            sdc_bc = vslot()
            sd_ap = [vslot() for _ in range(DC)]
            sg_ap = vslot(); su_ap = vslot()
            combg = vslot(); combu = vslot()
            comb2 = [vslot() for _ in range(DC)]
            clipx = vslot1(); clipd = vslot1()
            gmax = vslot1(); sxr = vslot1(); sxinv = vslot1(); sx127 = vslot1()
            ghm = vslot1(); s2r = vslot1(); s2inv = vslot1(); s2127 = vslot1()
            hmax_sc = vslot1()
            hgath = vecs[0:16, 100:108]
            mgath = vecs[0:1, 116:124]
            m2gath = vecs[0:1, 124:132]
            sel_sb = vecs[0:16, 108:116]

            SY.dma_start(out=tmpA[0:16, 0:1024], in_=at_d, inc=s_ld, n=16)
            SY.dma_start(out=sel_sb, in_=sel_d, inc=s_ld, n=16)
            SY.dma_start(out=clipx, in_=clipx_d, inc=s_ld, n=16)
            SY.dma_start(out=clipd, in_=clipd_d, inc=s_ld, n=16)
            for dc in range(DC):
                SY.dma_start(out=bd_ap[dc], in_=bd_d[dc * 128:(dc + 1) * 128, :],
                             inc=s_ld, n=16)
                SY.dma_start(out=bo_ap[dc], in_=bo_d[dc * 128:(dc + 1) * 128, :],
                             inc=s_ld, n=16)
            SY.dma_start(out=bb_ap, in_=bb_d, inc=s_ld, n=16)
            SY.dma_start(out=bc_ap, in_=bc_d, inc=s_ld, n=16)
            VE.wait(s_ld)
            VE.reduce_sum(sa_ap, tmpA[0:16, 0:1024], AX, inc=s_ve)
            SC.wait(s_ve)
            SC.activation(sa_ap, sa_ap, AF.Sigmoid, scale=1.0 / DM, inc=s_sc)
            SC.activation(clipx, clipx, AF.Sigmoid, inc=s_sc)
            SC.activation(clipd, clipd, AF.Sigmoid, inc=s_sc)

            TE.wait(s_ld)
            VE.wait(s_ld)
            GP.wait(s_ld)
            emit_layernorm(x32, xnb, absmax=xnmax)

            if dbg:
                tap(fm(xnb, 0, 0))
            for dc in range(DC):
                GP.dma_start(out=bfa[:, dc * 1024:(dc + 1) * 1024],
                             in_=wd_d[dc * 128:(dc + 1) * 128, :],
                             inc=s_gp, n=16)
                GP.dma_start(out=bfb[:, 1024 + dc * 16:1024 + (dc + 1) * 16],
                             in_=wb_d[dc * 128:(dc + 1) * 128, :],
                             inc=s_gp, n=16)
                GP.dma_start(out=bfb[:, 1152 + dc * 16:1152 + (dc + 1) * 16],
                             in_=wc_d[dc * 128:(dc + 1) * 128, :],
                             inc=s_gp, n=16)
            wb_sb = bfb[:, 1024:1152]
            wc_sb = bfb[:, 1152:1280]
            wload_mark = s_gp.v
            ld_mark = s_ld.v
            delta0 = bfb[:, 0:512]
            delta1 = bfb[:, 512:1024]

            TE.wait(s_gp, wload_mark)
            TE.wait(s_ve)
            SC.wait(s_ld, ld_mark)
            for tb in range(NTB):
                tsl = slice(tb * 512, (tb + 1) * 512)
                for ddc in range(DC):
                    pss = psB if ddc % 2 == 0 else psC
                    ddst = delta0 if ddc % 2 == 0 else delta1
                    for dc in range(DC):
                        TE.matmul(pss[:, :],
                                  bfa[:, dc * 1024 + ddc * 128:
                                      dc * 1024 + (ddc + 1) * 128],
                                  fm(xnb, dc, tb), start=(dc == 0),
                                  stop=(dc == DC - 1), inc=s_mm)
                    SC.wait(s_mm)
                    SC.activation(tmpC[:, 0:512], pss[:, :], AF.Exp,
                                  bias=bd_ap[ddc], inc=s_sc)
                    SC.activation(ddst, tmpC[:, 0:512], AF.Ln, bias=1.0,
                                  inc=s_sc)
                    TE.wait(s_sc)
                    TE.matmul(psA[:, tsl], OD_BF, ddst,
                              start=(ddc == 0), stop=(ddc == DC - 1), inc=s_mm)
                for dc in range(DC):
                    TE.matmul(psD[0:16, :], wb_sb[:, dc * 16:(dc + 1) * 16],
                              fm(xnb, dc, tb), start=(dc == 0),
                              stop=(dc == DC - 1), inc=s_mm)
                for dc in range(DC):
                    TE.matmul(psE[0:16, :], wc_sb[:, dc * 16:(dc + 1) * 16],
                              fm(xnb, dc, tb), start=(dc == 0),
                              stop=(dc == DC - 1), inc=s_mm)
                VE.wait(s_mm)
                VE.wait(s_ld, ld_mark)
                VE.tensor_scalar(a_sb[:, tsl], psA[0:16, tsl], -1.0, 1.0,
                                 ALU.mult, ALU.add, inc=s_ve)
                VE.tensor_scalar(tmpA[0:16, 0:512], psD[0:16, :], bb_ap, None,
                                 ALU.add, inc=s_ve)
                VE.tensor_mul(b_sb[:, tsl], tmpA[0:16, 0:512],
                              psA[0:16, tsl], inc=s_ve)
                VE.tensor_scalar(b_sb[:, tsl], b_sb[:, tsl], sa_ap, None,
                                 ALU.mult, inc=s_ve)
                VE.tensor_scalar(ctb[:, tsl], psE[0:16, :], bc_ap, None,
                                 ALU.add, inc=s_ve)
                TE.wait(s_ve)

            if dbg:
                tap(a_sb[:, 0:512], rows=16)
                tap(b_sb[:, 0:512], rows=16)
                tap(ctb[:, 0:512], rows=16)
            VE.tensor_tensor_scan(hs_sb, a_sb, b_sb, 0.0, ALU.mult, ALU.add,
                                  inc=s_ve)
            VE.tensor_tensor_scan(pp_sb, a_sb, a_sb, 1.0, ALU.mult, ALU.bypass,
                                  inc=s_ve)
            if dbg:
                tap(hs_sb[:, 0:512], rows=16)
                tap(pp_sb[:, 0:512], rows=16)
            zcol = vslot()
            VE.memset(zcol, 0.0, inc=s_ve)
            GP.wait(s_ve)
            GP.dma_start(out=cc_pub1.ap(), in_=zcol[0:64, :], inc=s_gp, n=16)
            GP.dma_start(out=cc_pub2.ap(), in_=zcol[0:8, :], inc=s_gp, n=16)
            GP.wait(s_gp)
            GP.dma_start(out=cc_pub1.ap()[0:16, :], in_=hs_sb[:, T - 1:T],
                         inc=s_gp, n=16)
            GP.dma_start(out=cc_pub1.ap()[32:33, :], in_=xnmax, inc=s_gp, n=16)
            GP.wait(s_gp)
            GP.collective_compute("AllGather", ALU.bypass, replica_groups=RG,
                                  ins=[cc_pub1.ap()], outs=[cc_ag1.ap()],
                                  inc=s_cc)
            GP.wait(s_cc)
            agv = cc_ag1.ap().rearrange("(r k) o -> r (k o)", r=NCORES)
            GP.dma_start(out=tmpB[0:8, 0:64], in_=agv, inc=s_gp, n=16)
            TE.wait(s_gp)
            TE.transpose(psB[0:64, 0:8], tmpB[0:8, 0:64], ID_F[0:8, 0:8],
                         inc=s_mm)
            VE.wait(s_mm)
            VE.tensor_copy(hgath, psB[0:16, 0:8], inc=s_ve)
            VE.tensor_copy(mgath, psB[32:33, 0:8], inc=s_ve)
            VE.tensor_mul(hgath, hgath, sel_sb, inc=s_ve)
            VE.reduce_sum(hin_ap, hgath, AX, inc=s_ve)
            VE.scalar_tensor_tensor(pp_sb, pp_sb, hin_ap, hs_sb, ALU.mult,
                                    ALU.add, inc=s_ve)
            VE.tensor_mul(a_sb, pp_sb, ctb[:, :], inc=s_ve)
            if dbg:
                tap(b_sb[:, 0:512], rows=16)
                tap(a_sb[:, 0:512], rows=16)
            GP.wait(s_mm)
            for dc in range(DC):
                GP.dma_start(out=bfa[:, dc * 1024:(dc + 1) * 1024],
                             in_=wo_d[dc * 128:(dc + 1) * 128, :],
                             inc=s_gp, n=16)
            wo_mark = s_gp.v
            def wo_sb(dc, do):
                return bfa[:, dc * 1024 + do * 128: dc * 1024 + (do + 1) * 128]
            def yt(dc):
                return bfa[:, 8192 + dc * 512: 8192 + (dc + 1) * 512]
            TE.wait(s_gp, wo_mark)
            srow = tmpA[0:1, 1024:1088]
            for tb in range(NTB):
                tsl = slice(tb * 512, (tb + 1) * 512)
                TE.wait(s_ve)
                TE.matmul(psB[0:1, :], cons[0:16, 0:1], a_sb[:, tsl],
                          start=True, stop=True, inc=s_mm)
                VE.wait(s_mm)
                VE.tensor_copy(tmpB[0:1, 0:512], psB[0:1, :], inc=s_ve)
                TE.wait(s_ve)
                TE.matmul(psA[:, 0:512], ONES_ROW, tmpB[0:1, 0:512],
                          start=True, stop=True, inc=s_mm)
                VE.wait(s_mm)
                if tb > 0:
                    VE.wait(s_mm)
                for dc in range(DC):
                    VE.tensor_mul(yt(dc), fm(xnb, dc, tb), psA[:, 0:512],
                                  inc=s_ve)
                if dbg and tb == 0:
                    tap(yt(0))
                for do in range(DC):
                    pss = psB if do % 2 == 0 else psC
                    TE.wait(s_ve)
                    for dc in range(DC):
                        TE.matmul(pss[:, :], wo_sb(dc, do), yt(dc),
                                  start=(dc == 0), stop=(dc == DC - 1),
                                  inc=s_mm)
                    VE.wait(s_mm)
                    resadd(pss[:, :], do, tb, bias_ap=bo_ap[do])

            if dbg:
                tap(fm(x32, 0, 0))
            VE.wait(s_gp)
            VE.reduce_max(gmax, mgath, AX, inc=s_ve)
            VE.wait(s_sc)
            VE.tensor_mul(ghm, clipx, gmax, inc=s_ve)
            VE.tensor_scalar(sxr, ghm, 1.0 / 127.0, None, ALU.mult, inc=s_ve)
            VE.reciprocal(sxinv, sxr, inc=s_ve)
            VE.tensor_scalar(sx127, sxr, 1.0 / 127.0, None, ALU.mult, inc=s_ve)
            bcast_col(sxinv, invx_bc)
            bcast_col(sx127, sxc_bc)
            VE.wait(s_mm)
            def xq(dc, tb=None):
                if tb is None:
                    return bfc[:, dc * T:(dc + 1) * T]
                return bfc[:, dc * T + tb * 512: dc * T + (tb + 1) * 512]
            for dc in range(DC):
                for tb in range(NTB):
                    quant_int8(xq(dc, tb), fm(xnb, dc, tb), invx_bc, 512)

            if dbg:
                tap(xq(0, 0))
            wtile = fpw[:, 0:2816]
            ttag = bfb[:, 2816:3072]
            for do in range(DC):
                SY.wait(s_ve)
                SY.dma_start(out=wtile, in_=wdn_d[do * 128:(do + 1) * 128, :],
                             inc=s_ld, n=16)
                VE.wait(s_ld)
                if do > 0:
                    VE.wait(s_mm)
                VE.tensor_reduce(sd_ap[do], wtile, AX, ALU.max,
                                 apply_absolute_value=True, inc=s_ve)
                VE.reciprocal(tmpA[:, 1024:1025], sd_ap[do], inc=s_ve)
                VE.tensor_scalar(tmpA[:, 1025:1026], tmpA[:, 1024:1025], 127.0,
                                 None, ALU.mult, inc=s_ve)
                for hh in range(4):
                    cs = slice(hh * 704, (hh + 1) * 704)
                    quant_int8(wtile[:, cs], wtile[:, cs], tmpA[:, 1025:1026],
                               704)
                TE.wait(s_ve)
                for ic in range(22):
                    pss = psB if ic % 2 == 0 else psC
                    stg = ttag[:, (ic % 2) * 128:(ic % 2) * 128 + 128]
                    if ic >= 2:
                        TE.wait(s_st)
                    TE.transpose(pss[:, 0:128],
                                 wtile[:, ic * 128:(ic + 1) * 128], ID_F,
                                 inc=s_mm)
                    VE.wait(s_mm)
                    if ic >= 2 or do > 0:
                        VE.wait(s_st)
                    VE.tensor_copy(stg, pss[:, 0:128], inc=s_ve)
                    SY.wait(s_ve)
                    SY.dma_start(
                        out=wdqT_hbm.ap()[ic * 128:(ic + 1) * 128,
                                          do * 128:(do + 1) * 128],
                        in_=stg, inc=s_st, n=16)

            VE.memset(hmax_acc, 0.0, inc=s_ve)
            wgt = fpw[:, 0:1024]
            wut = fpw[:, 1024:2048]
            upt = fpw[:, 2048:2560]
            wqgT = bfb[:, 0:1024]
            wquT = bfb[:, 1024:2048]
            silu_t = bfb[:, 2048:2560]
            h_t = bfb[:, 2560:3072]
            for ic in range(22):
                SY.wait(s_ve)
                SY.dma_start(out=wgt, in_=wg_d[ic * 128:(ic + 1) * 128, :],
                             inc=s_ld, n=16)
                SY.dma_start(out=wut, in_=wu_d[ic * 128:(ic + 1) * 128, :],
                             inc=s_ld, n=16)
                VE.wait(s_ld)
                if ic > 0:
                    VE.wait(s_mm)
                VE.tensor_reduce(sg_ap, wgt, AX, ALU.max,
                                 apply_absolute_value=True, inc=s_ve)
                VE.tensor_reduce(su_ap, wut, AX, ALU.max,
                                 apply_absolute_value=True, inc=s_ve)
                VE.reciprocal(tmpA[:, 1024:1025], sg_ap, inc=s_ve)
                VE.tensor_scalar(tmpA[:, 1025:1026], tmpA[:, 1024:1025], 127.0,
                                 None, ALU.mult, inc=s_ve)
                VE.reciprocal(tmpA[:, 1026:1027], su_ap, inc=s_ve)
                VE.tensor_scalar(tmpA[:, 1027:1028], tmpA[:, 1026:1027], 127.0,
                                 None, ALU.mult, inc=s_ve)
                quant_int8(wgt, wgt, tmpA[:, 1025:1026], 1024)
                quant_int8(wut, wut, tmpA[:, 1027:1028], 1024)
                VE.tensor_mul(combg, sg_ap, sxc_bc, inc=s_ve)
                VE.tensor_mul(combu, su_ap, sxc_bc, inc=s_ve)
                TE.wait(s_ve)
                for j in range(DC):
                    for (src_, dst_) in ((wgt, wqgT), (wut, wquT)):
                        TE.transpose(psB[:, 0:128],
                                     src_[:, j * 128:(j + 1) * 128], ID_F,
                                     inc=s_mm)
                        VE.wait(s_mm)
                        VE.tensor_copy(dst_[:, j * 128:(j + 1) * 128],
                                       psB[:, 0:128], inc=s_ve)
                        TE.wait(s_ve)
                for tb in range(NTB):
                    for dc in range(DC):
                        TE.matmul(psD[:, :], wqgT[:, dc * 128:(dc + 1) * 128],
                                  xq(dc, tb), start=(dc == 0),
                                  stop=(dc == DC - 1), inc=s_mm)
                    for dc in range(DC):
                        TE.matmul(psE[:, :], wquT[:, dc * 128:(dc + 1) * 128],
                                  xq(dc, tb), start=(dc == 0),
                                  stop=(dc == DC - 1), inc=s_mm)
                    SC.wait(s_mm)
                    SC.wait(s_ve)
                    SC.activation(silu_t, psD[:, :], AF.Silu, scale=combg,
                                  inc=s_sc)
                    VE.wait(s_mm)
                    VE.wait(s_st)
                    VE.tensor_scalar(upt, psE[:, :], combu, None, ALU.mult,
                                     inc=s_ve)
                    VE.wait(s_sc)
                    VE.tensor_mul(h_t, silu_t, upt, inc=s_ve)
                    VE.tensor_reduce(tmpA[:, 1028:1029], h_t, AX, ALU.max,
                                     apply_absolute_value=True, inc=s_ve)
                    VE.tensor_tensor(hmax_acc, hmax_acc, tmpA[:, 1028:1029],
                                     ALU.max, inc=s_ve)
                    SY.wait(s_ve)
                    SY.dma_start(out=h_hbm.ap()[ic * 128:(ic + 1) * 128,
                                                tb * 512:(tb + 1) * 512],
                                 in_=h_t, inc=s_st, n=16)
                    TE.wait(s_ve)

            pmax_to_scalar(hmax_acc, hmax_sc)
            GP.wait(s_ve)
            GP.dma_start(out=cc_pub2.ap()[0:1, :], in_=hmax_sc, inc=s_gp, n=16)
            GP.wait(s_gp)
            GP.collective_compute("AllGather", ALU.bypass, replica_groups=RG,
                                  ins=[cc_pub2.ap()], outs=[cc_ag2.ap()],
                                  inc=s_cc)
            GP.wait(s_cc)
            GP.dma_start(out=tmpB[0:8, 0:8],
                         in_=cc_ag2.ap().rearrange("(r k) o -> r (k o)",
                                                   r=NCORES), inc=s_gp, n=16)
            TE.wait(s_gp)
            TE.transpose(psB[0:8, 0:8], tmpB[0:8, 0:8], ID_F[0:8, 0:8],
                         inc=s_mm)
            VE.wait(s_mm)
            VE.tensor_copy(m2gath, psB[0:1, 0:8], inc=s_ve)
            VE.reduce_max(ghm, m2gath, AX, inc=s_ve)
            VE.tensor_mul(gmax, clipd, ghm, inc=s_ve)
            VE.tensor_scalar(s2r, gmax, 1.0 / 127.0, None, ALU.mult, inc=s_ve)
            VE.reciprocal(s2inv, s2r, inc=s_ve)
            VE.tensor_scalar(s2127, s2r, 1.0 / 127.0, None, ALU.mult, inc=s_ve)
            bcast_col(s2inv, inv2_bc)
            bcast_col(s2127, sdc_bc)
            for do in range(DC):
                VE.tensor_mul(comb2[do], sd_ap[do], sdc_bc, inc=s_ve)

            wdt = [bfb[:, k * 1024:(k + 1) * 1024] for k in range(2)]
            hti = [bfb[:, 2048 + k * 512: 2048 + (k + 1) * 512]
                   for k in range(2)]
            hqi = [bfb[:, 3072 + k * 512: 3072 + (k + 1) * 512]
                   for k in range(2)]
            pslist = [psA[:, 0:512], psA[:, 512:1024], psA[:, 1024:1536],
                      psA[:, 1536:2048], psB[:, :], psC[:, :], psD[:, :],
                      psE[:, :]]
            SY.wait(s_st)
            def dn_load(tb, ic):
                k = ic % 2
                SY.dma_start(out=wdt[k],
                             in_=wdqT_hbm.ap()[ic * 128:(ic + 1) * 128, :],
                             inc=s_ld, n=16)
                SY.dma_start(out=hti[k],
                             in_=h_hbm.ap()[ic * 128:(ic + 1) * 128,
                                            tb * 512:(tb + 1) * 512],
                             inc=s_ld, n=16)
            for tb in range(NTB):
                for ic in range(22):
                    k = ic % 2
                    if ic == 0:
                        if tb > 0:
                            SY.wait(s_ve)
                            SY.wait(s_mm)
                        dn_load(tb, 0)
                    if ic < 21:
                        if ic >= 1 or tb > 0:
                            SY.wait(s_ve)
                            SY.wait(s_mm)
                        dn_load(tb, ic + 1)
                    VE.wait(s_ld)
                    if ic >= 2 or tb > 0:
                        VE.wait(s_mm)
                    quant_int8(hqi[k], hti[k], inv2_bc, 512)
                    TE.wait(s_ve)
                    for do in range(DC):
                        TE.matmul(pslist[do],
                                  wdt[k][:, do * 128:(do + 1) * 128],
                                  hqi[k], start=(ic == 0), stop=(ic == 21),
                                  inc=s_mm)
                VE.wait(s_mm)
                for do in range(DC):
                    resadd(pslist[do], do, tb, scale_ap=comb2[do])
                TE.wait(s_ve)


        def quant_fp(dst, srcA, man, min_exp, w, sA, sB, sC):
            RC = 12582912.0
            SC.wait(s_ve)
            SC.activation(sA, srcA, AF.Abs, inc=s_sc)
            SC.activation(sB, sA, AF.Ln, bias=qcn[:, 7:8], inc=s_sc)
            VE.wait(s_sc)
            VE.tensor_scalar(sA, sB, 1.0 / LN2, -0.5, ALU.mult, ALU.add,
                             inc=s_ve)
            VE.tensor_scalar(sA, sA, RC, -RC, ALU.add, ALU.add, inc=s_ve)
            VE.tensor_scalar(sA, sA, float(min_exp), None, ALU.max, inc=s_ve)
            SC.wait(s_ve)
            mi = 2 if man == 3.0 else 4
            SC.activation(sB, sA, AF.Exp, scale=qcn[:, 1:2],
                          bias=qcn[:, mi:mi + 1], inc=s_sc)
            SC.activation(sC, sA, AF.Exp, scale=qcn[:, 0:1],
                          bias=qcn[:, mi + 1:mi + 2], inc=s_sc)
            VE.wait(s_sc)
            VE.tensor_mul(sA, srcA, sB, inc=s_ve)
            VE.tensor_scalar(sA, sA, RC, -RC, ALU.add, ALU.add, inc=s_ve)
            VE.tensor_mul(dst, sA, sC, inc=s_ve)

        def emit_l1():
            _vs[0] = 0
            bq_ap = vslot(); bk_ap = vslot()
            negc = [vslot(), vslot()]
            b1_sb = vecs[:, 40:72]
            b2_sb = vecs[:, 72:80]
            b2q_sb = vecs[:, 80:88]
            bo_sb = vecs[:, 88:96]
            qm = [vslot1(), vslot1()]
            km = [vslot1(), vslot1()]
            t11a = vslot1(); t11b = vslot1()
            zc = vslot()
            wq_sb = bfb[:, 0:1024]
            wk_sb = bfb[:, 1024:2048]
            wv_sb = bfb[:, 2048:3072]
            bvb_sb = bfb[:, 3072:3200]
            w2qt = bfb[:, 3200:4224]
            wo_sb = bfb[0:64, 3200:5248]
            bvrow = tmpB[0:1, 0:128]
            VE.memset(zc, 0.0, inc=s_ve)

            emit_layernorm(x32, bfc)
            ln1_ve = s_ve.v

            SY.wait(s_ve, ln1_ve)
            for dc in range(DC):
                SY.dma_start(out=xn_pub.ap()[dc * 128:(dc + 1) * 128, :],
                             in_=fm(bfc, dc), inc=s_st, n=16)
            pub_ve = st_barrier()
            GP.wait(s_ve, pub_ve)
            GP.collective_compute("AllGather", ALU.bypass, replica_groups=RG,
                                  ins=[xn_pub.ap()], outs=[xn_ag.ap()],
                                  inc=s_cc)
            ag_mark = s_cc.v

            GP.dma_start(out=wq_sb.rearrange("q (dc j) -> q dc j", dc=8),
                         in_=l1["wq"].rearrange("(dc p) j -> p dc j", p=128),
                         inc=s_gp, n=16)
            GP.dma_start(out=wk_sb.rearrange("q (dc j) -> q dc j", dc=8),
                         in_=l1["wk"].rearrange("(dc p) j -> p dc j", p=128),
                         inc=s_gp, n=16)
            GP.dma_start(out=wv_sb.rearrange("q (dc j) -> q dc j", dc=8),
                         in_=l1["wv"].rearrange("(dc p) j -> p dc j", p=128),
                         inc=s_gp, n=16)
            l1w_mark = s_gp.v
            SY.dma_start(out=bq_ap, in_=l1["bq"], inc=s_ld, n=16)
            SY.dma_start(out=bk_ap, in_=l1["bk"], inc=s_ld, n=16)
            SY.dma_start(out=bvrow, in_=l1["bv"], inc=s_ld, n=16)
            SY.dma_start(out=b1_sb, in_=l1["b1"], inc=s_ld, n=16)
            SY.dma_start(out=b2_sb, in_=l1["b2"], inc=s_ld, n=16)
            SY.dma_start(out=bo_sb, in_=l1["bo"], inc=s_ld, n=16)
            l1b_mark = s_ld.v
            TE.wait(s_ld, l1b_mark)
            TE.wait(s_ve)
            TE.matmul(psD[:, 0:128], ONES_ROW, bvrow, start=True, stop=True,
                      inc=s_mm)
            VE.wait(s_mm)
            VE.tensor_copy(bvb_sb, psD[:, 0:128], inc=s_ve)
            VE.wait(s_ld, l1b_mark)
            SC.wait(s_ld, l1b_mark)
            quant_fp(b2q_sb, b2_sb, 10.0, -14.0, 8, tmpA[:, 0:8],
                     tmpA[:, 16:24], tmpA[:, 32:40])

            for hc in range(32):
                SY.wait(s_ve)
                SY.wait(s_sc)
                SY.dma_start(out=fpw[:, 0:1024],
                             in_=l1["w2T"][hc * 128:(hc + 1) * 128, :],
                             inc=s_ld, n=16)
                VE.wait(s_ld)
                SC.wait(s_ld)
                if hc > 0:
                    VE.wait(s_st)
                quant_fp(w2qt, fpw[:, 0:1024], 3.0, -6.0, 1024,
                         fpw[:, 1024:2048], fpw[:, 2048:3072],
                         fpw[:, 3072:4096])
                SY.wait(s_ve)
                SY.dma_start(out=w2q_hbm.ap()[hc * 128:(hc + 1) * 128, :],
                             in_=w2qt, inc=s_st, n=16)

            GP.wait(s_ve, st_barrier())
            GP.dma_start(out=wo_sb, in_=l1["wo"], inc=s_gp, n=16)
            wo_mark1 = s_gp.v

            w1blk = [bfa[:, 8192 + k * 1024: 8192 + (k + 1) * 1024]
                     for k in range(2)]
            hqt = bfa[:, 0:512]
            def w1_load(tb, hc):
                k = hc % 2
                GP.dma_start(
                    out=w1blk[k].rearrange("q (dc j) -> q dc j", dc=8),
                    in_=l1["w1T"][:, hc * 128:(hc + 1) * 128].rearrange(
                        "(dc p) j -> p dc j", p=128), inc=s_gp, n=16)
            for tb in range(NTB):
                for hc in range(32):
                    k = hc % 2
                    if hc == 0:
                        if tb > 0:
                            GP.wait(s_mm)
                        w1_load(tb, 0)
                    if hc < 31:
                        if hc >= 1 or tb > 0:
                            GP.wait(s_mm)
                        w1_load(tb, hc + 1)
                    pss = psB if hc % 2 == 0 else psC
                    TE.wait(s_gp)
                    TE.wait(s_sc)
                    TE.wait(s_ve)
                    for dc in range(DC):
                        TE.matmul(pss[:, :],
                                  w1blk[k][:, dc * 128:(dc + 1) * 128],
                                  fm(bfc, dc, tb), start=(dc == 0),
                                  stop=(dc == DC - 1), inc=s_mm)
                    SC.wait(s_mm)
                    SC.activation(fpw[:, 0:512], pss[:, :], AF.Relu,
                                  bias=b1_sb[:, hc:hc + 1], inc=s_sc)
                    VE.wait(s_sc)
                    VE.wait(s_st)
                    quant_fp(hqt, fpw[:, 0:512], 3.0, -6.0, 512,
                             fpw[:, 512:1024], fpw[:, 1024:1536],
                             fpw[:, 1536:2048])
                    SY.wait(s_ve)
                    SY.dma_start(out=hq_hbm.ap()[hc * 128:(hc + 1) * 128,
                                                  tb * 512:(tb + 1) * 512],
                                 in_=hqt, inc=s_st, n=16)

            wdt1 = [bfa[:, k * 1024:(k + 1) * 1024] for k in range(2)]
            hti1 = [bfa[:, 2048 + k * 512: 2048 + (k + 1) * 512]
                    for k in range(2)]
            pslist1 = [psA[:, 0:512], psA[:, 512:1024], psA[:, 1024:1536],
                       psA[:, 1536:2048], psB[:, :], psC[:, :], psD[:, :],
                       psE[:, :]]
            def l1p2_load(tb, hc):
                k = hc % 2
                SY.dma_start(out=wdt1[k],
                             in_=w2q_hbm.ap()[hc * 128:(hc + 1) * 128, :],
                             inc=s_ld, n=16)
                SY.dma_start(out=hti1[k],
                             in_=hq_hbm.ap()[hc * 128:(hc + 1) * 128,
                                             tb * 512:(tb + 1) * 512],
                             inc=s_ld, n=16)
            SY.wait(s_st)
            SY.wait(s_mm)
            for tb in range(NTB):
                for hc in range(32):
                    k = hc % 2
                    if hc == 0:
                        if tb > 0:
                            SY.wait(s_mm)
                        l1p2_load(tb, 0)
                    if hc < 31:
                        if hc >= 1 or tb > 0:
                            SY.wait(s_mm)
                        l1p2_load(tb, hc + 1)
                    TE.wait(s_ld)
                    for do in range(DC):
                        TE.matmul(pslist1[do],
                                  wdt1[k][:, do * 128:(do + 1) * 128],
                                  hti1[k], start=(hc == 0), stop=(hc == 31),
                                  inc=s_mm)
                VE.wait(s_mm)
                for do in range(DC):
                    VE.tensor_scalar(tmpA[:, 0:512], pslist1[do],
                                     b2q_sb[:, do:do + 1], None, ALU.add,
                                     inc=s_ve)
                    quant_fp(tmpB[:, 0:512], tmpA[:, 0:512], 10.0, -14.0, 512,
                             tmpA[:, 512:1024], fpw[:, 0:512],
                             fpw[:, 512:1024])
                    VE.tensor_add(fm(x32, do, tb), fm(x32, do, tb),
                                  tmpB[:, 0:512], inc=s_ve)
                TE.wait(s_ve)

            slab = lambda dc: bfc[:, dc * 512:(dc + 1) * 512]
            E_t = [bfc[:, 4096 + k * 512: 4096 + (k + 1) * 512]
                   for k in range(2)]
            k_sb = bfc[:, 5120:9216]
            q_sb = bfa[:, 0:4096]
            vaug = bfa[:, 4096:8256]
            po_t = bfa[:, 8256:8768]
            def vaug_ck(h, ck):
                o = 4096 + h * 2080 + ck * 65
                return bfa[:, o:o + 65]
            TE.wait(s_gp)
            for h in range(2):
                for ck in range(32):
                    VE.tensor_copy(vaug_ck(h, ck)[:, 64:65], ONES_COL,
                                   inc=s_ve)
            for seq in range(4):
                for tseg in range(8):
                    slot = 2 * seq + tseg // 4
                    col = (tseg % 4) * 512
                    SY.wait(s_cc, ag_mark)
                    SY.wait(s_mm)
                    for dc in range(DC):
                        SY.dma_start(
                            out=slab(dc),
                            in_=xn_ag.ap()[slot * DM + dc * 128:
                                           slot * DM + (dc + 1) * 128,
                                           col:col + 512], inc=s_ld, n=16)
                    TE.wait(s_ld)
                    TE.wait(s_ve)
                    for dc in range(DC):
                        TE.matmul(psD[:, :], wq_sb[:, dc * 128:(dc + 1) * 128],
                                  slab(dc), start=(dc == 0),
                                  stop=(dc == DC - 1), inc=s_mm)
                    for dc in range(DC):
                        TE.matmul(psE[:, :], wk_sb[:, dc * 128:(dc + 1) * 128],
                                  slab(dc), start=(dc == 0),
                                  stop=(dc == DC - 1), inc=s_mm)
                    for tk in range(4):
                        pss = psB if tk % 2 == 0 else psC
                        for dc in range(DC):
                            TE.matmul(pss[:, 0:128],
                                      slab(dc)[:, tk * 128:(tk + 1) * 128],
                                      wv_sb[:, dc * 128:(dc + 1) * 128],
                                      start=(dc == 0), stop=(dc == DC - 1),
                                      inc=s_mm)
                        ck = tseg * 4 + tk
                        VE.wait(s_mm)
                        for h in range(2):
                            VE.tensor_tensor(
                                vaug_ck(h, ck)[:, 0:64],
                                pss[:, h * 64:h * 64 + 64],
                                bvb_sb[:, h * 64:h * 64 + 64],
                                ALU.add, inc=s_ve)
                        if tk >= 1:
                            TE.wait(s_ve)
                    VE.wait(s_mm)
                    VE.tensor_scalar(q_sb[:, tseg * 512:(tseg + 1) * 512],
                                     psD[:, :], bq_ap, None, ALU.add,
                                     inc=s_ve)
                    VE.tensor_scalar(k_sb[:, tseg * 512:(tseg + 1) * 512],
                                     psE[:, :], bk_ap, None, ALU.add,
                                     inc=s_ve)
                for h in range(2):
                    VE.tensor_copy(qm[h], zc[0:1, :], inc=s_ve)
                    VE.tensor_copy(km[h], zc[0:1, :], inc=s_ve)
                for part in range(8):
                    SC.wait(s_ve)
                    SC.activation(tmpC[:, 0:512],
                                  q_sb[:, part * 512:(part + 1) * 512],
                                  AF.Square, inc=s_sc)
                    for h in range(2):
                        TE.wait(s_sc)
                        TE.wait(s_ve)
                        TE.matmul(psD[0:1, 0:512], half[:, h:h + 1],
                                  tmpC[:, 0:512], start=True, stop=True,
                                  inc=s_mm)
                        VE.wait(s_mm)
                        VE.reduce_max(t11a, psD[0:1, 0:512], AX, inc=s_ve)
                        VE.tensor_tensor(qm[h], qm[h], t11a, ALU.max,
                                         inc=s_ve)
                    SC.wait(s_ve)
                    SC.activation(tmpC[:, 0:512],
                                  k_sb[:, part * 512:(part + 1) * 512],
                                  AF.Square, inc=s_sc)
                    for h in range(2):
                        TE.wait(s_sc)
                        TE.wait(s_ve)
                        TE.matmul(psD[0:1, 0:512], half[:, h:h + 1],
                                  tmpC[:, 0:512], start=True, stop=True,
                                  inc=s_mm)
                        VE.wait(s_mm)
                        VE.reduce_max(t11a, psD[0:1, 0:512], AX, inc=s_ve)
                        VE.tensor_tensor(km[h], km[h], t11a, ALU.max,
                                         inc=s_ve)
                for h in range(2):
                    VE.tensor_mul(t11a, qm[h], km[h], inc=s_ve)
                    SC.wait(s_ve)
                    SC.activation(t11b, t11a, AF.Sqrt, inc=s_sc)
                    VE.wait(s_sc)
                    VE.tensor_scalar(t11a, t11b, -0.125, None, ALU.mult,
                                     inc=s_ve)
                    bcast_col(t11a, negc[h])

                for qb in range(8):
                    for h in range(2):
                        nkb = 4 * qb + 4
                        av_marks = []
                        for kb in range(nkb):
                            pqk = psA[:, (kb % 2) * 512:(kb % 2) * 512 + 512]
                            if kb >= 2:
                                TE.wait(s_sc)
                            TE.matmul(pqk,
                                      k_sb[h * 64:h * 64 + 64,
                                           kb * 128:(kb + 1) * 128],
                                      q_sb[h * 64:h * 64 + 64,
                                           qb * 512:(qb + 1) * 512],
                                      start=True, stop=True, inc=s_mm)
                            SC.wait(s_mm)
                            if kb >= 2:
                                SC.wait(s_mm, av_marks[kb - 2])
                            SC.activation(E_t[kb % 2], pqk, AF.Exp,
                                          scale=qcn[:, 6:7], bias=negc[h],
                                          inc=s_sc)
                            m = kb - 4 * qb
                            if m >= 0:
                                VE.wait(s_sc)
                                VE.tensor_mul(E_t[kb % 2], E_t[kb % 2],
                                              mskb[:, m * 512:(m + 1) * 512],
                                              inc=s_ve)
                                TE.wait(s_ve)
                            else:
                                TE.wait(s_sc)
                            TE.matmul(psB[0:65, :], vaug_ck(h, kb),
                                      E_t[kb % 2], start=(kb == 0),
                                      stop=(kb == nkb - 1), inc=s_mm)
                            av_marks.append(s_mm.v)
                        VE.wait(s_mm)
                        VE.reciprocal(tmpB[0:1, 0:512], psB[64:65, :],
                                      inc=s_ve)
                        TE.wait(s_ve)
                        TE.matmul(psC[0:64, :], ONES_ROW[0:1, 0:64],
                                  tmpB[0:1, 0:512], start=True, stop=True,
                                  inc=s_mm)
                        VE.wait(s_mm)
                        VE.tensor_copy(tmpA[0:64, 0:512], psC[0:64, :],
                                       inc=s_ve)
                        VE.tensor_mul(bfo[:, h * 512:(h + 1) * 512],
                                      psB[0:64, :], tmpA[0:64, 0:512],
                                      inc=s_ve)
                        TE.wait(s_ve)
                    rank = 2 * seq + qb // 4
                    col = (qb % 4) * 512
                    for do in range(DC):
                        pss = psD if do % 2 == 0 else psE
                        TE.wait(s_ve)
                        for h in range(2):
                            TE.matmul(pss[:, :],
                                      wo_sb[:, h * 1024 + do * 128:
                                            h * 1024 + (do + 1) * 128],
                                      bfo[:, h * 512:(h + 1) * 512],
                                      start=(h == 0), stop=(h == 1),
                                      inc=s_mm)
                        VE.wait(s_mm)
                        VE.wait(s_st)
                        VE.tensor_copy(po_t, pss[:, :], inc=s_ve)
                        SY.wait(s_ve)
                        SY.dma_start(
                            out=po_hbm.ap()[rank * DM + do * 128:
                                            rank * DM + (do + 1) * 128,
                                            col:col + 512],
                            in_=po_t, inc=s_st, n=16)
            GP.wait(s_ve, st_barrier())
            GP.collective_compute("ReduceScatter", ALU.add, replica_groups=RG,
                                  ins=[po_hbm.ap()], outs=[rs_out.ap()],
                                  inc=s_cc)
            rs_mark = s_cc.v
            SY.wait(s_cc, rs_mark)
            SY.wait(s_mm)
            for dc in range(DC):
                SY.dma_start(out=fm(bfc, dc),
                             in_=rs_out.ap()[dc * 128:(dc + 1) * 128, :],
                             inc=s_ld, n=16)
            VE.wait(s_ld)
            for dc in range(DC):
                for tb in range(NTB):
                    VE.tensor_scalar(tmpB[:, 0:512], fm(bfc, dc, tb),
                                     bo_sb[:, dc:dc + 1], None, ALU.add,
                                     inc=s_ve)
                    VE.tensor_add(fm(x32, dc, tb), fm(x32, dc, tb),
                                  tmpB[:, 0:512], inc=s_ve)


        def l2drain():
            SY.wait(s_gp); SY.wait(s_ld); SY.wait(s_st)
            SY.wait(s_cc, s_cc.v); SY.wait(s_ve); SY.wait(s_sc)
            SY.wait(s_mm)
            GP.wait(s_st); GP.wait(s_ve)

        def emit_l2():
            _vs[0] = 0
            LR = 0.001
            flc = vecs[:, 200:204]
            fl1 = vecs[:, 204:208]
            qm = [vecs[0:1, 226:227], vecs[0:1, 227:228]]
            km = [vecs[0:1, 228:229], vecs[0:1, 229:230]]
            t11a2 = vecs[0:1, 231:232]
            t11b2 = vecs[0:1, 232:233]
            negc2 = [vecs[:, 233:234], vecs[:, 234:235]]
            zc2 = vecs[:, 235:236]
            b1c = vecs[:, 208:210]
            b2c = vecs[:, 210:218]
            bproj_sb = vecs[:, 218:226]
            w1m = fpw[:, 0:2048]
            w2m = fpw[:, 2048:4096]
            b1row = s16[0:1, 0:256]
            b2row = s16[0:1, 256:1280]
            w1b = bfb[:, 0:2048]
            w2b = bfb[:, 2048:4096]
            b1bc = bfb[:, 4096:4352]
            b2bc = bfb[:, 4352:5376]
            hgel = lambda tc: bfa[:, tc * 256:(tc + 1) * 256]
            dpre = lambda tc: bfa[:, 4096 + tc * 256: 4096 + (tc + 1) * 256]
            w2dom = bfa[:, 8192:10240]
            p1 = bfa[:, 10240:12288]

            VE.memset(zc2, 0.0, inc=s_ve)
            emit_layernorm(x32, bfc)
            ln2_ve = s_ve.v
            SY.wait(s_ve, ln2_ve)
            for dc in range(DC):
                SY.dma_start(out=xn_pub.ap()[dc * 128:(dc + 1) * 128, :],
                             in_=fm(bfc, dc), inc=s_st, n=16)
            GP.wait(s_ve, st_barrier())
            GP.collective_compute("AllGather", ALU.bypass, replica_groups=RG,
                                  ins=[xn_pub.ap()], outs=[xn_ag.ap()],
                                  inc=s_cc)
            ag2_mark = s_cc.v

            SY.dma_start(out=w1m.rearrange("q (dc j) -> q dc j", dc=8),
                         in_=l2["w1T"].rearrange("(dc p) j -> p dc j", p=128),
                         inc=s_ld, n=16)
            SY.dma_start(out=w2m.rearrange("q (kc j) -> q kc j", kc=2),
                         in_=l2["w2T"].rearrange("(kc p) j -> p kc j", p=128),
                         inc=s_ld, n=16)
            SY.dma_start(out=b1row, in_=l2["b1"], inc=s_ld, n=16)
            SY.dma_start(out=b2row, in_=l2["b2"], inc=s_ld, n=16)
            SY.dma_start(out=flc, in_=l2["flc"], inc=s_ld, n=16)
            SY.dma_start(out=fl1, in_=l2["fl1"], inc=s_ld, n=16)
            SY.dma_start(out=bproj_sb, in_=l2["bproj"], inc=s_ld, n=16)
            l2ld_mark = s_ld.v
            VE.memset(tmpA[:, 0:1024], 0.0, inc=s_ve)
            SY.wait(s_ve)
            for r0 in range(0, 514, 128):
                rr = min(128, 514 - r0)
                SY.dma_start(out=g_pub.ap()[r0:r0 + rr, :],
                             in_=tmpA[0:rr, 0:1024], inc=s_st, n=16)

            pslist8 = [psA[:, 0:512], psA[:, 512:1024], psA[:, 1024:1536],
                       psA[:, 1536:2048], psB[:, :], psC[:, :], psD[:, :],
                       psE[:, :]]
            def wstream(dram, dm, k):
                GP.dma_start(
                    out=zstr[:, k * 1024:(k + 1) * 1024],
                    in_=dram[dm * 128:(dm + 1) * 128, :], inc=s_gp, n=16)
            for tb in range(NTB):
                for (wi, wd) in ((0, l2["tkT"]), (1, l2["tvT"]),
                                 (2, l2["tqT"])):
                    for dm in range(DC):
                        k = dm % 2
                        if dm == 0:
                            GP.wait(s_mm)
                            wstream(wd, 0, 0)
                        if dm < 7:
                            GP.wait(s_mm)
                            wstream(wd, dm + 1, (dm + 1) % 2)
                        TE.wait(s_gp)
                        TE.wait(s_ve)
                        if wi < 2:
                            for tc in range(4):
                                for g in range(2):
                                    TE.matmul(
                                        pslist8[tc * 2 + g],
                                        fm(bfc, dm, tb)[:, tc * 128:
                                                        (tc + 1) * 128],
                                        zstr[:, k * 1024 + g * 512:
                                             k * 1024 + (g + 1) * 512],
                                        start=(dm == 0), stop=(dm == 7),
                                        inc=s_mm)
                        else:
                            for dq in range(DC):
                                TE.matmul(
                                    pslist8[dq],
                                    zstr[:, k * 1024 + dq * 128:
                                         k * 1024 + (dq + 1) * 128],
                                    fm(bfc, dm, tb),
                                    start=(dm == 0), stop=(dm == 7),
                                    inc=s_mm)
                    VE.wait(s_mm)
                    VE.wait(s_st)
                    if wi == 0:
                        for i in range(8):
                            VE.tensor_copy(bfa[:, i * 512:(i + 1) * 512],
                                           pslist8[i], inc=s_ve)
                        SY.wait(s_ve)
                        for tc in range(4):
                            for g in range(2):
                                SY.dma_start(
                                    out=z_hbm.ap()[tb * 512 + tc * 128:
                                                   tb * 512 + (tc + 1) * 128,
                                                   g * 512:(g + 1) * 512],
                                    in_=bfa[:, (tc * 2 + g) * 512:
                                            (tc * 2 + g + 1) * 512],
                                    inc=s_st, n=16)
                    elif wi == 1:
                        for i in range(8):
                            VE.tensor_sub(bfa[:, 4096 + i * 512:
                                              4096 + (i + 1) * 512],
                                          bfa[:, i * 512:(i + 1) * 512],
                                          pslist8[i], inc=s_ve)
                        SY.wait(s_ve)
                        for tc in range(4):
                            for g in range(2):
                                SY.dma_start(
                                    out=zy_hbm.ap()[tb * 512 + tc * 128:
                                                    tb * 512 + (tc + 1) * 128,
                                                    g * 512:(g + 1) * 512],
                                    in_=bfa[:, 4096 + (tc * 2 + g) * 512:
                                            4096 + (tc * 2 + g + 1) * 512],
                                    inc=s_st, n=16)
                    else:
                        for dq in range(DC):
                            VE.tensor_copy(bfa[:, dq * 512:(dq + 1) * 512],
                                           pslist8[dq], inc=s_ve)
                        SY.wait(s_ve)
                        for dq in range(DC):
                            SY.dma_start(
                                out=q2_hbm.ap()[dq * 128:(dq + 1) * 128,
                                                tb * 512:(tb + 1) * 512],
                                in_=bfa[:, dq * 512:(dq + 1) * 512],
                                inc=s_st, n=16)
                    TE.wait(s_ve)

            if L2STOP == 1:
                l2drain()
                return
            def l2_prep():
                VE.wait(s_ld, l2ld_mark)
                VE.tensor_copy(w1b, w1m, inc=s_ve)
                VE.tensor_copy(w2b, w2m, inc=s_ve)
                for koc in range(2):
                    TE.wait(s_ve)
                    TE.transpose(psE[:, 0:1],
                                 b1row[:, koc * 128:(koc + 1) * 128],
                                 ID_F[0:1, 0:1], inc=s_mm)
                    VE.wait(s_mm)
                    VE.tensor_copy(b1c[:, koc:koc + 1], psE[:, 0:1], inc=s_ve)
                for do in range(DC):
                    TE.wait(s_ve)
                    TE.transpose(psE[:, 0:1],
                                 b2row[:, do * 128:(do + 1) * 128],
                                 ID_F[0:1, 0:1], inc=s_mm)
                    VE.wait(s_mm)
                    VE.tensor_copy(b2c[:, do:do + 1], psE[:, 0:1], inc=s_ve)
                TE.wait(s_ve)
                TE.matmul(psD[:, 0:256], ONES_ROW, b1row, start=True,
                          stop=True, inc=s_mm)
                VE.wait(s_mm)
                VE.tensor_copy(b1bc, psD[:, 0:256], inc=s_ve)
                for g in range(2):
                    TE.wait(s_ve)
                    TE.matmul(psD[:, :], ONES_ROW,
                              b2row[:, g * 512:(g + 1) * 512],
                              start=True, stop=True, inc=s_mm)
                    VE.wait(s_mm)
                    VE.tensor_copy(b2bc[g * 512:(g + 1) * 512] if False else
                                   b2bc[:, g * 512:(g + 1) * 512],
                                   psD[:, :], inc=s_ve)
                for kc in range(2):
                    for doc in range(DC):
                        TE.wait(s_ve)
                        TE.transpose(psC[:, 0:128],
                                     w2m[:, kc * 1024 + doc * 128:
                                         kc * 1024 + (doc + 1) * 128],
                                     ID_F, inc=s_mm)
                        VE.wait(s_mm)
                        VE.tensor_copy(
                            w2dom[:, doc * 256 + kc * 128:
                                  doc * 256 + (kc + 1) * 128],
                            psC[:, 0:128], inc=s_ve)
                for dmc in range(DC):
                    k = dmc % 2
                    if dmc == 0:
                        GP.wait(s_ve)
                        GP.dma_start(
                            out=zstr[:, 0:1024].rearrange(
                                "q (dz j) -> q dz j", dz=8),
                            in_=l2["tk"][:, 0:128].rearrange(
                                "(dz p) j -> p dz j", p=128),
                            inc=s_gp, n=16)
                    if dmc < 7:
                        GP.wait(s_mm)
                        GP.dma_start(
                            out=zstr[:, ((dmc + 1) % 2) * 1024:
                                     ((dmc + 1) % 2 + 1) * 1024].rearrange(
                                "q (dz j) -> q dz j", dz=8),
                            in_=l2["tk"][:, (dmc + 1) * 128:
                                         (dmc + 2) * 128].rearrange(
                                "(dz p) j -> p dz j", p=128),
                            inc=s_gp, n=16)
                    pss = psB if dmc % 2 == 0 else psC
                    TE.wait(s_gp)
                    TE.wait(s_ve)
                    for dz in range(DC):
                        TE.matmul(pss[:, 0:256],
                                  zstr[:, k * 1024 + dz * 128:
                                       k * 1024 + (dz + 1) * 128],
                                  w1b[:, dz * 256:(dz + 1) * 256],
                                  start=(dz == 0), stop=(dz == 7), inc=s_mm)
                    VE.wait(s_mm)
                    VE.tensor_copy(p1[:, dmc * 256:(dmc + 1) * 256],
                                   pss[:, 0:256], inc=s_ve)

            l2_prep()

            for s in range(4):
                fcol = flc[:, s:s + 1]
                VE.wait(s_st)
                for tc in range(16):
                    tb, tci = tc // 4, tc % 4
                    SY.wait(s_ve)
                    SY.wait(s_st)
                    SY.dma_start(out=zstr[:, 1024:2048],
                                 in_=zy_hbm.ap()[tc * 128:(tc + 1) * 128, :],
                                 inc=s_ld, n=16)
                    TE.wait(s_ve)
                    for dm in range(DC):
                        TE.matmul(psB[:, 0:256],
                                  fm(bfc, dm, tb)[:, tci * 128:
                                                  (tci + 1) * 128],
                                  p1[:, dm * 256:(dm + 1) * 256],
                                  start=(dm == 0), stop=(dm == 7), inc=s_mm)
                    VE.wait(s_mm)
                    VE.tensor_tensor(tmpB[:, 0:256], psB[:, 0:256], b1bc,
                                     ALU.add, inc=s_ve)
                    SC.wait(s_ve)
                    SC.activation(tmpB[:, 256:512], tmpB[:, 0:256], AF.Gelu,
                                  inc=s_sc)
                    VE.wait(s_sc)
                    VE.tensor_copy(hgel(tc), tmpB[:, 256:512], inc=s_ve)
                    for kc in range(2):
                        TE.wait(s_ve)
                        TE.transpose(psC[:, 0:128],
                                     tmpB[:, 256 + kc * 128:
                                          256 + (kc + 1) * 128],
                                     ID_F, inc=s_mm)
                        VE.wait(s_mm)
                        VE.tensor_copy(zstr[:, kc * 128:(kc + 1) * 128],
                                       psC[:, 0:128], inc=s_ve)
                    VE.wait(s_ld)
                    for dog in range(2):
                        pss = psD if dog == 0 else psE
                        TE.wait(s_ve)
                        for kc in range(2):
                            TE.matmul(pss[:, :],
                                      zstr[:, kc * 128:(kc + 1) * 128],
                                      w2b[:, kc * 1024 + dog * 512:
                                          kc * 1024 + (dog + 1) * 512],
                                      start=(kc == 0), stop=(kc == 1),
                                      inc=s_mm)
                        VE.wait(s_mm)
                        VE.tensor_tensor(tmpA[:, dog * 512:(dog + 1) * 512],
                                         pss[:, :],
                                         zstr[:, 1024 + dog * 512:
                                              1024 + (dog + 1) * 512],
                                         ALU.add, inc=s_ve)
                        VE.tensor_tensor(tmpA[:, dog * 512:(dog + 1) * 512],
                                         tmpA[:, dog * 512:(dog + 1) * 512],
                                         b2bc[:, dog * 512:(dog + 1) * 512],
                                         ALU.add, inc=s_ve)
                    VE.tensor_scalar(zstr[:, 1024:2048], tmpA[:, 0:1024],
                                     fcol, None, ALU.mult, inc=s_ve)
                    SY.wait(s_ve)
                    SY.dma_start(out=g_hbm.ap()[tc * 128:(tc + 1) * 128, :],
                                 in_=zstr[:, 1024:2048], inc=s_st, n=16)
                    for doc in range(DC):
                        TE.wait(s_ve)
                        TE.transpose(psC[:, 0:128],
                                     tmpA[:, doc * 128:(doc + 1) * 128],
                                     ID_F, inc=s_mm)
                        VE.wait(s_mm)
                        VE.tensor_scalar(zstr[:, doc * 128:(doc + 1) * 128],
                                         psC[:, 0:128], fcol, None, ALU.mult,
                                         inc=s_ve)
                    TE.wait(s_ve)
                    for doc in range(DC):
                        TE.matmul(psB[:, 0:256],
                                  zstr[:, doc * 128:(doc + 1) * 128],
                                  w2dom[:, doc * 256:(doc + 1) * 256],
                                  start=(doc == 0), stop=(doc == 7), inc=s_mm)
                    SC.wait(s_ve)
                    SC.activation(tmpC[:, 0:256], tmpB[:, 0:256], AF.Abs,
                                  inc=s_sc)
                    SC.activation(tmpC[:, 256:512], tmpB[:, 0:256], AF.Sign,
                                  inc=s_sc)
                    VE.wait(s_sc)
                    VE.tensor_scalar(tmpA[:, 0:256], tmpC[:, 0:256], 0.01,
                                     None, ALU.max, inc=s_ve)
                    VE.reciprocal(tmpA[:, 0:256], tmpA[:, 0:256], inc=s_ve)
                    VE.tensor_mul(tmpA[:, 256:512], hgel(tc), tmpA[:, 0:256],
                                  inc=s_ve)
                    VE.tensor_mul(tmpA[:, 256:512], tmpA[:, 256:512],
                                  tmpC[:, 256:512], inc=s_ve)
                    VE.tensor_scalar(tmpA[:, 512:768], tmpB[:, 0:256],
                                     0.3989422804014327, 0.5, ALU.mult,
                                     ALU.add, inc=s_ve)
                    VE.tensor_scalar(tmpA[:, 768:1024], tmpC[:, 0:256], 0.01,
                                     None, ALU.is_gt, inc=s_ve)
                    VE.tensor_sub(tmpA[:, 256:512], tmpA[:, 256:512],
                                  tmpA[:, 512:768], inc=s_ve)
                    VE.tensor_mul(tmpA[:, 256:512], tmpA[:, 256:512],
                                  tmpA[:, 768:1024], inc=s_ve)
                    VE.tensor_add(tmpA[:, 512:768], tmpA[:, 512:768],
                                  tmpA[:, 256:512], inc=s_ve)
                    SC.wait(s_ve)
                    SC.activation(tmpC[:, 0:256], tmpB[:, 0:256], AF.Square,
                                  inc=s_sc)
                    SC.activation(tmpC[:, 256:512], tmpC[:, 0:256], AF.Exp,
                                  scale=qcn[:, 8:9], inc=s_sc)
                    VE.wait(s_sc)
                    VE.tensor_mul(tmpA[:, 0:256], tmpB[:, 0:256],
                                  tmpC[:, 256:512], inc=s_ve)
                    VE.scalar_tensor_tensor(tmpA[:, 0:256], tmpA[:, 0:256],
                                            0.3989422804014327,
                                            tmpA[:, 512:768], ALU.mult,
                                            ALU.add, inc=s_ve)
                    VE.wait(s_mm)
                    VE.tensor_mul(dpre(tc), psB[:, 0:256], tmpA[:, 0:256],
                                  inc=s_ve)
                    TE.wait(s_ve)

                for tc in range(16):
                    k = tc % 2
                    if tc == 0:
                        SY.wait(s_ve)
                        SY.wait(s_mm)
                        SY.wait(s_st)
                        SY.dma_start(out=zstr[:, 0:1024],
                                     in_=g_hbm.ap()[0:128, :], inc=s_ld, n=16)
                    if tc < 15:
                        SY.wait(s_mm)
                        SY.dma_start(
                            out=zstr[:, ((tc + 1) % 2) * 1024:
                                     ((tc + 1) % 2) * 1024 + 1024],
                            in_=g_hbm.ap()[(tc + 1) * 128:(tc + 2) * 128, :],
                            inc=s_ld, n=16)
                    gt = zstr[:, k * 1024:(k + 1) * 1024]
                    TE.wait(s_ld)
                    for koc in range(2):
                        for dog in range(2):
                            TE.matmul(psA[:, koc * 1024 + dog * 512:
                                          koc * 1024 + (dog + 1) * 512],
                                      hgel(tc)[:, koc * 128:(koc + 1) * 128],
                                      gt[:, dog * 512:(dog + 1) * 512],
                                      start=(tc == 0), stop=(tc == 15),
                                      inc=s_mm)
                    for dog in range(2):
                        pss = psB if dog == 0 else psC
                        TE.matmul(pss[0:1, :], onebf[:, 0:1],
                                  gt[:, dog * 512:(dog + 1) * 512],
                                  start=(tc == 0), stop=(tc == 15), inc=s_mm)
                VE.wait(s_mm)
                for koc in range(2):
                    for dog in range(2):
                        VE.wait(s_st)
                        VE.tensor_copy(tmpA[:, 0:512],
                                       psA[:, koc * 1024 + dog * 512:
                                           koc * 1024 + (dog + 1) * 512],
                                       inc=s_ve)
                        SY.wait(s_ve)
                        SY.dma_start(
                            out=g_pub.ap()[koc * 128:(koc + 1) * 128,
                                           dog * 512:(dog + 1) * 512],
                            in_=tmpA[:, 0:512], inc=s_st, n=16)
                for dog in range(2):
                    pss = psB if dog == 0 else psC
                    VE.wait(s_st)
                    VE.tensor_copy(tmpB[0:1, 0:512], pss[0:1, :], inc=s_ve)
                    SY.wait(s_ve)
                    SY.dma_start(out=g_pub.ap()[512:513,
                                                dog * 512:(dog + 1) * 512],
                                 in_=tmpB[0:1, 0:512], inc=s_st, n=16)
                TE.wait(s_ve)
                for tc in range(16):
                    k = tc % 2
                    if tc == 0:
                        SY.wait(s_ve)
                        SY.wait(s_mm)
                        SY.wait(s_st)
                        SY.dma_start(out=zstr[:, 0:1024],
                                     in_=z_hbm.ap()[0:128, :], inc=s_ld, n=16)
                    if tc < 15:
                        SY.wait(s_mm)
                        SY.dma_start(
                            out=zstr[:, ((tc + 1) % 2) * 1024:
                                     ((tc + 1) % 2) * 1024 + 1024],
                            in_=z_hbm.ap()[(tc + 1) * 128:(tc + 2) * 128, :],
                            inc=s_ld, n=16)
                    zt = zstr[:, (tc % 2) * 1024:(tc % 2) * 1024 + 1024]
                    TE.wait(s_ld)
                    for dc in range(DC):
                        TE.matmul(psA[:, dc * 256:(dc + 1) * 256],
                                  zt[:, dc * 128:(dc + 1) * 128],
                                  dpre(tc), start=(tc == 0), stop=(tc == 15),
                                  inc=s_mm)
                    TE.matmul(psB[0:1, 0:256], onebf[:, 0:1], dpre(tc),
                              start=(tc == 0), stop=(tc == 15), inc=s_mm)
                VE.wait(s_mm)
                for dc in range(DC):
                    VE.wait(s_st)
                    VE.tensor_copy(tmpA[:, 0:256],
                                   psA[:, dc * 256:(dc + 1) * 256], inc=s_ve)
                    SY.wait(s_ve)
                    SY.dma_start(
                        out=g_pub.ap()[256 + 128 * (dc // 4):
                                       256 + 128 * (dc // 4) + 128,
                                       (dc % 4) * 256:(dc % 4 + 1) * 256],
                        in_=tmpA[:, 0:256], inc=s_st, n=16)
                VE.wait(s_st)
                VE.tensor_copy(tmpB[0:1, 0:256], psB[0:1, 0:256], inc=s_ve)
                SY.wait(s_ve)
                SY.dma_start(out=g_pub.ap()[513:514, 0:256],
                             in_=tmpB[0:1, 0:256], inc=s_st, n=16)
                GP.wait(s_ve, st_barrier())
                GP.collective_compute("AllReduce", ALU.add, replica_groups=RG,
                                      ins=[g_pub.ap()], outs=[g_ar.ap()],
                                      inc=s_cc)
                ar_mark = s_cc.v
                GP.wait(s_cc, ar_mark)
                for koc in range(2):
                    GP.wait(s_ve)
                    GP.dma_start(out=zstr[:, 0:1024],
                                 in_=g_ar.ap()[koc * 128:(koc + 1) * 128, :],
                                 inc=s_gp, n=16)
                    VE.wait(s_gp)
                    VE.scalar_tensor_tensor(
                        w2m[:, koc * 1024:(koc + 1) * 1024], zstr[:, 0:1024],
                        -LR, w2m[:, koc * 1024:(koc + 1) * 1024], ALU.mult,
                        ALU.add, inc=s_ve)
                for dc in range(DC):
                    GP.wait(s_ve)
                    GP.dma_start(
                        out=zstr[:, 0:256],
                        in_=g_ar.ap()[256 + 128 * (dc // 4):
                                      256 + 128 * (dc // 4) + 128,
                                      (dc % 4) * 256:(dc % 4 + 1) * 256],
                        inc=s_gp, n=16)
                    VE.wait(s_gp)
                    VE.scalar_tensor_tensor(
                        w1m[:, dc * 256:(dc + 1) * 256], zstr[:, 0:256],
                        -LR, w1m[:, dc * 256:(dc + 1) * 256], ALU.mult,
                        ALU.add, inc=s_ve)
                SY.wait(s_ve)
                SY.dma_start(out=tmpB[0:1, 0:1024], in_=g_ar.ap()[512:513, :],
                             inc=s_ld, n=16)
                VE.wait(s_ld)
                VE.scalar_tensor_tensor(b2row, tmpB[0:1, 0:1024], -LR, b2row,
                                        ALU.mult, ALU.add, inc=s_ve)
                SY.wait(s_ve)
                SY.dma_start(out=tmpB[0:1, 0:256],
                             in_=g_ar.ap()[513:514, 0:256], inc=s_ld, n=16)
                VE.wait(s_ld)
                VE.scalar_tensor_tensor(b1row, tmpB[0:1, 0:256], -LR, b1row,
                                        ALU.mult, ALU.add, inc=s_ve)
                l2_prep()
                f1col = fl1[:, s:s + 1]
                for tb in range(NTB):
                    SY.wait(s_ve)
                    SY.wait(s_mm)
                    for dq in range(DC):
                        SY.dma_start(
                            out=bfa[:, dq * 512:(dq + 1) * 512],
                            in_=q2_hbm.ap()[dq * 128:(dq + 1) * 128,
                                            tb * 512:(tb + 1) * 512],
                            inc=s_ld, n=16)
                    TE.wait(s_ld)
                    TE.wait(s_ve)
                    for koc in range(2):
                        pss = psD if koc == 0 else psE
                        for dq in range(DC):
                            TE.matmul(pss[:, :],
                                      w1b[:, dq * 256 + koc * 128:
                                          dq * 256 + (koc + 1) * 128],
                                      bfa[:, dq * 512:(dq + 1) * 512],
                                      start=(dq == 0), stop=(dq == 7),
                                      inc=s_mm)
                        SC.wait(s_mm)
                        SC.wait(s_ve)
                        SC.activation(zstr[:, koc * 512:(koc + 1) * 512],
                                      pss[:, :], AF.Gelu,
                                      bias=b1c[:, koc:koc + 1], inc=s_sc)
                    TE.wait(s_sc)
                    for do in range(DC):
                        pss = psB if do % 2 == 0 else psC
                        TE.wait(s_ve)
                        for koc in range(2):
                            TE.matmul(pss[:, :],
                                      w2b[:, koc * 1024 + do * 128:
                                          koc * 1024 + (do + 1) * 128],
                                      zstr[:, koc * 512:(koc + 1) * 512],
                                      start=(koc == 0), stop=(koc == 1),
                                      inc=s_mm)
                        VE.wait(s_mm)
                        VE.tensor_scalar(tmpB[:, 0:512], pss[:, :],
                                         b2c[:, do:do + 1], None, ALU.add,
                                         inc=s_ve)
                        VE.tensor_tensor(tmpB[:, 0:512], tmpB[:, 0:512],
                                         bfa[:, do * 512:(do + 1) * 512],
                                         ALU.add, inc=s_ve)
                        VE.tensor_scalar(tmpB[:, 0:512], tmpB[:, 0:512],
                                         f1col, None, ALU.mult, inc=s_ve)
                        VE.tensor_add(fm(x32, do, tb), fm(x32, do, tb),
                                      tmpB[:, 0:512], inc=s_ve)

            if L2STOP == 2:
                l2drain()
                return
            wq2 = bfb[:, 0:1024]
            wk2 = bfb[:, 1024:2048]
            wv2 = bfb[:, 2048:3072]
            qgn = bfb[:, 3072:3076]
            mposT = vecs[:, 256:384]
            mnegT = vecs[:, 384:512]
            wproj_sb = bfa[0:64, 8832:10880]
            vaug_p = lambda h, ck: bfa[:, h * 2080 + ck * 65:
                                       h * 2080 + ck * 65 + 65]
            vaug_n = lambda h, ck: bfa[:, 4160 + h * 2080 + ck * 65:
                                       4160 + h * 2080 + ck * 65 + 65]
            po_t2 = bfa[:, 8320:8832]
            slab2 = lambda dc: bfc[:, dc * 512:(dc + 1) * 512]
            E2 = [bfc[:, 13312 + k * 512: 13312 + (k + 1) * 512]
                  for k in range(4)]
            k2_sb = bfc[:, 5120:9216]
            q2_sb = bfc[:, 9216:13312]

            GP.wait(s_mm)
            GP.wait(s_ve)
            GP.dma_start(out=wq2.rearrange("q (dc j) -> q dc j", dc=8),
                         in_=l2["wq"].rearrange("(dc p) j -> p dc j", p=128),
                         inc=s_gp, n=16)
            GP.dma_start(out=wk2.rearrange("q (dc j) -> q dc j", dc=8),
                         in_=l2["wk"].rearrange("(dc p) j -> p dc j", p=128),
                         inc=s_gp, n=16)
            GP.dma_start(out=wv2.rearrange("q (dc j) -> q dc j", dc=8),
                         in_=l2["wv"].rearrange("(dc p) j -> p dc j", p=128),
                         inc=s_gp, n=16)
            GP.dma_start(out=wproj_sb, in_=l2["wproj"], inc=s_gp, n=16)
            l2fw_mark = s_gp.v
            SY.wait(s_ve)
            SY.dma_start(out=tmpA[:, 0:4], in_=l2["qg"], inc=s_ld, n=16)
            SC.wait(s_ld)
            SC.activation(tmpC[:, 0:4], tmpA[:, 0:4], AF.Square, inc=s_sc)
            TE.wait(s_sc)
            TE.wait(s_ve)
            TE.matmul(psB[0:1, 0:4], ONES_COL, tmpC[:, 0:4], start=True,
                      stop=True, inc=s_mm)
            VE.wait(s_mm)
            VE.reciprocal(tmpB[0:1, 0:4], psB[0:1, 0:4], inc=s_ve)
            SC.wait(s_ve)
            SC.activation(tmpB[0:1, 4:8], tmpB[0:1, 0:4], AF.Sqrt, inc=s_sc)
            TE.wait(s_sc)
            TE.matmul(psD[:, 0:4], ONES_ROW, tmpB[0:1, 4:8], start=True,
                      stop=True, inc=s_mm)
            VE.wait(s_mm)
            VE.tensor_mul(qgn, tmpA[:, 0:4], psD[:, 0:4], inc=s_ve)

            if L2STOP in (24, 25):
                if L2STOP == 24:
                    for _ in range(660):
                        TE.matmul(psB[0:1, 0:4], ONES_COL, tmpC[:, 0:4],
                                  start=True, stop=True, inc=s_mm)
                    for _ in range(600):
                        VE.tensor_copy(tmpB[:, 0:1], ONES_COL, inc=s_ve)
                l2drain()
                return
            TE.wait(s_gp, l2fw_mark)
            for seq in range(4):
                for tseg in range(8):
                    slot = 2 * seq + tseg // 4
                    col = (tseg % 4) * 512
                    SY.wait(s_cc, ag2_mark)
                    SY.wait(s_mm)
                    SY.wait(s_ve)
                    for dc in range(DC):
                        SY.dma_start(
                            out=slab2(dc),
                            in_=xn_ag.ap()[slot * DM + dc * 128:
                                           slot * DM + (dc + 1) * 128,
                                           col:col + 512], inc=s_ld, n=16)
                    if L2STOP == 21:
                        continue
                    TE.wait(s_ld)
                    TE.wait(s_ve)
                    for dc in range(DC):
                        TE.matmul(psD[:, :], wq2[:, dc * 128:(dc + 1) * 128],
                                  slab2(dc), start=(dc == 0),
                                  stop=(dc == DC - 1), inc=s_mm)
                    VE.wait(s_mm)
                    VE.wait(s_st)
                    VE.tensor_copy(E2[0], psD[:, :], inc=s_ve)
                    VE.tensor_copy(fpw[:, 0:512], psD[:, :], inc=s_ve)
                    SC.wait(s_ve)
                    SC.activation(tmpC[:, 0:512], fpw[:, 0:512], AF.Square,
                                  inc=s_sc)
                    if L2STOP in (20, 22):
                        continue
                    for h in range(2):
                        TE.wait(s_ve)
                        if L2STOP == 26:
                            TE.matmul(psB[0:1, :], half[:, h:h + 1],
                                      tmpC[:, 0:512], start=True, stop=True,
                                      inc=s_mm)
                        else:
                            TE.matmul(psB[0:1, :], qgn[:, 2 * h:2 * h + 1],
                                      E2[0], start=True, stop=True, inc=s_mm)
                        TE.wait(s_sc)
                        TE.matmul(psC[0:1, :], half[:, h:h + 1],
                                  tmpC[:, 0:512], start=True, stop=True,
                                  inc=s_mm)
                        VE.wait(s_mm)
                        VE.reciprocal(tmpB[0:1, 0:512], psC[0:1, :], inc=s_ve)
                        SC.wait(s_ve)
                        SC.activation(tmpB[0:1, 512:1024], tmpB[0:1, 0:512],
                                      AF.Sqrt, inc=s_sc)
                        VE.wait(s_sc)
                        if h == 0:
                            VE.tensor_mul(tmpA[0:1, 0:512], psB[0:1, :],
                                          tmpB[0:1, 512:1024], inc=s_ve)
                        else:
                            VE.tensor_mul(tmpB[0:1, 0:512], psB[0:1, :],
                                          tmpB[0:1, 512:1024], inc=s_ve)
                            VE.tensor_add(tmpA[0:1, 0:512], tmpA[0:1, 0:512],
                                          tmpB[0:1, 0:512], inc=s_ve)
                        TE.wait(s_ve)
                    g0 = seq * 4096 + tseg * 512
                    if L2STOP not in (21, 22, 23):
                        SY.wait(s_ve)
                        SY.dma_start(out=sim_pub.ap()[0:1, g0:g0 + 512],
                                     in_=tmpA[0:1, 0:512], inc=s_st, n=16)
            if L2STOP in (20, 21, 22, 23, 26, 27):
                l2drain()
                return
            GP.wait(s_ve, st_barrier())
            GP.collective_compute("AllReduce", ALU.add, replica_groups=RG,
                                  ins=[sim_pub.ap()], outs=[sim_ar.ap()],
                                  inc=s_cc)
            sim_mark = s_cc.v

            SY.wait(s_cc, sim_mark)
            SY.dma_start(out=tmpA[:, 0:128],
                         in_=sim_ar.ap().rearrange("o (p j) -> p (o j)",
                                                   p=128), inc=s_ld, n=16)
            VE.wait(s_ld)
            if L2STOP == 28:
                l2drain()
                return
            VE.scalar_tensor_tensor(tmpB[:, 0:128], regt[:, :], -1e30,
                                    tmpA[:, 0:128], ALU.mult, ALU.add,
                                    inc=s_ve)
            VE.reduce_max(tmpB[:, 128:129], tmpB[:, 0:128], AX, inc=s_ve)
            pmax_to_scalar(tmpB[:, 128:129], tmpB[0:1, 130:131])
            VE.scalar_tensor_tensor(tmpB[:, 0:128], regt[:, :], 1e30,
                                    tmpA[:, 0:128], ALU.mult, ALU.add,
                                    inc=s_ve)
            VE.tensor_reduce(tmpB[:, 129:130], tmpB[:, 0:128], AX, ALU.min,
                             inc=s_ve)
            TE.wait(s_ve)
            TE.transpose(psE[0:1, 0:128], tmpB[:, 129:130], ID_F, inc=s_mm)
            VE.wait(s_mm)
            VE.tensor_reduce(tmpB[0:1, 131:132], psE[0:1, 0:128], AX, ALU.min,
                             inc=s_ve)
            VE.tensor_sub(tmpB[0:1, 132:133], tmpB[0:1, 130:131],
                          tmpB[0:1, 131:132], inc=s_ve)
            VE.scalar_tensor_tensor(tmpB[0:1, 132:133], tmpB[0:1, 132:133],
                                    0.9, tmpB[0:1, 131:132], ALU.mult,
                                    ALU.add, inc=s_ve)
            thb = vecs[:, 230:231]
            bcast_col(tmpB[0:1, 132:133], thb)
            VE.tensor_scalar(tmpB[:, 0:128], tmpA[:, 0:128], thb, None,
                             ALU.is_gt, inc=s_ve)
            VE.tensor_max(tmpA[:, 128:256], tmpB[:, 0:128], regt[:, :], inc=s_ve)
            VE.tensor_scalar(tmpA[:, 256:384], tmpB[:, 0:128], -1.0, 1.0,
                             ALU.mult, ALU.add, inc=s_ve)
            VE.tensor_max(tmpA[:, 256:384], tmpA[:, 256:384], regt[:, :], inc=s_ve)
            VE.tensor_mul(tmpA[:, 384:512], tmpB[:, 0:128], regt[:, :], inc=s_ve)
            VE.tensor_sub(tmpA[:, 384:512], tmpB[:, 0:128], tmpA[:, 384:512],
                          inc=s_ve)
            VE.scalar_tensor_tensor(tmpA[:, 384:512], regt[:, :], 0.5,
                                    tmpA[:, 384:512], ALU.mult, ALU.add,
                                    inc=s_ve)
            SY.wait(s_ve)
            SY.dma_start(out=mq_hbm.ap().rearrange("o (p j) -> p (o j)",
                                                    p=128),
                         in_=tmpA[:, 384:512], inc=s_st, n=16)
            TE.wait(s_ve)
            TE.transpose(psB[:, 0:128], tmpA[:, 128:256], ID_F, inc=s_mm)
            VE.wait(s_mm)
            VE.tensor_copy(mposT, psB[:, 0:128], inc=s_ve)
            TE.wait(s_ve)
            TE.transpose(psB[:, 0:128], tmpA[:, 256:384], ID_F, inc=s_mm)
            VE.wait(s_mm)
            VE.tensor_copy(mnegT, psB[:, 0:128], inc=s_ve)

            if L2STOP == 3:
                l2drain()
                return
            VE.wait(s_st)
            for seq in range(4):
                for tseg in range(8):
                    slot = 2 * seq + tseg // 4
                    col = (tseg % 4) * 512
                    SY.wait(s_mm)
                    SY.wait(s_ve)
                    for dc in range(DC):
                        SY.dma_start(
                            out=slab2(dc),
                            in_=xn_ag.ap()[slot * DM + dc * 128:
                                           slot * DM + (dc + 1) * 128,
                                           col:col + 512], inc=s_ld, n=16)
                    TE.wait(s_ld)
                    TE.wait(s_ve)
                    for dc in range(DC):
                        TE.matmul(psD[:, :], wq2[:, dc * 128:(dc + 1) * 128],
                                  slab2(dc), start=(dc == 0),
                                  stop=(dc == DC - 1), inc=s_mm)
                    for dc in range(DC):
                        TE.matmul(psE[:, :], wk2[:, dc * 128:(dc + 1) * 128],
                                  slab2(dc), start=(dc == 0),
                                  stop=(dc == DC - 1), inc=s_mm)
                    for tk in range(4):
                        pss = psB if tk % 2 == 0 else psC
                        for dc in range(DC):
                            TE.matmul(pss[:, 0:128],
                                      slab2(dc)[:, tk * 128:(tk + 1) * 128],
                                      wv2[:, dc * 128:(dc + 1) * 128],
                                      start=(dc == 0), stop=(dc == DC - 1),
                                      inc=s_mm)
                        ck = tseg * 4 + tk
                        gck = seq * 32 + ck
                        VE.wait(s_mm)
                        for h in range(2):
                            VE.tensor_scalar(
                                vaug_p(h, ck)[:, 0:64],
                                pss[:, h * 64:h * 64 + 64],
                                mposT[:, gck:gck + 1], None, ALU.mult,
                                inc=s_ve)
                            VE.tensor_scalar(
                                vaug_n(h, ck)[:, 0:64],
                                pss[:, h * 64:h * 64 + 64],
                                mnegT[:, gck:gck + 1], None, ALU.mult,
                                inc=s_ve)
                            VE.tensor_copy(vaug_p(h, ck)[:, 64:65],
                                           mposT[:, gck:gck + 1], inc=s_ve)
                            VE.tensor_copy(vaug_n(h, ck)[:, 64:65],
                                           mnegT[:, gck:gck + 1], inc=s_ve)
                        if tk >= 1:
                            TE.wait(s_ve)
                    VE.wait(s_mm)
                    VE.tensor_copy(q2_sb[:, tseg * 512:(tseg + 1) * 512],
                                   psD[:, :], inc=s_ve)
                    VE.tensor_copy(k2_sb[:, tseg * 512:(tseg + 1) * 512],
                                   psE[:, :], inc=s_ve)
                for h in range(2):
                    VE.tensor_copy(qm[h], zc2[0:1, :], inc=s_ve)
                    VE.tensor_copy(km[h], zc2[0:1, :], inc=s_ve)
                for part in range(8):
                    for (buf, acc) in ((q2_sb, qm), (k2_sb, km)):
                        SC.wait(s_ve)
                        SC.activation(tmpC[:, 0:512],
                                      buf[:, part * 512:(part + 1) * 512],
                                      AF.Square, inc=s_sc)
                        for h in range(2):
                            TE.wait(s_sc)
                            TE.wait(s_ve)
                            TE.matmul(psD[0:1, 0:512], half[:, h:h + 1],
                                      tmpC[:, 0:512], start=True, stop=True,
                                      inc=s_mm)
                            VE.wait(s_mm)
                            VE.reduce_max(t11a2, psD[0:1, 0:512], AX,
                                          inc=s_ve)
                            VE.tensor_tensor(acc[h], acc[h], t11a2, ALU.max,
                                             inc=s_ve)
                for h in range(2):
                    VE.tensor_mul(t11a2, qm[h], km[h], inc=s_ve)
                    SC.wait(s_ve)
                    SC.activation(t11b2, t11a2, AF.Sqrt, inc=s_sc)
                    VE.wait(s_sc)
                    VE.tensor_scalar(t11a2, t11b2, -0.125, None, ALU.mult,
                                     inc=s_ve)
                    bcast_col(t11a2, negc2[h])
                for qb in range(8):
                    for h in range(2):
                        av_marks = []
                        qk_marks = []
                        sc_marks = []
                        def qk2(kb):
                            if kb >= 4:
                                TE.wait(s_sc, sc_marks[kb - 4])
                            TE.matmul(psA[:, (kb % 4) * 512:
                                          (kb % 4) * 512 + 512],
                                      k2_sb[h * 64:h * 64 + 64,
                                            kb * 128:(kb + 1) * 128],
                                      q2_sb[h * 64:h * 64 + 64,
                                            qb * 512:(qb + 1) * 512],
                                      start=True, stop=True, inc=s_mm)
                            qk_marks.append(s_mm.v)
                        for kb in range(32):
                            if kb == 0:
                                qk2(0)
                            if kb < 31:
                                qk2(kb + 1)
                            SC.wait(s_mm, qk_marks[kb])
                            if kb >= 4:
                                SC.wait(s_mm, av_marks[kb - 4])
                            SC.activation(E2[kb % 4],
                                          psA[:, (kb % 4) * 512:
                                              (kb % 4) * 512 + 512],
                                          AF.Exp, scale=qcn[:, 6:7],
                                          bias=negc2[h], inc=s_sc)
                            sc_marks.append(s_sc.v)
                            TE.wait(s_sc, sc_marks[kb])
                            TE.matmul(psB[0:65, :], vaug_p(h, kb),
                                      E2[kb % 4], start=(kb == 0),
                                      stop=(kb == 31), inc=s_mm)
                            TE.matmul(psC[0:65, :], vaug_n(h, kb),
                                      E2[kb % 4], start=(kb == 0),
                                      stop=(kb == 31), inc=s_mm)
                            av_marks.append(s_mm.v)
                        for (psx, oofs) in ((psB, 0), (psC, 1024)):
                            VE.wait(s_mm)
                            VE.reciprocal(tmpB[0:1, 0:512], psx[64:65, :],
                                          inc=s_ve)
                            TE.wait(s_ve)
                            TE.matmul(psD[0:64, :], ONES_ROW[0:1, 0:64],
                                      tmpB[0:1, 0:512], start=True, stop=True,
                                      inc=s_mm)
                            VE.wait(s_mm)
                            VE.tensor_copy(tmpA[0:64, 0:512], psD[0:64, :],
                                           inc=s_ve)
                            VE.tensor_mul(bfo[:, oofs + h * 512:
                                              oofs + (h + 1) * 512],
                                          psx[0:64, :], tmpA[0:64, 0:512],
                                          inc=s_ve)
                            TE.wait(s_ve)
                    g0 = seq * 4096 + qb * 512
                    SY.wait(s_ve)
                    SY.dma_start(out=tmpB[0:1, 512:1024],
                                 in_=mq_hbm.ap()[0:1, g0:g0 + 512],
                                 inc=s_ld, n=16)
                    TE.wait(s_ld)
                    TE.matmul(psD[0:64, :], ONES_ROW[0:1, 0:64],
                              tmpB[0:1, 512:1024], start=True, stop=True,
                              inc=s_mm)
                    VE.wait(s_mm)
                    VE.tensor_copy(tmpA[0:64, 512:1024], psD[0:64, :],
                                   inc=s_ve)
                    for h in range(2):
                        VE.tensor_sub(bfo[:, h * 512:(h + 1) * 512],
                                      bfo[:, h * 512:(h + 1) * 512],
                                      bfo[:, 1024 + h * 512:
                                          1024 + (h + 1) * 512], inc=s_ve)
                        VE.tensor_mul(bfo[:, h * 512:(h + 1) * 512],
                                      bfo[:, h * 512:(h + 1) * 512],
                                      tmpA[0:64, 512:1024], inc=s_ve)
                        VE.tensor_add(bfo[:, h * 512:(h + 1) * 512],
                                      bfo[:, h * 512:(h + 1) * 512],
                                      bfo[:, 1024 + h * 512:
                                          1024 + (h + 1) * 512], inc=s_ve)
                    rank = 2 * seq + qb // 4
                    colq = (qb % 4) * 512
                    for do in range(DC):
                        pss = psD if do % 2 == 0 else psE
                        TE.wait(s_ve)
                        for h in range(2):
                            TE.matmul(pss[:, :],
                                      wproj_sb[:, h * 1024 + do * 128:
                                               h * 1024 + (do + 1) * 128],
                                      bfo[:, h * 512:(h + 1) * 512],
                                      start=(h == 0), stop=(h == 1),
                                      inc=s_mm)
                        VE.wait(s_mm)
                        VE.wait(s_st)
                        VE.tensor_copy(po_t2, pss[:, :], inc=s_ve)
                        SY.wait(s_ve)
                        SY.dma_start(
                            out=po_hbm.ap()[rank * DM + do * 128:
                                            rank * DM + (do + 1) * 128,
                                            colq:colq + 512],
                            in_=po_t2, inc=s_st, n=16)
            GP.wait(s_ve, st_barrier())
            GP.collective_compute("ReduceScatter", ALU.add, replica_groups=RG,
                                  ins=[po_hbm.ap()], outs=[rs_out.ap()],
                                  inc=s_cc)
            rs2_mark = s_cc.v
            SY.wait(s_cc, rs2_mark)
            SY.wait(s_mm)
            for dc in range(DC):
                SY.dma_start(out=fm(bfc, dc),
                             in_=rs_out.ap()[dc * 128:(dc + 1) * 128, :],
                             inc=s_ld, n=16)
            VE.wait(s_ld)
            for dc in range(DC):
                for tb in range(NTB):
                    VE.tensor_scalar(tmpB[:, 0:512], fm(bfc, dc, tb),
                                     bproj_sb[:, dc:dc + 1], None, ALU.add,
                                     inc=s_ve)
                    VE.tensor_add(fm(x32, dc, tb), fm(x32, dc, tb),
                                  tmpB[:, 0:512], inc=s_ve)

        def emit_l3():
            _vs[0] = 0
            bout_sb = vecs[:, 96:104]
            qm3 = [vecs[0:1, 104:105], vecs[0:1, 105:106]]
            km3 = [vecs[0:1, 106:107], vecs[0:1, 107:108]]
            t11a3 = vecs[0:1, 108:109]
            t11b3 = vecs[0:1, 109:110]
            negc3 = [vecs[:, 110:111], vecs[:, 111:112]]
            zc3 = vecs[:, 112:113]
            VE.memset(zc3, 0.0, inc=s_ve)

            emit_layernorm(x32, bfc)
            ln3_ve = s_ve.v
            SY.wait(s_ve, ln3_ve)
            for dc in range(DC):
                SY.dma_start(out=xn_pub.ap()[dc * 128:(dc + 1) * 128, :],
                             in_=fm(bfc, dc), inc=s_st, n=16)
            GP.wait(s_ve, st_barrier())
            GP.collective_compute("AllGather", ALU.bypass, replica_groups=RG,
                                  ins=[xn_pub.ap()], outs=[xn_ag.ap()],
                                  inc=s_cc)
            ag3_mark = s_cc.v

            wq3 = bfb[:, 0:1024]
            wk3 = bfb[:, 1024:2048]
            wv3 = bfb[:, 2048:3072]
            wout_sb = bfb[0:64, 3072:5120]
            GP.wait(s_mm)
            GP.dma_start(out=wq3.rearrange("q (dc j) -> q dc j", dc=8),
                         in_=l3["wq"].rearrange("(dc p) j -> p dc j", p=128),
                         inc=s_gp, n=16)
            GP.dma_start(out=wk3.rearrange("q (dc j) -> q dc j", dc=8),
                         in_=l3["wk"].rearrange("(dc p) j -> p dc j", p=128),
                         inc=s_gp, n=16)
            GP.dma_start(out=wv3.rearrange("q (dc j) -> q dc j", dc=8),
                         in_=l3["wv"].rearrange("(dc p) j -> p dc j", p=128),
                         inc=s_gp, n=16)
            GP.dma_start(out=wout_sb, in_=l3["wout"], inc=s_gp, n=16)
            SY.dma_start(out=bout_sb, in_=l3["bout"], inc=s_ld, n=16)
            l3w_mark = s_gp.v

            for g in range(32):
                k = g % 2
                GP.wait(s_mm)
                GP.dma_start(out=zstr[:, k * 1024:k * 1024 + 128],
                             in_=l3["rmbd"][g * 128:(g + 1) * 128, :],
                             inc=s_gp, n=16)
                GP.dma_start(out=bfa[:, k * 512:(k + 1) * 512],
                             in_=l3["rubd"][g * 128:(g + 1) * 128, :],
                             inc=s_gp, n=16)
                TE.wait(s_gp)
                TE.wait(s_ve)
                pss = psB if k == 0 else psC
                TE.matmul(pss[:, :], zstr[:, k * 1024:k * 1024 + 128],
                          bfa[:, k * 512:(k + 1) * 512], start=True,
                          stop=True, inc=s_mm)
                VE.wait(s_mm)
                VE.wait(s_st)
                VE.tensor_copy(bfa[:, 1024 + k * 512:1024 + (k + 1) * 512],
                               pss[:, :], inc=s_ve)
                SY.wait(s_ve)
                SY.dma_start(out=gg_hbm.ap()[g * 128:(g + 1) * 128, :],
                             in_=bfa[:, 1024 + k * 512:1024 + (k + 1) * 512],
                             inc=s_st, n=16)

            for tb in range(NTB):
                for g in range(32):
                    dj = g % 8
                    k = g % 2
                    if g == 0:
                        GP.wait(s_mm)
                        if tb == 0:
                            GP.wait(s_ve, st_barrier())
                        GP.dma_start(out=bfa[:, 0:512],
                                     in_=gg_hbm.ap()[0:128, :], inc=s_gp,
                                     n=16)
                    if g < 31:
                        GP.wait(s_mm)
                        GP.dma_start(
                            out=bfa[:, ((g + 1) % 2) * 512:
                                    ((g + 1) % 2) * 512 + 512],
                            in_=gg_hbm.ap()[(g + 1) * 128:(g + 2) * 128, :],
                            inc=(s_gp if (g + 1) % 2 == 0 else s_go), n=16)
                    idx = tb * 32 + g
                    GP.wait(s_mm)
                    GP.wait(s_st)
                    GP.dma_start(
                        out=zstr[:, (idx % 2) * 1024:(idx % 2) * 1024 + 1024],
                        in_=l3["wcT"][idx * 128:(idx + 1) * 128, :],
                        inc=s_pc, n=16)
                    SY.wait(s_pc)
                    SY.dma_start(
                        out=wc_bh.ap()[idx * 128:(idx + 1) * 128, :],
                        in_=zstr[:, (idx % 2) * 1024:(idx % 2) * 1024 + 1024],
                        inc=s_st, n=16)
                    TE.wait(s_gp if k == 0 else s_go)
                    for p in range(4):
                        pss = [psB, psC, psD, psE][p]
                        TE.wait(s_sc)
                        TE.matmul(pss[:, :],
                                  bfa[:, k * 512 + p * 128:
                                      k * 512 + (p + 1) * 128],
                                  fm(bfc, dj, tb), start=True, stop=True,
                                  inc=s_mm)
                    SC.wait(s_mm)
                    SC.wait(s_st)
                    for p in range(4):
                        pss = [psB, psC, psD, psE][p]
                        SC.activation(bfa[:, 1024 + p * 512:
                                          1024 + (p + 1) * 512],
                                      pss[:, :], AF.Gelu,
                                      bias=qcn[:, 9:10], inc=s_sc)
                    SY.wait(s_sc)
                    for p in range(4):
                        hc = g * 4 + p
                        SY.dma_start(
                            out=h3_hbm.ap()[hc * 128:(hc + 1) * 128,
                                            tb * 512:(tb + 1) * 512],
                            in_=bfa[:, 1024 + p * 512:1024 + (p + 1) * 512],
                            inc=s_st, n=16)

            pslist3 = [psA[:, 0:512], psA[:, 512:1024], psA[:, 1024:1536],
                       psA[:, 1536:2048], psB[:, :], psC[:, :], psD[:, :],
                       psE[:, :]]
            def l3_load(tb, hc):
                k = hc % 2
                SY.dma_start(out=bfa[:, k * 1024:(k + 1) * 1024],
                             in_=wc_bh.ap()[hc * 128:(hc + 1) * 128, :],
                             inc=(s_ld if k == 0 else s_lo), n=16)
                SY.dma_start(out=bfa[:, 2048 + k * 512:2048 + (k + 1) * 512],
                             in_=h3_hbm.ap()[hc * 128:(hc + 1) * 128,
                                             tb * 512:(tb + 1) * 512],
                             inc=(s_gp if k == 0 else s_go), n=16)
            for tb in range(NTB):
                for hc in range(128):
                    k = hc % 2
                    if hc == 0:
                        SY.wait(s_mm)
                        if tb == 0:
                            SY.wait(s_st)
                            TE.wait(s_sc)
                        l3_load(tb, 0)
                    if hc < 127:
                        SY.wait(s_mm)
                        l3_load(tb, hc + 1)
                    TE.wait(s_ld if k == 0 else s_lo)
                    TE.wait(s_gp if k == 0 else s_go)
                    for do in range(DC):
                        TE.matmul(pslist3[do],
                                  bfa[:, k * 1024 + do * 128:
                                      k * 1024 + (do + 1) * 128],
                                  bfa[:, 2048 + k * 512:2048 + (k + 1) * 512],
                                  start=(hc == 0), stop=(hc == 127),
                                  inc=s_mm)
                VE.wait(s_mm)
                for do in range(DC):
                    resadd(pslist3[do], do, tb)
                TE.wait(s_ve)

            slab3 = lambda dc: bfc[:, dc * 512:(dc + 1) * 512]
            E3 = [bfc[:, 13312 + k * 512: 13312 + (k + 1) * 512]
                  for k in range(4)]
            k3_sb = bfc[:, 5120:9216]
            q3_sb = bfc[:, 9216:13312]
            vaug3 = lambda h, ck: bfa[:, h * 2080 + ck * 65:
                                      h * 2080 + ck * 65 + 65]
            po_t3 = bfa[:, 8320:8832]
            TE.wait(s_gp)
            TE.wait(s_ve)
            for h in range(2):
                for ck in range(32):
                    VE.tensor_copy(vaug3(h, ck)[:, 64:65], ONES_COL,
                                   inc=s_ve)
            for seq in range(4):
                for tseg in range(8):
                    slot = 2 * seq + tseg // 4
                    col = (tseg % 4) * 512
                    SY.wait(s_cc, ag3_mark)
                    SY.wait(s_mm)
                    SY.wait(s_ve)
                    for dc in range(DC):
                        SY.dma_start(
                            out=slab3(dc),
                            in_=xn_ag.ap()[slot * DM + dc * 128:
                                           slot * DM + (dc + 1) * 128,
                                           col:col + 512], inc=s_ld, n=16)
                    TE.wait(s_ld)
                    TE.wait(s_ve)
                    for dc in range(DC):
                        TE.matmul(psD[:, :], wq3[:, dc * 128:(dc + 1) * 128],
                                  slab3(dc), start=(dc == 0),
                                  stop=(dc == DC - 1), inc=s_mm)
                    for dc in range(DC):
                        TE.matmul(psE[:, :], wk3[:, dc * 128:(dc + 1) * 128],
                                  slab3(dc), start=(dc == 0),
                                  stop=(dc == DC - 1), inc=s_mm)
                    for tk in range(4):
                        pss = psB if tk % 2 == 0 else psC
                        for dc in range(DC):
                            TE.matmul(pss[:, 0:128],
                                      slab3(dc)[:, tk * 128:(tk + 1) * 128],
                                      wv3[:, dc * 128:(dc + 1) * 128],
                                      start=(dc == 0), stop=(dc == DC - 1),
                                      inc=s_mm)
                        ck = tseg * 4 + tk
                        VE.wait(s_mm)
                        for h in range(2):
                            VE.tensor_copy(vaug3(h, ck)[:, 0:64],
                                           pss[:, h * 64:h * 64 + 64],
                                           inc=s_ve)
                        if tk >= 1:
                            TE.wait(s_ve)
                    VE.wait(s_mm)
                    VE.tensor_copy(q3_sb[:, tseg * 512:(tseg + 1) * 512],
                                   psD[:, :], inc=s_ve)
                    VE.tensor_copy(k3_sb[:, tseg * 512:(tseg + 1) * 512],
                                   psE[:, :], inc=s_ve)
                for h in range(2):
                    VE.tensor_copy(qm3[h], zc3[0:1, :], inc=s_ve)
                    VE.tensor_copy(km3[h], zc3[0:1, :], inc=s_ve)
                for part in range(8):
                    for (buf, acc) in ((q3_sb, qm3), (k3_sb, km3)):
                        SC.wait(s_ve)
                        SC.activation(tmpC[:, 0:512],
                                      buf[:, part * 512:(part + 1) * 512],
                                      AF.Square, inc=s_sc)
                        for h in range(2):
                            TE.wait(s_sc)
                            TE.wait(s_ve)
                            TE.matmul(psD[0:1, 0:512], half[:, h:h + 1],
                                      tmpC[:, 0:512], start=True, stop=True,
                                      inc=s_mm)
                            VE.wait(s_mm)
                            VE.reduce_max(t11a3, psD[0:1, 0:512], AX,
                                          inc=s_ve)
                            VE.tensor_tensor(acc[h], acc[h], t11a3, ALU.max,
                                             inc=s_ve)
                for h in range(2):
                    VE.tensor_mul(t11a3, qm3[h], km3[h], inc=s_ve)
                    SC.wait(s_ve)
                    SC.activation(t11b3, t11a3, AF.Sqrt, inc=s_sc)
                    VE.wait(s_sc)
                    VE.tensor_scalar(t11a3, t11b3, -0.125, None, ALU.mult,
                                     inc=s_ve)
                    bcast_col(t11a3, negc3[h])
                for qb in range(8):
                    for h in range(2):
                        av_marks = []
                        qk_marks = []
                        sc_marks = []
                        def qk3(kb):
                            if kb >= 4:
                                TE.wait(s_sc, sc_marks[kb - 4])
                            TE.matmul(psA[:, (kb % 4) * 512:
                                          (kb % 4) * 512 + 512],
                                      k3_sb[h * 64:h * 64 + 64,
                                            kb * 128:(kb + 1) * 128],
                                      q3_sb[h * 64:h * 64 + 64,
                                            qb * 512:(qb + 1) * 512],
                                      start=True, stop=True, inc=s_mm)
                            qk_marks.append(s_mm.v)
                        for kb in range(32):
                            if kb == 0:
                                qk3(0)
                            if kb < 31:
                                qk3(kb + 1)
                            SC.wait(s_mm, qk_marks[kb])
                            if kb >= 4:
                                SC.wait(s_mm, av_marks[kb - 4])
                            SC.activation(E3[kb % 4],
                                          psA[:, (kb % 4) * 512:
                                              (kb % 4) * 512 + 512],
                                          AF.Exp, scale=qcn[:, 6:7],
                                          bias=negc3[h], inc=s_sc)
                            sc_marks.append(s_sc.v)
                            TE.wait(s_sc, sc_marks[kb])
                            TE.matmul(psB[0:65, :], vaug3(h, kb),
                                      E3[kb % 4], start=(kb == 0),
                                      stop=(kb == 31), inc=s_mm)
                            av_marks.append(s_mm.v)
                        VE.wait(s_mm)
                        VE.reciprocal(tmpB[0:1, 0:512], psB[64:65, :],
                                      inc=s_ve)
                        TE.wait(s_ve)
                        TE.matmul(psC[0:64, :], ONES_ROW[0:1, 0:64],
                                  tmpB[0:1, 0:512], start=True, stop=True,
                                  inc=s_mm)
                        VE.wait(s_mm)
                        VE.tensor_copy(tmpA[0:64, 0:512], psC[0:64, :],
                                       inc=s_ve)
                        VE.tensor_mul(bfo[:, h * 512:(h + 1) * 512],
                                      psB[0:64, :], tmpA[0:64, 0:512],
                                      inc=s_ve)
                        TE.wait(s_ve)
                    rank = 2 * seq + qb // 4
                    colq = (qb % 4) * 512
                    for do in range(DC):
                        pss = psD if do % 2 == 0 else psE
                        TE.wait(s_ve)
                        for h in range(2):
                            TE.matmul(pss[:, :],
                                      wout_sb[:, h * 1024 + do * 128:
                                              h * 1024 + (do + 1) * 128],
                                      bfo[:, h * 512:(h + 1) * 512],
                                      start=(h == 0), stop=(h == 1),
                                      inc=s_mm)
                        VE.wait(s_mm)
                        VE.wait(s_st)
                        VE.tensor_copy(po_t3, pss[:, :], inc=s_ve)
                        SY.wait(s_ve)
                        SY.dma_start(
                            out=po_hbm.ap()[rank * DM + do * 128:
                                            rank * DM + (do + 1) * 128,
                                            colq:colq + 512],
                            in_=po_t3, inc=s_st, n=16)
            GP.wait(s_ve, st_barrier())
            GP.collective_compute("ReduceScatter", ALU.add, replica_groups=RG,
                                  ins=[po_hbm.ap()], outs=[rs_out.ap()],
                                  inc=s_cc)
            rs3_mark = s_cc.v
            SY.wait(s_cc, rs3_mark)
            SY.wait(s_mm)
            for dc in range(DC):
                SY.dma_start(out=fm(bfc, dc),
                             in_=rs_out.ap()[dc * 128:(dc + 1) * 128, :],
                             inc=s_ld, n=16)
            VE.wait(s_ld)
            for dc in range(DC):
                for tb in range(NTB):
                    VE.tensor_scalar(tmpB[:, 0:512], fm(bfc, dc, tb),
                                     bout_sb[:, dc:dc + 1], None, ALU.add,
                                     inc=s_ve)
                    VE.tensor_add(fm(x32, dc, tb), fm(x32, dc, tb),
                                  tmpB[:, 0:512], inc=s_ve)


        emit_l0()
        if nlayers >= 2:
            emit_l1()
        if nlayers >= 3:
            emit_l2()
        if nlayers >= 4:
            emit_l3()

        SY.wait(s_ve)
        SY.wait(s_mm)
        SY.wait(s_sc)
        SY.wait(s_gp)
        for dc in range(DC):
            SY.dma_start(out=out_fm[dc * 128:(dc + 1) * 128, :],
                         in_=fm(x32, dc), inc=s_st, n=16)
        SY.wait(s_st)

        block = es.enter_context(nc.Block())
        block.sync(lambda e: [fn(e) for fn in Q["sy"]])
        block.tensor(lambda e: [fn(e) for fn in Q["te"]])
        block.vector(lambda e: [fn(e) for fn in Q["ve"]])
        block.scalar(lambda e: [fn(e) for fn in Q["sc"]])
        block.gpsimd(lambda e: [fn(e) for fn in Q["gp"]])
    nc._qlens = {k: len(v) for k, v in Q.items()}
    nc._semvals = {n: s.v for n, s in zip(
        ("s_ld", "s_gp", "s_cc", "s_mm", "s_ve", "s_sc", "s_st"),
        (s_ld, s_gp, s_cc, s_mm, s_ve, s_sc, s_st))}
    return nc


def stage_core(c, I, nlayers=NLAYERS):
    x = np.asarray(I["L0_attn_x"]).reshape(16384, DM)
    d = {}
    d["x0_fm"] = np.ascontiguousarray(x[c * T:(c + 1) * T].T)
    d["L0_wd"] = np.asarray(I["L0_attn_Wd"])
    d["L0_bd"] = np.asarray(I["L0_attn_bd"]).reshape(DM, 1)
    d["L0_wb"] = np.asarray(I["L0_attn_Wb"])
    d["L0_bb"] = np.asarray(I["L0_attn_bb"]).reshape(16, 1)
    d["L0_wc"] = np.asarray(I["L0_attn_Wc"])
    d["L0_bc"] = np.asarray(I["L0_attn_bc"]).reshape(16, 1)
    d["L0_wo"] = np.asarray(I["L0_attn_Wo"])
    d["L0_bo"] = np.asarray(I["L0_attn_bo"]).reshape(DM, 1)
    d["L0_at"] = np.ascontiguousarray(np.asarray(I["L0_attn_A"]).T)
    d["L0_clipx"] = np.asarray(I["L0_ffn_clip_x"]).reshape(1, 1)
    d["L0_clipd"] = np.asarray(I["L0_ffn_clip_down"]).reshape(1, 1)
    d["L0_wg"] = np.asarray(I["L0_ffn_w_gate"])
    d["L0_wu"] = np.asarray(I["L0_ffn_w_up"])
    d["L0_wdn"] = np.asarray(I["L0_ffn_w_down"])
    sel = np.zeros((16, 8), np.float32)
    if c % 2 == 1:
        sel[:, c - 1] = 1.0
    d["sel_bc"] = sel
    hs = 128 * c
    WqT = np.ascontiguousarray(np.asarray(I["L1_attn_Wq"]).T)
    WkT = np.ascontiguousarray(np.asarray(I["L1_attn_Wk"]).T)
    WvT = np.ascontiguousarray(np.asarray(I["L1_attn_Wv"]).T)
    WoT = np.ascontiguousarray(np.asarray(I["L1_attn_Wo"]).T)
    d["L1_wq"] = np.ascontiguousarray(WqT[:, hs:hs + 128])
    d["L1_wk"] = np.ascontiguousarray(WkT[:, hs:hs + 128])
    d["L1_wv"] = np.ascontiguousarray(WvT[:, hs:hs + 128])
    d["L1_bq"] = np.asarray(I["L1_attn_bq"])[hs:hs + 128].reshape(128, 1)
    d["L1_bk"] = np.asarray(I["L1_attn_bk"])[hs:hs + 128].reshape(128, 1)
    d["L1_bv"] = np.asarray(I["L1_attn_bv"])[hs:hs + 128].reshape(1, 128)
    d["L1_wo"] = np.ascontiguousarray(
        np.concatenate([WoT[hs:hs + 64, :], WoT[hs + 64:hs + 128, :]], axis=1))
    d["L1_bo"] = np.ascontiguousarray(
        np.asarray(I["L1_attn_bo"]).reshape(8, 128).T)
    d["L1_w1T"] = np.ascontiguousarray(np.asarray(I["L1_ffn_w1"]).T)
    d["L1_b1"] = np.ascontiguousarray(
        np.asarray(I["L1_ffn_b1"]).reshape(32, 128).T)
    d["L1_w2T"] = np.ascontiguousarray(np.asarray(I["L1_ffn_w2"]).T)
    d["L1_b2"] = np.ascontiguousarray(
        np.asarray(I["L1_ffn_b2"]).reshape(8, 128).T)
    d["L2_tk"] = np.asarray(I["L2_attn_t_k"])
    d["L2_tkT"] = np.ascontiguousarray(np.asarray(I["L2_attn_t_k"]).T)
    d["L2_tvT"] = np.ascontiguousarray(np.asarray(I["L2_attn_t_v"]).T)
    d["L2_tqT"] = np.ascontiguousarray(np.asarray(I["L2_attn_t_q"]).T)
    d["L2_w1T"] = np.ascontiguousarray(np.asarray(I["L2_attn_W1"]).T)
    d["L2_b1"] = np.asarray(I["L2_attn_b1"]).reshape(1, 256)
    d["L2_w2T"] = np.ascontiguousarray(np.asarray(I["L2_attn_W2"]).T)
    d["L2_b2"] = np.asarray(I["L2_attn_b2"]).reshape(1, 1024)
    flc = np.zeros((128, 4), np.float32)
    fl1 = np.zeros((128, 4), np.float32)
    flc[:, c // 2] = 2.0 / (4096.0 * 1024.0)
    fl1[:, c // 2] = 1.0
    d["L2_flc"] = flc
    d["L2_fl1"] = fl1
    Wqkv = np.asarray(I["L2_ffn_W_qkv"])
    d["L2f_wq"] = np.ascontiguousarray(Wqkv[:, hs:hs + 128])
    d["L2f_wk"] = np.ascontiguousarray(Wqkv[:, 1024 + hs:1024 + hs + 128])
    d["L2f_wv"] = np.ascontiguousarray(Wqkv[:, 2048 + hs:2048 + hs + 128])
    Wp = np.asarray(I["L2_ffn_W_proj"])
    d["L2f_wproj"] = np.ascontiguousarray(
        np.concatenate([Wp[hs:hs + 64, :], Wp[hs + 64:hs + 128, :]], axis=1))
    d["L2f_bproj"] = np.ascontiguousarray(
        np.asarray(I["L2_ffn_b_proj"]).reshape(8, 128).T)
    qg = np.asarray(I["L2_ffn_g_info"])[0].reshape(-1, 16, 64)[0]
    qgp = np.ones((128, 4), np.float32)
    for hh in range(2):
        qgp[:, 2 * hh] = 0.0
        qgp[64 * hh:64 * hh + 64, 2 * hh] = qg[2 * c + hh]
    d["L2f_qg"] = qgp
    WqT3 = np.ascontiguousarray(np.asarray(I["L3_attn_Wq"]).T)
    WkT3 = np.ascontiguousarray(np.asarray(I["L3_attn_Wk"]).T)
    WvT3 = np.ascontiguousarray(np.asarray(I["L3_attn_Wv"]).T)
    WoT3 = np.ascontiguousarray(np.asarray(I["L3_attn_Wout"]).T)
    d["L3_wq"] = np.ascontiguousarray(WqT3[:, hs:hs + 128])
    d["L3_wk"] = np.ascontiguousarray(WkT3[:, hs:hs + 128])
    d["L3_wv"] = np.ascontiguousarray(WvT3[:, hs:hs + 128])
    d["L3_wout"] = np.ascontiguousarray(
        np.concatenate([WoT3[hs:hs + 64, :], WoT3[hs + 64:hs + 128, :]],
                       axis=1))
    d["L3_bout"] = np.ascontiguousarray(
        np.asarray(I["L3_attn_bout"]).reshape(8, 128).T)
    rm = np.asarray(I["L3_ffn_rail_maker"])
    ru = np.asarray(I["L3_ffn_rail_up_proj"])
    rmbd = np.zeros((32 * 128, 128), np.float32)
    rubd = np.zeros((32 * 128, 512), np.float32)
    for g in range(32):
        h3, dj = g // 8, g % 8
        for m in range(8):
            r = 8 * dj + m
            rmbd[g * 128 + 16 * m: g * 128 + 16 * m + 16,
                 16 * m:16 * m + 16] = rm[h3, r].T
            rubd[g * 128 + 16 * m: g * 128 + 16 * m + 16,
                 64 * m:64 * m + 64] = ru[h3, r].T
    d["L3_rmbd"] = rmbd
    d["L3_rubd"] = rubd
    d["L3_wcT"] = np.ascontiguousarray(np.asarray(I["L3_ffn_w_cproj"]).T)
    return d


_NC_CACHE = {}

def kernel(**inputs):
    if NLAYERS not in _NC_CACHE:
        _NC_CACHE[NLAYERS] = build(NLAYERS)
    nc = _NC_CACHE[NLAYERS]
    in_maps = [stage_core(c, inputs, NLAYERS) for c in range(NCORES)]
    res = run_bass_kernel_spmd(nc, in_maps, core_ids=list(range(NCORES)))
    out = np.concatenate([r["out_fm"].T for r in res.results], axis=0)
    return np.ascontiguousarray(out.reshape(4, 4096, DM)).astype(np.float32)


# version 07 (final)
# speedup vs baseline: 1.2275x; 1.0400x over previous
import contextlib
import numpy as np
import ml_dtypes
import concourse.bass as bass
import concourse.mybir as mybir
from concourse.bass_utils import run_bass_kernel_spmd

F32 = mybir.dt.float32
BF16 = mybir.dt.bfloat16
I32 = mybir.dt.int32
AF = mybir.ActivationFunctionType
ALU = mybir.AluOpType
AX = mybir.AxisListType.X
BF = ml_dtypes.bfloat16

NCORES = 8
T = 2048
DM = 1024
DC = DM // 128
NTB = T // 512
NLAYERS = 4
L2STOP = 0
DBG = False
EPS = 1e-5
LN2 = float(np.log(2.0))
RG = [list(range(NCORES))]


class Sem:
    __slots__ = ("h", "v")
    def __init__(self, h):
        self.h = h
        self.v = 0


class Rec:
    def __init__(self, q, own_sem=None):
        object.__setattr__(self, "_q", q)
        object.__setattr__(self, "_own", own_sem)

    def set_own(self, sem):
        object.__setattr__(self, "_own", sem)

    def wait(self, sem, val=None):
        v = sem.v if val is None else val
        assert v <= sem.v, "wait on a future semaphore value"
        if v <= 0:
            return
        self._q.append(lambda e, s=sem.h, w=v: e.wait_ge(s, w))

    def __getattr__(self, meth):
        def call(*args, inc=None, n=1, **kw):
            if meth == "matmul":
                kw.setdefault("skip_group_check", True)
            own = self._own
            if own is not None:
                if own.v > 0:
                    self._q.append(
                        lambda e, s=own.h, w=own.v: e.wait_ge(s, w))
                if inc is None:
                    inc, n = own, 1
            if inc is not None:
                inc.v += n
                self._q.append(lambda e, a=args, k=kw, s=inc.h, m=n:
                               getattr(e, meth)(*a, **k).then_inc(s, m))
            else:
                self._q.append(lambda e, a=args, k=kw: getattr(e, meth)(*a, **k))
        return call


def build(nlayers=NLAYERS, dbg=None):
    nc = bass.Bass("TRN2", target_bir_lowering=False)
    es = contextlib.ExitStack()
    if dbg is None:
        dbg = DBG
    Q = {k: [] for k in ("te", "ve", "sc", "gp", "sy")}
    TE, VE, SC, GP, SY = (Rec(Q[k]) for k in ("te", "ve", "sc", "gp", "sy"))

    def inp(name, shape, dt=F32):
        return nc.dram_tensor(name, list(shape), dt, kind="ExternalInput").ap()

    def idram(name, shape, dt=F32):
        return nc.dram_tensor(name, list(shape), dt, kind="Internal")

    x0 = inp("x0_fm", (DM, T))
    out_fm = nc.dram_tensor("out_fm", [DM, T], F32, kind="ExternalOutput").ap()
    dbg_d = nc.dram_tensor("dbg", [128, 8192], F32, kind="ExternalOutput").ap()

    wd_d = inp("L0_wd", (DM, DM))
    bd_d = inp("L0_bd", (DM, 1))
    wb_d = inp("L0_wb", (DM, 16))
    bb_d = inp("L0_bb", (16, 1))
    wc_d = inp("L0_wc", (DM, 16))
    bc_d = inp("L0_bc", (16, 1))
    wo_d = inp("L0_wo", (DM, DM))
    bo_d = inp("L0_bo", (DM, 1))
    at_d = inp("L0_at", (16, DM))
    clipx_d = inp("L0_clipx", (1, 1))
    clipd_d = inp("L0_clipd", (1, 1))
    wg_d = inp("L0_wg", (2816, DM))
    wu_d = inp("L0_wu", (2816, DM))
    wdn_d = inp("L0_wdn", (DM, 2816))
    sel_d = inp("sel_bc", (16, 8))

    l1 = {}
    if True:
        l1["wq"] = inp("L1_wq", (DM, 128))
        l1["wk"] = inp("L1_wk", (DM, 128))
        l1["wv"] = inp("L1_wv", (DM, 128))
        l1["bq"] = inp("L1_bq", (128, 1))
        l1["bk"] = inp("L1_bk", (128, 1))
        l1["bv"] = inp("L1_bv", (1, 128))
        l1["wo"] = inp("L1_wo", (64, 2 * DM))
        l1["bo"] = inp("L1_bo", (128, 8))
        l1["w1T"] = inp("L1_w1T", (DM, 4096))
        l1["b1"] = inp("L1_b1", (128, 32))
        l1["w2T"] = inp("L1_w2T", (4096, DM))
        l1["b2"] = inp("L1_b2", (128, 8))

    l2 = {}
    l2["tk"] = inp("L2_tk", (DM, DM))
    l2["tkT"] = inp("L2_tkT", (DM, DM))
    l2["tvT"] = inp("L2_tvT", (DM, DM))
    l2["tqT"] = inp("L2_tqT", (DM, DM))
    l2["w1T"] = inp("L2_w1T", (DM, 256))
    l2["b1"] = inp("L2_b1", (1, 256))
    l2["w2T"] = inp("L2_w2T", (256, DM))
    l2["b2"] = inp("L2_b2", (1, DM))
    l2["flc"] = inp("L2_flc", (128, 4))
    l2["fl1"] = inp("L2_fl1", (128, 4))
    l2["wq"] = inp("L2f_wq", (DM, 128))
    l2["wk"] = inp("L2f_wk", (DM, 128))
    l2["wv"] = inp("L2f_wv", (DM, 128))
    l2["wproj"] = inp("L2f_wproj", (64, 2 * DM))
    l2["bproj"] = inp("L2f_bproj", (128, 8))
    l2["qg"] = inp("L2f_qg", (128, 4))

    l3 = {}
    l3["wq"] = inp("L3_wq", (DM, 128))
    l3["wk"] = inp("L3_wk", (DM, 128))
    l3["wv"] = inp("L3_wv", (DM, 128))
    l3["wout"] = inp("L3_wout", (64, 2 * DM))
    l3["bout"] = inp("L3_bout", (128, 8))
    l3["rmbd"] = inp("L3_rmbd", (32 * 128, 128))
    l3["rubd"] = inp("L3_rubd", (32 * 128, 512))
    l3["wcT"] = inp("L3_wcT", (16384, DM))

    cc_pub1 = idram("cc_pub1", (64, 1))
    cc_ag1 = idram("cc_ag1", (NCORES * 64, 1))
    cc_pub2 = idram("cc_pub2", (8, 1))
    cc_ag2 = idram("cc_ag2", (NCORES * 8, 1))
    h_hbm = idram("h_hbm", (2816, T), BF16)
    wdqT_hbm = idram("wdqT_hbm", (2816, DM), BF16)
    xn_pub = idram("xn_pub", (DM, T), BF16)
    xn_ag = idram("xn_ag", (NCORES * DM, T), BF16)
    po_hbm = idram("po_hbm", (NCORES * DM, T), BF16)
    rs_out = idram("rs_out", (DM, T), BF16)
    w2q_hbm = idram("w2q_hbm", (4096, DM), BF16)
    hq_hbm = idram("hq_hbm", (4096, T), BF16)
    z_hbm = idram("z_hbm", (T, DM), BF16)
    zy_hbm = idram("zy_hbm", (T, DM), BF16)
    q2_hbm = idram("q2_hbm", (DM, T), BF16)
    g_hbm = idram("g_hbm", (T, DM), BF16)
    g_pub = idram("g_pub", (514, 1024))
    g_ar = idram("g_ar", (514, 1024))
    sim_pub = idram("sim_pub", (1, 16384))
    sim_ar = idram("sim_ar", (1, 16384))
    mq_hbm = idram("mq_hbm", (1, 16384))
    gg_hbm = idram("gg_hbm", (32 * 128, 512), BF16)
    h3_hbm = idram("h3_hbm", (16384, T), BF16)
    wc_bh = idram("wc_bh", (16384, DM), BF16)

    c_ones_f = nc.inline_tensor(np.ones((128, 128), np.float32), name="c1f")
    c_zeros = nc.inline_tensor(np.zeros((128, 256), np.float32), name="c0f")
    c_id_f = nc.inline_tensor(np.eye(128, dtype=np.float32), name="cidf")
    c_od_bf = nc.inline_tensor(
        np.full((128, 128), 1.0 / DM, np.float32).astype(BF), name="codb")
    c_id_bf = nc.inline_tensor(
        np.eye(128, dtype=np.float32).astype(BF), name="cidb")
    _msk = np.zeros((128, 4 * 512), np.float32)
    for m in range(4):
        i = np.arange(128)[:, None]
        j = np.arange(512)[None, :]
        _msk[:, m * 512:(m + 1) * 512] = (128 * m + i <= j)
    c_masks = nc.inline_tensor(_msk.astype(BF), name="cmsk")
    _ho = np.zeros((128, 2), np.float32)
    _ho[0:64, 0] = 1.0
    _ho[64:128, 1] = 1.0
    c_halfones = nc.inline_tensor(_ho, name="chalf")
    _reg = np.zeros((128, 128), np.float32)
    for _b in range(4):
        for _r in range(5):
            _t = 4096 * _b + _r
            _reg[_t // 128, _t % 128] = 1.0
    c_reg = nc.inline_tensor(_reg, name="creg")
    c_ones_bf = nc.inline_tensor(np.ones((128, 1), np.float32).astype(BF),
                                 name="c1bf")
    _ln2 = float(np.log(2.0))
    _qc = np.zeros((128, 10), np.float32)
    for _i, _v in enumerate([_ln2, -_ln2, 3 * _ln2, -3 * _ln2,
                             10 * _ln2, -10 * _ln2, 0.125, 2.0 ** -40,
                             -0.5, 0.0]):
        _qc[:, _i] = _v
    c_qc = nc.inline_tensor(_qc, name="cqc")


    with es:
        def sb(name, shape, dt):
            return es.enter_context(nc.sbuf_tensor(name, shape, dt))
        def psum(name, shape):
            return es.enter_context(nc.psum_tensor(name, shape, F32))
        def sem(name):
            return Sem(es.enter_context(nc.semaphore(name)))

        x32 = sb("x32", [128, DC * T], F32)
        bfc = sb("bfc", [128, DC * T], BF16)
        bfa = sb("bfa", [128, 12288], BF16)
        bfb = sb("bfb", [128, 6144], BF16)
        fpw = sb("fpw", [128, 4096], F32)
        s16 = sb("s16", [16, 6144], F32)
        bfo = sb("bfo", [64, 2048], BF16)
        mskb = sb("mskb", [128, 2048], BF16)
        half = sb("half", [128, 2], F32)
        ctb = sb("ctb", [16, 2048], BF16)
        vecs = sb("vecs", [128, 512], F32)
        cons = sb("cons", [128, 256], F32)
        consb = sb("consb", [128, 256], BF16)
        tmpA = sb("tmpA", [128, 1088], F32)
        tmpB = sb("tmpB", [128, 1536], F32)
        tmpC = sb("tmpC", [128, 512], F32)

        psA = psum("psA", [128, 2048])
        psB = psum("psB", [128, 512])
        psC = psum("psC", [128, 512])
        psD = psum("psD", [128, 512])
        psE = psum("psE", [128, 512])

        s_ld = sem("s_ld"); s_gp = sem("s_gp"); s_cc = sem("s_cc")
        s_mm = sem("s_mm"); s_ve = sem("s_ve"); s_sc = sem("s_sc")
        s_st = sem("s_st")
        s_pc = sem("s_pc"); s_lo = sem("s_lo"); s_go = sem("s_go")
        VE.set_own(s_ve)
        SC.set_own(s_sc)

        _vs = [0]
        def vslot():
            i = _vs[0]; _vs[0] += 1
            assert i < 256
            return vecs[:, i:i + 1]
        def vslot1():
            return vslot()[0:1, :]

        ONES_COL = cons[:, 0:1]
        ONES_ROW = cons[0:1, 0:128]
        ID_F = cons[:, 128:256]
        OD_BF = consb[:, 0:128]
        ID_BF = consb[:, 128:256]

        SY.dma_start(out=cons[:, 0:128], in_=c_ones_f.ap(), inc=s_ld, n=16)
        SY.dma_start(out=cons[:, 128:256], in_=c_id_f.ap(), inc=s_ld, n=16)
        SY.dma_start(out=consb[:, 0:128], in_=c_od_bf.ap(), inc=s_ld, n=16)
        SY.dma_start(out=consb[:, 128:256], in_=c_id_bf.ap(), inc=s_ld, n=16)
        SY.dma_start(out=mskb[:, :], in_=c_masks.ap(), inc=s_ld, n=16)
        SY.dma_start(out=half[:, :], in_=c_halfones.ap(), inc=s_ld, n=16)
        qcn = sb("qcn", [128, 10], F32)
        SY.dma_start(out=qcn[:, :], in_=c_qc.ap(), inc=s_ld, n=16)
        regt = sb("regt", [128, 128], F32)
        SY.dma_start(out=regt[:, :], in_=c_reg.ap(), inc=s_ld, n=16)
        onebf = sb("onebf", [128, 1], BF16)
        SY.dma_start(out=onebf[:, :], in_=c_ones_bf.ap(), inc=s_ld, n=16)
        zstr = sb("zstr", [128, 2048], BF16)
        for dc in range(DC):
            SY.dma_start(out=x32[:, dc * T:(dc + 1) * T],
                         in_=x0[dc * 128:(dc + 1) * 128, :], inc=s_ld, n=16)

        _tap = [0]
        tapst = sb("tapst", [128, 512], F32) if dbg else None
        def tap(ap, rows=128, cols=512):
            i = _tap[0]
            if i >= 16:
                return
            _tap[0] += 1
            VE.tensor_copy(tapst[0:rows, 0:cols], ap, inc=s_ve)
            SY.wait(s_ve)
            SY.dma_start(out=dbg_d[0:rows, i * 512:i * 512 + cols],
                         in_=tapst[0:rows, 0:cols], inc=s_st, n=16)
            VE.wait(s_st)

        def fm(tsr, dc, tb=None):
            if tb is None:
                return tsr[:, dc * T:(dc + 1) * T]
            return tsr[:, dc * T + tb * 512: dc * T + (tb + 1) * 512]

        def resadd(pss, dc, tb, bias_ap=None, scale_ap=None):
            t = tmpB[:, 1024:1536]
            if scale_ap is not None:
                VE.tensor_scalar(t, pss, scale_ap, None, ALU.mult, inc=s_ve)
            elif bias_ap is not None:
                VE.tensor_scalar(t, pss, bias_ap, None, ALU.add, inc=s_ve)
            else:
                VE.tensor_copy(t, pss, inc=s_ve)
            VE.tensor_add(fm(x32, dc, tb), fm(x32, dc, tb), t, inc=s_ve)

        def bcast_col(src11, dst_col):
            TE.wait(s_ve)
            TE.matmul(psE[:, 0:1], ONES_ROW, src11, start=True, stop=True,
                      inc=s_mm)
            VE.wait(s_mm)
            VE.tensor_copy(dst_col, psE[:, 0:1], inc=s_ve)

        def pmax_to_scalar(col128, dst11):
            TE.wait(s_ve)
            TE.transpose(psE[0:1, 0:128], col128, ID_F, inc=s_mm)
            VE.wait(s_mm)
            VE.reduce_max(dst11, psE[0:1, 0:128], AX, inc=s_ve)

        def emit_layernorm(src32, dstbf, absmax=None):
            TE.wait(s_ve)
            SC.wait(s_ve)
            m_ = tmpA[0:1, 0:512]
            t_ = tmpA[0:1, 512:1024]
            q_ = tmpB[0:1, 0:512]
            r_ = tmpB[0:1, 512:1024]
            for tb in range(NTB):
                for dc in range(DC):
                    TE.matmul(psB[0:1, :], ONES_COL, fm(src32, dc, tb),
                              start=(dc == 0), stop=(dc == DC - 1), inc=s_mm)
                for dc in range(DC):
                    SC.wait(s_mm)
                    SC.activation(tmpC[:, 0:512], fm(src32, dc, tb), AF.Square,
                                  inc=s_sc)
                    TE.wait(s_sc)
                    TE.matmul(psC[0:1, :], ONES_COL, tmpC[:, 0:512],
                              start=(dc == 0), stop=(dc == DC - 1), inc=s_mm)
                VE.wait(s_mm)
                VE.tensor_scalar(m_, psB[0:1, :], 1.0 / DM, None, ALU.mult,
                                 inc=s_ve)
                VE.tensor_scalar(t_, psC[0:1, :], 1.0 / DM, None, ALU.mult,
                                 inc=s_ve)
                VE.tensor_mul(q_, m_, m_, inc=s_ve)
                VE.tensor_sub(t_, t_, q_, inc=s_ve)
                VE.tensor_scalar(t_, t_, EPS, None, ALU.add, inc=s_ve)
                VE.reciprocal(t_, t_, inc=s_ve)
                SC.wait(s_ve)
                SC.activation(r_, t_, AF.Sqrt, inc=s_sc)
                VE.wait(s_sc)
                VE.tensor_mul(m_, m_, r_, inc=s_ve)
                TE.wait(s_ve)
                TE.matmul(psD[:, :], ONES_ROW, r_, start=True, stop=True,
                          inc=s_mm)
                TE.matmul(psE[:, :], ONES_ROW, m_, start=True, stop=True,
                          inc=s_mm)
                VE.wait(s_mm)
                for dc in range(DC):
                    VE.tensor_mul(tmpB[:, 0:512], fm(src32, dc, tb), psD[:, :],
                                  inc=s_ve)
                    VE.tensor_sub(fm(dstbf, dc, tb), tmpB[:, 0:512], psE[:, :],
                                  inc=s_ve)
                TE.wait(s_ve)
            if absmax is not None:
                t = vslot(); acc = vslot()
                VE.memset(acc, 0.0, inc=s_ve)
                for dc in range(DC):
                    VE.tensor_reduce(t, fm(dstbf, dc), AX, ALU.max,
                                     apply_absolute_value=True, inc=s_ve)
                    VE.tensor_tensor(acc, acc, t, ALU.max, inc=s_ve)
                pmax_to_scalar(acc, absmax)

        RNE_C = 12582912.0

        def quant_int8(dst_bf, src, inv_ap, w):
            VE.tensor_scalar(tmpA[:, 0:w], src, inv_ap, RNE_C, ALU.mult,
                             ALU.add, inc=s_ve)
            VE.tensor_scalar(tmpA[:, 0:w], tmpA[:, 0:w], -RNE_C, -128.0,
                             ALU.add, ALU.max, inc=s_ve)
            VE.tensor_scalar(dst_bf, tmpA[:, 0:w], 127.0, None, ALU.min,
                             inc=s_ve)

        def st_barrier():
            VE.wait(s_st)
            VE.tensor_copy(tmpA[0:1, 1087:1088], ONES_COL[0:1, :])
            return s_ve.v

        def emit_l0():
            xnb = bfc
            a_sb = s16[:, 0:2048]
            b_sb = s16[:, 2048:4096]
            hs_sb = s16[:, 4096:6144]
            pp_sb = b_sb
            bd_ap = [vslot() for _ in range(DC)]
            bb_ap = vecs[0:16, 30:31]
            bc_ap = vecs[0:16, 31:32]
            sa_ap = vecs[0:16, 32:33]
            hin_ap = vecs[0:16, 33:34]
            xnmax = vecs[0:1, 34:35]
            _vs[0] = 35
            hmax_acc = vslot()
            bo_ap = [vslot() for _ in range(DC)]
            invx_bc = vslot()
            sxc_bc = vslot()
            inv2_bc = vslot()
            sdc_bc = vslot()
            sd_ap = [vslot() for _ in range(DC)]
            sg_ap = vslot(); su_ap = vslot()
            combg = vslot(); combu = vslot()
            comb2 = [vslot() for _ in range(DC)]
            clipx = vslot1(); clipd = vslot1()
            gmax = vslot1(); sxr = vslot1(); sxinv = vslot1(); sx127 = vslot1()
            ghm = vslot1(); s2r = vslot1(); s2inv = vslot1(); s2127 = vslot1()
            hmax_sc = vslot1()
            hgath = vecs[0:16, 100:108]
            mgath = vecs[0:1, 116:124]
            m2gath = vecs[0:1, 124:132]
            sel_sb = vecs[0:16, 108:116]

            SY.dma_start(out=tmpA[0:16, 0:1024], in_=at_d, inc=s_ld, n=16)
            SY.dma_start(out=sel_sb, in_=sel_d, inc=s_ld, n=16)
            SY.dma_start(out=clipx, in_=clipx_d, inc=s_ld, n=16)
            SY.dma_start(out=clipd, in_=clipd_d, inc=s_ld, n=16)
            for dc in range(DC):
                SY.dma_start(out=bd_ap[dc], in_=bd_d[dc * 128:(dc + 1) * 128, :],
                             inc=s_ld, n=16)
                SY.dma_start(out=bo_ap[dc], in_=bo_d[dc * 128:(dc + 1) * 128, :],
                             inc=s_ld, n=16)
            SY.dma_start(out=bb_ap, in_=bb_d, inc=s_ld, n=16)
            SY.dma_start(out=bc_ap, in_=bc_d, inc=s_ld, n=16)
            VE.wait(s_ld)
            VE.reduce_sum(sa_ap, tmpA[0:16, 0:1024], AX, inc=s_ve)
            SC.wait(s_ve)
            SC.activation(sa_ap, sa_ap, AF.Sigmoid, scale=1.0 / DM, inc=s_sc)
            SC.activation(clipx, clipx, AF.Sigmoid, inc=s_sc)
            SC.activation(clipd, clipd, AF.Sigmoid, inc=s_sc)

            TE.wait(s_ld)
            VE.wait(s_ld)
            GP.wait(s_ld)
            emit_layernorm(x32, xnb, absmax=xnmax)

            if dbg:
                tap(fm(xnb, 0, 0))
            for dc in range(DC):
                GP.dma_start(out=bfa[:, dc * 1024:(dc + 1) * 1024],
                             in_=wd_d[dc * 128:(dc + 1) * 128, :],
                             inc=s_gp, n=16)
                GP.dma_start(out=bfb[:, 1024 + dc * 16:1024 + (dc + 1) * 16],
                             in_=wb_d[dc * 128:(dc + 1) * 128, :],
                             inc=s_gp, n=16)
                GP.dma_start(out=bfb[:, 1152 + dc * 16:1152 + (dc + 1) * 16],
                             in_=wc_d[dc * 128:(dc + 1) * 128, :],
                             inc=s_gp, n=16)
            wb_sb = bfb[:, 1024:1152]
            wc_sb = bfb[:, 1152:1280]
            wload_mark = s_gp.v
            ld_mark = s_ld.v
            delta0 = bfb[:, 0:512]
            delta1 = bfb[:, 512:1024]

            TE.wait(s_gp, wload_mark)
            TE.wait(s_ve)
            SC.wait(s_ld, ld_mark)
            for tb in range(NTB):
                tsl = slice(tb * 512, (tb + 1) * 512)
                for ddc in range(DC):
                    pss = psB if ddc % 2 == 0 else psC
                    ddst = delta0 if ddc % 2 == 0 else delta1
                    for dc in range(DC):
                        TE.matmul(pss[:, :],
                                  bfa[:, dc * 1024 + ddc * 128:
                                      dc * 1024 + (ddc + 1) * 128],
                                  fm(xnb, dc, tb), start=(dc == 0),
                                  stop=(dc == DC - 1), inc=s_mm)
                    SC.wait(s_mm)
                    SC.activation(tmpC[:, 0:512], pss[:, :], AF.Exp,
                                  bias=bd_ap[ddc], inc=s_sc)
                    SC.activation(ddst, tmpC[:, 0:512], AF.Ln, bias=1.0,
                                  inc=s_sc)
                    TE.wait(s_sc)
                    TE.matmul(psA[:, tsl], OD_BF, ddst,
                              start=(ddc == 0), stop=(ddc == DC - 1), inc=s_mm)
                for dc in range(DC):
                    TE.matmul(psD[0:16, :], wb_sb[:, dc * 16:(dc + 1) * 16],
                              fm(xnb, dc, tb), start=(dc == 0),
                              stop=(dc == DC - 1), inc=s_mm)
                for dc in range(DC):
                    TE.matmul(psE[0:16, :], wc_sb[:, dc * 16:(dc + 1) * 16],
                              fm(xnb, dc, tb), start=(dc == 0),
                              stop=(dc == DC - 1), inc=s_mm)
                VE.wait(s_mm)
                VE.wait(s_ld, ld_mark)
                VE.tensor_scalar(a_sb[:, tsl], psA[0:16, tsl], -1.0, 1.0,
                                 ALU.mult, ALU.add, inc=s_ve)
                VE.tensor_scalar(tmpA[0:16, 0:512], psD[0:16, :], bb_ap, None,
                                 ALU.add, inc=s_ve)
                VE.tensor_mul(b_sb[:, tsl], tmpA[0:16, 0:512],
                              psA[0:16, tsl], inc=s_ve)
                VE.tensor_scalar(b_sb[:, tsl], b_sb[:, tsl], sa_ap, None,
                                 ALU.mult, inc=s_ve)
                VE.tensor_scalar(ctb[:, tsl], psE[0:16, :], bc_ap, None,
                                 ALU.add, inc=s_ve)
                TE.wait(s_ve)

            if dbg:
                tap(a_sb[:, 0:512], rows=16)
                tap(b_sb[:, 0:512], rows=16)
                tap(ctb[:, 0:512], rows=16)
            VE.tensor_tensor_scan(hs_sb, a_sb, b_sb, 0.0, ALU.mult, ALU.add,
                                  inc=s_ve)
            VE.tensor_tensor_scan(pp_sb, a_sb, a_sb, 1.0, ALU.mult, ALU.bypass,
                                  inc=s_ve)
            if dbg:
                tap(hs_sb[:, 0:512], rows=16)
                tap(pp_sb[:, 0:512], rows=16)
            zcol = vslot()
            VE.memset(zcol, 0.0, inc=s_ve)
            GP.wait(s_ve)
            GP.dma_start(out=cc_pub1.ap(), in_=zcol[0:64, :], inc=s_gp, n=16)
            GP.dma_start(out=cc_pub2.ap(), in_=zcol[0:8, :], inc=s_gp, n=16)
            GP.wait(s_gp)
            GP.dma_start(out=cc_pub1.ap()[0:16, :], in_=hs_sb[:, T - 1:T],
                         inc=s_gp, n=16)
            GP.dma_start(out=cc_pub1.ap()[32:33, :], in_=xnmax, inc=s_gp, n=16)
            GP.wait(s_gp)
            GP.collective_compute("AllGather", ALU.bypass, replica_groups=RG,
                                  ins=[cc_pub1.ap()], outs=[cc_ag1.ap()],
                                  inc=s_cc)
            GP.wait(s_cc)
            agv = cc_ag1.ap().rearrange("(r k) o -> r (k o)", r=NCORES)
            GP.dma_start(out=tmpB[0:8, 0:64], in_=agv, inc=s_gp, n=16)
            TE.wait(s_gp)
            TE.transpose(psB[0:64, 0:8], tmpB[0:8, 0:64], ID_F[0:8, 0:8],
                         inc=s_mm)
            VE.wait(s_mm)
            VE.tensor_copy(hgath, psB[0:16, 0:8], inc=s_ve)
            VE.tensor_copy(mgath, psB[32:33, 0:8], inc=s_ve)
            VE.tensor_mul(hgath, hgath, sel_sb, inc=s_ve)
            VE.reduce_sum(hin_ap, hgath, AX, inc=s_ve)
            VE.scalar_tensor_tensor(pp_sb, pp_sb, hin_ap, hs_sb, ALU.mult,
                                    ALU.add, inc=s_ve)
            VE.tensor_mul(a_sb, pp_sb, ctb[:, :], inc=s_ve)
            if dbg:
                tap(b_sb[:, 0:512], rows=16)
                tap(a_sb[:, 0:512], rows=16)
            GP.wait(s_mm)
            for dc in range(DC):
                GP.dma_start(out=bfa[:, dc * 1024:(dc + 1) * 1024],
                             in_=wo_d[dc * 128:(dc + 1) * 128, :],
                             inc=s_gp, n=16)
            wo_mark = s_gp.v
            def wo_sb(dc, do):
                return bfa[:, dc * 1024 + do * 128: dc * 1024 + (do + 1) * 128]
            def yt(dc):
                return bfa[:, 8192 + dc * 512: 8192 + (dc + 1) * 512]
            TE.wait(s_gp, wo_mark)
            srow = tmpA[0:1, 1024:1088]
            for tb in range(NTB):
                tsl = slice(tb * 512, (tb + 1) * 512)
                TE.wait(s_ve)
                TE.matmul(psB[0:1, :], cons[0:16, 0:1], a_sb[:, tsl],
                          start=True, stop=True, inc=s_mm)
                VE.wait(s_mm)
                VE.tensor_copy(tmpB[0:1, 0:512], psB[0:1, :], inc=s_ve)
                TE.wait(s_ve)
                TE.matmul(psA[:, 0:512], ONES_ROW, tmpB[0:1, 0:512],
                          start=True, stop=True, inc=s_mm)
                VE.wait(s_mm)
                if tb > 0:
                    VE.wait(s_mm)
                for dc in range(DC):
                    VE.tensor_mul(yt(dc), fm(xnb, dc, tb), psA[:, 0:512],
                                  inc=s_ve)
                if dbg and tb == 0:
                    tap(yt(0))
                for do in range(DC):
                    pss = psB if do % 2 == 0 else psC
                    TE.wait(s_ve)
                    for dc in range(DC):
                        TE.matmul(pss[:, :], wo_sb(dc, do), yt(dc),
                                  start=(dc == 0), stop=(dc == DC - 1),
                                  inc=s_mm)
                    VE.wait(s_mm)
                    resadd(pss[:, :], do, tb, bias_ap=bo_ap[do])

            if dbg:
                tap(fm(x32, 0, 0))
            VE.wait(s_gp)
            VE.reduce_max(gmax, mgath, AX, inc=s_ve)
            VE.wait(s_sc)
            VE.tensor_mul(ghm, clipx, gmax, inc=s_ve)
            VE.tensor_scalar(sxr, ghm, 1.0 / 127.0, None, ALU.mult, inc=s_ve)
            VE.reciprocal(sxinv, sxr, inc=s_ve)
            VE.tensor_scalar(sx127, sxr, 1.0 / 127.0, None, ALU.mult, inc=s_ve)
            bcast_col(sxinv, invx_bc)
            bcast_col(sx127, sxc_bc)
            VE.wait(s_mm)
            def xq(dc, tb=None):
                if tb is None:
                    return bfc[:, dc * T:(dc + 1) * T]
                return bfc[:, dc * T + tb * 512: dc * T + (tb + 1) * 512]
            for dc in range(DC):
                for tb in range(NTB):
                    quant_int8(xq(dc, tb), fm(xnb, dc, tb), invx_bc, 512)

            if dbg:
                tap(xq(0, 0))
            wtile = fpw[:, 0:2816]
            ttag = bfb[:, 2816:3072]
            for do in range(DC):
                SY.wait(s_ve)
                SY.dma_start(out=wtile, in_=wdn_d[do * 128:(do + 1) * 128, :],
                             inc=s_ld, n=16)
                VE.wait(s_ld)
                if do > 0:
                    VE.wait(s_mm)
                VE.tensor_reduce(sd_ap[do], wtile, AX, ALU.max,
                                 apply_absolute_value=True, inc=s_ve)
                VE.reciprocal(tmpA[:, 1024:1025], sd_ap[do], inc=s_ve)
                VE.tensor_scalar(tmpA[:, 1025:1026], tmpA[:, 1024:1025], 127.0,
                                 None, ALU.mult, inc=s_ve)
                for hh in range(4):
                    cs = slice(hh * 704, (hh + 1) * 704)
                    quant_int8(wtile[:, cs], wtile[:, cs], tmpA[:, 1025:1026],
                               704)
                TE.wait(s_ve)
                for ic in range(22):
                    pss = psB if ic % 2 == 0 else psC
                    stg = ttag[:, (ic % 2) * 128:(ic % 2) * 128 + 128]
                    if ic >= 2:
                        TE.wait(s_st)
                    TE.transpose(pss[:, 0:128],
                                 wtile[:, ic * 128:(ic + 1) * 128], ID_F,
                                 inc=s_mm)
                    VE.wait(s_mm)
                    if ic >= 2 or do > 0:
                        VE.wait(s_st)
                    VE.tensor_copy(stg, pss[:, 0:128], inc=s_ve)
                    SY.wait(s_ve)
                    SY.dma_start(
                        out=wdqT_hbm.ap()[ic * 128:(ic + 1) * 128,
                                          do * 128:(do + 1) * 128],
                        in_=stg, inc=s_st, n=16)

            VE.memset(hmax_acc, 0.0, inc=s_ve)
            wgt = fpw[:, 0:1024]
            wut = fpw[:, 1024:2048]
            upt = fpw[:, 2048:2560]
            wqgT = bfb[:, 0:1024]
            wquT = bfb[:, 1024:2048]
            silu_t = bfb[:, 2048:2560]
            h_t = bfb[:, 2560:3072]
            for ic in range(22):
                SY.wait(s_ve)
                SY.dma_start(out=wgt, in_=wg_d[ic * 128:(ic + 1) * 128, :],
                             inc=s_ld, n=16)
                SY.dma_start(out=wut, in_=wu_d[ic * 128:(ic + 1) * 128, :],
                             inc=s_ld, n=16)
                VE.wait(s_ld)
                if ic > 0:
                    VE.wait(s_mm)
                VE.tensor_reduce(sg_ap, wgt, AX, ALU.max,
                                 apply_absolute_value=True, inc=s_ve)
                VE.tensor_reduce(su_ap, wut, AX, ALU.max,
                                 apply_absolute_value=True, inc=s_ve)
                VE.reciprocal(tmpA[:, 1024:1025], sg_ap, inc=s_ve)
                VE.tensor_scalar(tmpA[:, 1025:1026], tmpA[:, 1024:1025], 127.0,
                                 None, ALU.mult, inc=s_ve)
                VE.reciprocal(tmpA[:, 1026:1027], su_ap, inc=s_ve)
                VE.tensor_scalar(tmpA[:, 1027:1028], tmpA[:, 1026:1027], 127.0,
                                 None, ALU.mult, inc=s_ve)
                quant_int8(wgt, wgt, tmpA[:, 1025:1026], 1024)
                quant_int8(wut, wut, tmpA[:, 1027:1028], 1024)
                VE.tensor_mul(combg, sg_ap, sxc_bc, inc=s_ve)
                VE.tensor_mul(combu, su_ap, sxc_bc, inc=s_ve)
                TE.wait(s_ve)
                for j in range(DC):
                    for (src_, dst_) in ((wgt, wqgT), (wut, wquT)):
                        TE.transpose(psB[:, 0:128],
                                     src_[:, j * 128:(j + 1) * 128], ID_F,
                                     inc=s_mm)
                        VE.wait(s_mm)
                        VE.tensor_copy(dst_[:, j * 128:(j + 1) * 128],
                                       psB[:, 0:128], inc=s_ve)
                        TE.wait(s_ve)
                for tb in range(NTB):
                    for dc in range(DC):
                        TE.matmul(psD[:, :], wqgT[:, dc * 128:(dc + 1) * 128],
                                  xq(dc, tb), start=(dc == 0),
                                  stop=(dc == DC - 1), inc=s_mm)
                    for dc in range(DC):
                        TE.matmul(psE[:, :], wquT[:, dc * 128:(dc + 1) * 128],
                                  xq(dc, tb), start=(dc == 0),
                                  stop=(dc == DC - 1), inc=s_mm)
                    SC.wait(s_mm)
                    SC.wait(s_ve)
                    SC.activation(silu_t, psD[:, :], AF.Silu, scale=combg,
                                  inc=s_sc)
                    VE.wait(s_mm)
                    VE.wait(s_st)
                    VE.tensor_scalar(upt, psE[:, :], combu, None, ALU.mult,
                                     inc=s_ve)
                    VE.wait(s_sc)
                    VE.tensor_mul(h_t, silu_t, upt, inc=s_ve)
                    VE.tensor_reduce(tmpA[:, 1028:1029], h_t, AX, ALU.max,
                                     apply_absolute_value=True, inc=s_ve)
                    VE.tensor_tensor(hmax_acc, hmax_acc, tmpA[:, 1028:1029],
                                     ALU.max, inc=s_ve)
                    SY.wait(s_ve)
                    SY.dma_start(out=h_hbm.ap()[ic * 128:(ic + 1) * 128,
                                                tb * 512:(tb + 1) * 512],
                                 in_=h_t, inc=s_st, n=16)
                    TE.wait(s_ve)

            pmax_to_scalar(hmax_acc, hmax_sc)
            GP.wait(s_ve)
            GP.dma_start(out=cc_pub2.ap()[0:1, :], in_=hmax_sc, inc=s_gp, n=16)
            GP.wait(s_gp)
            GP.collective_compute("AllGather", ALU.bypass, replica_groups=RG,
                                  ins=[cc_pub2.ap()], outs=[cc_ag2.ap()],
                                  inc=s_cc)
            GP.wait(s_cc)
            GP.dma_start(out=tmpB[0:8, 0:8],
                         in_=cc_ag2.ap().rearrange("(r k) o -> r (k o)",
                                                   r=NCORES), inc=s_gp, n=16)
            TE.wait(s_gp)
            TE.transpose(psB[0:8, 0:8], tmpB[0:8, 0:8], ID_F[0:8, 0:8],
                         inc=s_mm)
            VE.wait(s_mm)
            VE.tensor_copy(m2gath, psB[0:1, 0:8], inc=s_ve)
            VE.reduce_max(ghm, m2gath, AX, inc=s_ve)
            VE.tensor_mul(gmax, clipd, ghm, inc=s_ve)
            VE.tensor_scalar(s2r, gmax, 1.0 / 127.0, None, ALU.mult, inc=s_ve)
            VE.reciprocal(s2inv, s2r, inc=s_ve)
            VE.tensor_scalar(s2127, s2r, 1.0 / 127.0, None, ALU.mult, inc=s_ve)
            bcast_col(s2inv, inv2_bc)
            bcast_col(s2127, sdc_bc)
            for do in range(DC):
                VE.tensor_mul(comb2[do], sd_ap[do], sdc_bc, inc=s_ve)

            wdt = [bfb[:, k * 1024:(k + 1) * 1024] for k in range(2)]
            hti = [bfb[:, 2048 + k * 512: 2048 + (k + 1) * 512]
                   for k in range(2)]
            hqi = [bfb[:, 3072 + k * 512: 3072 + (k + 1) * 512]
                   for k in range(2)]
            pslist = [psA[:, 0:512], psA[:, 512:1024], psA[:, 1024:1536],
                      psA[:, 1536:2048], psB[:, :], psC[:, :], psD[:, :],
                      psE[:, :]]
            SY.wait(s_st)
            def dn_load(tb, ic):
                k = ic % 2
                SY.dma_start(out=wdt[k],
                             in_=wdqT_hbm.ap()[ic * 128:(ic + 1) * 128, :],
                             inc=s_ld, n=16)
                SY.dma_start(out=hti[k],
                             in_=h_hbm.ap()[ic * 128:(ic + 1) * 128,
                                            tb * 512:(tb + 1) * 512],
                             inc=s_ld, n=16)
            for tb in range(NTB):
                for ic in range(22):
                    k = ic % 2
                    if ic == 0:
                        if tb > 0:
                            SY.wait(s_ve)
                            SY.wait(s_mm)
                        dn_load(tb, 0)
                    if ic < 21:
                        if ic >= 1 or tb > 0:
                            SY.wait(s_ve)
                            SY.wait(s_mm)
                        dn_load(tb, ic + 1)
                    VE.wait(s_ld)
                    if ic >= 2 or tb > 0:
                        VE.wait(s_mm)
                    quant_int8(hqi[k], hti[k], inv2_bc, 512)
                    TE.wait(s_ve)
                    for do in range(DC):
                        TE.matmul(pslist[do],
                                  wdt[k][:, do * 128:(do + 1) * 128],
                                  hqi[k], start=(ic == 0), stop=(ic == 21),
                                  inc=s_mm)
                VE.wait(s_mm)
                for do in range(DC):
                    resadd(pslist[do], do, tb, scale_ap=comb2[do])
                TE.wait(s_ve)


        def quant_fp(dst, srcA, man, min_exp, w, sA, sB, sC):
            RC = 12582912.0
            SC.wait(s_ve)
            SC.activation(sA, srcA, AF.Abs, inc=s_sc)
            SC.activation(sB, sA, AF.Ln, bias=qcn[:, 7:8], inc=s_sc)
            VE.wait(s_sc)
            VE.tensor_scalar(sA, sB, 1.0 / LN2, -0.5, ALU.mult, ALU.add,
                             inc=s_ve)
            VE.tensor_scalar(sA, sA, RC, -RC, ALU.add, ALU.add, inc=s_ve)
            VE.tensor_scalar(sA, sA, float(min_exp), None, ALU.max, inc=s_ve)
            SC.wait(s_ve)
            mi = 2 if man == 3.0 else 4
            SC.activation(sB, sA, AF.Exp, scale=qcn[:, 1:2],
                          bias=qcn[:, mi:mi + 1], inc=s_sc)
            SC.activation(sC, sA, AF.Exp, scale=qcn[:, 0:1],
                          bias=qcn[:, mi + 1:mi + 2], inc=s_sc)
            VE.wait(s_sc)
            VE.tensor_mul(sA, srcA, sB, inc=s_ve)
            VE.tensor_scalar(sA, sA, RC, -RC, ALU.add, ALU.add, inc=s_ve)
            VE.tensor_mul(dst, sA, sC, inc=s_ve)

        def emit_l1():
            _vs[0] = 0
            bq_ap = vslot(); bk_ap = vslot()
            negc = [vslot(), vslot()]
            b1_sb = vecs[:, 40:72]
            b2_sb = vecs[:, 72:80]
            b2q_sb = vecs[:, 80:88]
            bo_sb = vecs[:, 88:96]
            qm = [vslot1(), vslot1()]
            km = [vslot1(), vslot1()]
            t11a = vslot1(); t11b = vslot1()
            zc = vslot()
            wq_sb = bfb[:, 0:1024]
            wk_sb = bfb[:, 1024:2048]
            wv_sb = bfb[:, 2048:3072]
            bvb_sb = bfb[:, 3072:3200]
            w2qt = bfb[:, 3200:4224]
            wo_sb = bfb[0:64, 3200:5248]
            bvrow = tmpB[0:1, 0:128]
            VE.memset(zc, 0.0, inc=s_ve)

            emit_layernorm(x32, bfc)
            ln1_ve = s_ve.v

            SY.wait(s_ve, ln1_ve)
            for dc in range(DC):
                SY.dma_start(out=xn_pub.ap()[dc * 128:(dc + 1) * 128, :],
                             in_=fm(bfc, dc), inc=s_st, n=16)
            pub_ve = st_barrier()
            GP.wait(s_ve, pub_ve)
            GP.collective_compute("AllGather", ALU.bypass, replica_groups=RG,
                                  ins=[xn_pub.ap()], outs=[xn_ag.ap()],
                                  inc=s_cc)
            ag_mark = s_cc.v

            GP.dma_start(out=wq_sb.rearrange("q (dc j) -> q dc j", dc=8),
                         in_=l1["wq"].rearrange("(dc p) j -> p dc j", p=128),
                         inc=s_gp, n=16)
            GP.dma_start(out=wk_sb.rearrange("q (dc j) -> q dc j", dc=8),
                         in_=l1["wk"].rearrange("(dc p) j -> p dc j", p=128),
                         inc=s_gp, n=16)
            GP.dma_start(out=wv_sb.rearrange("q (dc j) -> q dc j", dc=8),
                         in_=l1["wv"].rearrange("(dc p) j -> p dc j", p=128),
                         inc=s_gp, n=16)
            l1w_mark = s_gp.v
            SY.dma_start(out=bq_ap, in_=l1["bq"], inc=s_ld, n=16)
            SY.dma_start(out=bk_ap, in_=l1["bk"], inc=s_ld, n=16)
            SY.dma_start(out=bvrow, in_=l1["bv"], inc=s_ld, n=16)
            SY.dma_start(out=b1_sb, in_=l1["b1"], inc=s_ld, n=16)
            SY.dma_start(out=b2_sb, in_=l1["b2"], inc=s_ld, n=16)
            SY.dma_start(out=bo_sb, in_=l1["bo"], inc=s_ld, n=16)
            l1b_mark = s_ld.v
            TE.wait(s_ld, l1b_mark)
            TE.wait(s_ve)
            TE.matmul(psD[:, 0:128], ONES_ROW, bvrow, start=True, stop=True,
                      inc=s_mm)
            VE.wait(s_mm)
            VE.tensor_copy(bvb_sb, psD[:, 0:128], inc=s_ve)
            VE.wait(s_ld, l1b_mark)
            SC.wait(s_ld, l1b_mark)
            quant_fp(b2q_sb, b2_sb, 10.0, -14.0, 8, tmpA[:, 0:8],
                     tmpA[:, 16:24], tmpA[:, 32:40])

            for hc in range(32):
                SY.wait(s_ve)
                SY.wait(s_sc)
                SY.dma_start(out=fpw[:, 0:1024],
                             in_=l1["w2T"][hc * 128:(hc + 1) * 128, :],
                             inc=s_ld, n=16)
                VE.wait(s_ld)
                SC.wait(s_ld)
                if hc > 0:
                    VE.wait(s_st)
                quant_fp(w2qt, fpw[:, 0:1024], 3.0, -6.0, 1024,
                         fpw[:, 1024:2048], fpw[:, 2048:3072],
                         fpw[:, 3072:4096])
                SY.wait(s_ve)
                SY.dma_start(out=w2q_hbm.ap()[hc * 128:(hc + 1) * 128, :],
                             in_=w2qt, inc=s_st, n=16)

            GP.wait(s_ve, st_barrier())
            GP.dma_start(out=wo_sb, in_=l1["wo"], inc=s_gp, n=16)
            wo_mark1 = s_gp.v

            w1blk = [bfa[:, 8192 + k * 1024: 8192 + (k + 1) * 1024]
                     for k in range(2)]
            hqt = bfa[:, 0:512]
            def w1_load(tb, hc):
                k = hc % 2
                GP.dma_start(
                    out=w1blk[k].rearrange("q (dc j) -> q dc j", dc=8),
                    in_=l1["w1T"][:, hc * 128:(hc + 1) * 128].rearrange(
                        "(dc p) j -> p dc j", p=128), inc=s_gp, n=16)
            for tb in range(NTB):
                for hc in range(32):
                    k = hc % 2
                    if hc == 0:
                        if tb > 0:
                            GP.wait(s_mm)
                        w1_load(tb, 0)
                    if hc < 31:
                        if hc >= 1 or tb > 0:
                            GP.wait(s_mm)
                        w1_load(tb, hc + 1)
                    pss = psB if hc % 2 == 0 else psC
                    TE.wait(s_gp)
                    TE.wait(s_sc)
                    TE.wait(s_ve)
                    for dc in range(DC):
                        TE.matmul(pss[:, :],
                                  w1blk[k][:, dc * 128:(dc + 1) * 128],
                                  fm(bfc, dc, tb), start=(dc == 0),
                                  stop=(dc == DC - 1), inc=s_mm)
                    SC.wait(s_mm)
                    SC.activation(fpw[:, 0:512], pss[:, :], AF.Relu,
                                  bias=b1_sb[:, hc:hc + 1], inc=s_sc)
                    VE.wait(s_sc)
                    VE.wait(s_st)
                    quant_fp(hqt, fpw[:, 0:512], 3.0, -6.0, 512,
                             fpw[:, 512:1024], fpw[:, 1024:1536],
                             fpw[:, 1536:2048])
                    SY.wait(s_ve)
                    SY.dma_start(out=hq_hbm.ap()[hc * 128:(hc + 1) * 128,
                                                  tb * 512:(tb + 1) * 512],
                                 in_=hqt, inc=s_st, n=16)

            wdt1 = [bfa[:, k * 1024:(k + 1) * 1024] for k in range(2)]
            hti1 = [bfa[:, 2048 + k * 512: 2048 + (k + 1) * 512]
                    for k in range(2)]
            pslist1 = [psA[:, 0:512], psA[:, 512:1024], psA[:, 1024:1536],
                       psA[:, 1536:2048], psB[:, :], psC[:, :], psD[:, :],
                       psE[:, :]]
            def l1p2_load(tb, hc):
                k = hc % 2
                SY.dma_start(out=wdt1[k],
                             in_=w2q_hbm.ap()[hc * 128:(hc + 1) * 128, :],
                             inc=s_ld, n=16)
                SY.dma_start(out=hti1[k],
                             in_=hq_hbm.ap()[hc * 128:(hc + 1) * 128,
                                             tb * 512:(tb + 1) * 512],
                             inc=s_ld, n=16)
            SY.wait(s_st)
            SY.wait(s_mm)
            for tb in range(NTB):
                for hc in range(32):
                    k = hc % 2
                    if hc == 0:
                        if tb > 0:
                            SY.wait(s_mm)
                        l1p2_load(tb, 0)
                    if hc < 31:
                        if hc >= 1 or tb > 0:
                            SY.wait(s_mm)
                        l1p2_load(tb, hc + 1)
                    TE.wait(s_ld)
                    for do in range(DC):
                        TE.matmul(pslist1[do],
                                  wdt1[k][:, do * 128:(do + 1) * 128],
                                  hti1[k], start=(hc == 0), stop=(hc == 31),
                                  inc=s_mm)
                VE.wait(s_mm)
                for do in range(DC):
                    VE.tensor_scalar(tmpA[:, 0:512], pslist1[do],
                                     b2q_sb[:, do:do + 1], None, ALU.add,
                                     inc=s_ve)
                    quant_fp(tmpB[:, 0:512], tmpA[:, 0:512], 10.0, -14.0, 512,
                             tmpA[:, 512:1024], fpw[:, 0:512],
                             fpw[:, 512:1024])
                    VE.tensor_add(fm(x32, do, tb), fm(x32, do, tb),
                                  tmpB[:, 0:512], inc=s_ve)
                TE.wait(s_ve)

            slab = lambda dc: bfc[:, dc * 512:(dc + 1) * 512]
            E_t = [bfc[:, 13312 + k * 512: 13312 + (k + 1) * 512]
                   for k in range(4)]
            k_sb = bfc[:, 5120:9216]
            q_sb = bfa[:, 0:4096]
            vaug = bfa[:, 4096:8256]
            po_t = bfa[:, 8256:8768]
            def vaug_ck(h, ck):
                o = 4096 + h * 2080 + ck * 65
                return bfa[:, o:o + 65]
            TE.wait(s_gp)
            for h in range(2):
                for ck in range(32):
                    VE.tensor_copy(vaug_ck(h, ck)[:, 64:65], ONES_COL,
                                   inc=s_ve)
            for seq in range(4):
                for tseg in range(8):
                    slot = 2 * seq + tseg // 4
                    col = (tseg % 4) * 512
                    SY.wait(s_cc, ag_mark)
                    SY.wait(s_mm)
                    for dc in range(DC):
                        SY.dma_start(
                            out=slab(dc),
                            in_=xn_ag.ap()[slot * DM + dc * 128:
                                           slot * DM + (dc + 1) * 128,
                                           col:col + 512], inc=s_ld, n=16)
                    TE.wait(s_ld)
                    TE.wait(s_ve)
                    for dc in range(DC):
                        TE.matmul(psD[:, :], wq_sb[:, dc * 128:(dc + 1) * 128],
                                  slab(dc), start=(dc == 0),
                                  stop=(dc == DC - 1), inc=s_mm)
                    for dc in range(DC):
                        TE.matmul(psE[:, :], wk_sb[:, dc * 128:(dc + 1) * 128],
                                  slab(dc), start=(dc == 0),
                                  stop=(dc == DC - 1), inc=s_mm)
                    for tk in range(4):
                        pss = psB if tk % 2 == 0 else psC
                        for dc in range(DC):
                            TE.matmul(pss[:, 0:128],
                                      slab(dc)[:, tk * 128:(tk + 1) * 128],
                                      wv_sb[:, dc * 128:(dc + 1) * 128],
                                      start=(dc == 0), stop=(dc == DC - 1),
                                      inc=s_mm)
                        ck = tseg * 4 + tk
                        VE.wait(s_mm)
                        for h in range(2):
                            VE.tensor_tensor(
                                vaug_ck(h, ck)[:, 0:64],
                                pss[:, h * 64:h * 64 + 64],
                                bvb_sb[:, h * 64:h * 64 + 64],
                                ALU.add, inc=s_ve)
                        if tk >= 1:
                            TE.wait(s_ve)
                    VE.wait(s_mm)
                    VE.tensor_scalar(q_sb[:, tseg * 512:(tseg + 1) * 512],
                                     psD[:, :], bq_ap, None, ALU.add,
                                     inc=s_ve)
                    VE.tensor_scalar(k_sb[:, tseg * 512:(tseg + 1) * 512],
                                     psE[:, :], bk_ap, None, ALU.add,
                                     inc=s_ve)
                for h in range(2):
                    VE.tensor_copy(qm[h], zc[0:1, :], inc=s_ve)
                    VE.tensor_copy(km[h], zc[0:1, :], inc=s_ve)
                for part in range(8):
                    SC.wait(s_ve)
                    SC.activation(tmpC[:, 0:512],
                                  q_sb[:, part * 512:(part + 1) * 512],
                                  AF.Square, inc=s_sc)
                    for h in range(2):
                        TE.wait(s_sc)
                        TE.wait(s_ve)
                        TE.matmul(psD[0:1, 0:512], half[:, h:h + 1],
                                  tmpC[:, 0:512], start=True, stop=True,
                                  inc=s_mm)
                        VE.wait(s_mm)
                        VE.reduce_max(t11a, psD[0:1, 0:512], AX, inc=s_ve)
                        VE.tensor_tensor(qm[h], qm[h], t11a, ALU.max,
                                         inc=s_ve)
                    SC.wait(s_ve)
                    SC.activation(tmpC[:, 0:512],
                                  k_sb[:, part * 512:(part + 1) * 512],
                                  AF.Square, inc=s_sc)
                    for h in range(2):
                        TE.wait(s_sc)
                        TE.wait(s_ve)
                        TE.matmul(psD[0:1, 0:512], half[:, h:h + 1],
                                  tmpC[:, 0:512], start=True, stop=True,
                                  inc=s_mm)
                        VE.wait(s_mm)
                        VE.reduce_max(t11a, psD[0:1, 0:512], AX, inc=s_ve)
                        VE.tensor_tensor(km[h], km[h], t11a, ALU.max,
                                         inc=s_ve)
                for h in range(2):
                    VE.tensor_mul(t11a, qm[h], km[h], inc=s_ve)
                    SC.wait(s_ve)
                    SC.activation(t11b, t11a, AF.Sqrt, inc=s_sc)
                    VE.wait(s_sc)
                    VE.tensor_scalar(t11a, t11b, -0.125, None, ALU.mult,
                                     inc=s_ve)
                    bcast_col(t11a, negc[h])

                for qb in range(8):
                    for h in range(2):
                        nkb = 4 * qb + 4
                        av_marks = []
                        qk_marks = []
                        exp_marks = []
                        def qk1(kb):
                            if kb >= 4:
                                TE.wait(s_sc, exp_marks[kb - 4])
                            TE.matmul(psA[:, (kb % 4) * 512:
                                          (kb % 4) * 512 + 512],
                                      k_sb[h * 64:h * 64 + 64,
                                           kb * 128:(kb + 1) * 128],
                                      q_sb[h * 64:h * 64 + 64,
                                           qb * 512:(qb + 1) * 512],
                                      start=True, stop=True, inc=s_mm)
                            qk_marks.append(s_mm.v)
                        for kb in range(nkb):
                            if kb == 0:
                                qk1(0)
                            if kb < nkb - 1:
                                qk1(kb + 1)
                            SC.wait(s_mm, qk_marks[kb])
                            if kb >= 4:
                                SC.wait(s_mm, av_marks[kb - 4])
                            SC.activation(E_t[kb % 4],
                                          psA[:, (kb % 4) * 512:
                                              (kb % 4) * 512 + 512],
                                          AF.Exp, scale=qcn[:, 6:7],
                                          bias=negc[h], inc=s_sc)
                            exp_marks.append(s_sc.v)
                            m = kb - 4 * qb
                            if m >= 0:
                                VE.wait(s_sc, exp_marks[kb])
                                VE.tensor_mul(E_t[kb % 4], E_t[kb % 4],
                                              mskb[:, m * 512:(m + 1) * 512],
                                              inc=s_ve)
                                TE.wait(s_ve)
                            else:
                                TE.wait(s_sc, exp_marks[kb])
                            TE.matmul(psB[0:65, :], vaug_ck(h, kb),
                                      E_t[kb % 4], start=(kb == 0),
                                      stop=(kb == nkb - 1), inc=s_mm)
                            av_marks.append(s_mm.v)
                        VE.wait(s_mm)
                        VE.reciprocal(tmpB[0:1, 0:512], psB[64:65, :],
                                      inc=s_ve)
                        TE.wait(s_ve)
                        TE.matmul(psC[0:64, :], ONES_ROW[0:1, 0:64],
                                  tmpB[0:1, 0:512], start=True, stop=True,
                                  inc=s_mm)
                        VE.wait(s_mm)
                        VE.tensor_copy(tmpA[0:64, 0:512], psC[0:64, :],
                                       inc=s_ve)
                        VE.tensor_mul(bfo[:, h * 512:(h + 1) * 512],
                                      psB[0:64, :], tmpA[0:64, 0:512],
                                      inc=s_ve)
                        TE.wait(s_ve)
                    rank = 2 * seq + qb // 4
                    col = (qb % 4) * 512
                    for do in range(DC):
                        pss = psD if do % 2 == 0 else psE
                        TE.wait(s_ve)
                        for h in range(2):
                            TE.matmul(pss[:, :],
                                      wo_sb[:, h * 1024 + do * 128:
                                            h * 1024 + (do + 1) * 128],
                                      bfo[:, h * 512:(h + 1) * 512],
                                      start=(h == 0), stop=(h == 1),
                                      inc=s_mm)
                        VE.wait(s_mm)
                        VE.wait(s_st)
                        VE.tensor_copy(po_t, pss[:, :], inc=s_ve)
                        SY.wait(s_ve)
                        SY.dma_start(
                            out=po_hbm.ap()[rank * DM + do * 128:
                                            rank * DM + (do + 1) * 128,
                                            col:col + 512],
                            in_=po_t, inc=s_st, n=16)
            GP.wait(s_ve, st_barrier())
            GP.collective_compute("ReduceScatter", ALU.add, replica_groups=RG,
                                  ins=[po_hbm.ap()], outs=[rs_out.ap()],
                                  inc=s_cc)
            rs_mark = s_cc.v
            SY.wait(s_cc, rs_mark)
            SY.wait(s_mm)
            for dc in range(DC):
                SY.dma_start(out=fm(bfc, dc),
                             in_=rs_out.ap()[dc * 128:(dc + 1) * 128, :],
                             inc=s_ld, n=16)
            VE.wait(s_ld)
            for dc in range(DC):
                for tb in range(NTB):
                    VE.tensor_scalar(tmpB[:, 0:512], fm(bfc, dc, tb),
                                     bo_sb[:, dc:dc + 1], None, ALU.add,
                                     inc=s_ve)
                    VE.tensor_add(fm(x32, dc, tb), fm(x32, dc, tb),
                                  tmpB[:, 0:512], inc=s_ve)


        def l2drain():
            SY.wait(s_gp); SY.wait(s_ld); SY.wait(s_st)
            SY.wait(s_cc, s_cc.v); SY.wait(s_ve); SY.wait(s_sc)
            SY.wait(s_mm)
            GP.wait(s_st); GP.wait(s_ve)

        def emit_l2():
            _vs[0] = 0
            LR = 0.001
            flc = vecs[:, 200:204]
            fl1 = vecs[:, 204:208]
            qm = [vecs[0:1, 226:227], vecs[0:1, 227:228]]
            km = [vecs[0:1, 228:229], vecs[0:1, 229:230]]
            t11a2 = vecs[0:1, 231:232]
            t11b2 = vecs[0:1, 232:233]
            negc2 = [vecs[:, 233:234], vecs[:, 234:235]]
            zc2 = vecs[:, 235:236]
            b1c = vecs[:, 208:210]
            b2c = vecs[:, 210:218]
            bproj_sb = vecs[:, 218:226]
            w1m = fpw[:, 0:2048]
            w2m = fpw[:, 2048:4096]
            b1row = s16[0:1, 0:256]
            b2row = s16[0:1, 256:1280]
            w1b = bfb[:, 0:2048]
            w2b = bfb[:, 2048:4096]
            b1bc = bfb[:, 4096:4352]
            b2bc = bfb[:, 4352:5376]
            hgel = lambda tc: bfa[:, tc * 256:(tc + 1) * 256]
            dpre = lambda tc: bfa[:, 4096 + tc * 256: 4096 + (tc + 1) * 256]
            w2dom = bfa[:, 8192:10240]
            p1 = bfa[:, 10240:12288]

            VE.memset(zc2, 0.0, inc=s_ve)
            emit_layernorm(x32, bfc)
            ln2_ve = s_ve.v
            SY.wait(s_ve, ln2_ve)
            for dc in range(DC):
                SY.dma_start(out=xn_pub.ap()[dc * 128:(dc + 1) * 128, :],
                             in_=fm(bfc, dc), inc=s_st, n=16)
            GP.wait(s_ve, st_barrier())
            GP.collective_compute("AllGather", ALU.bypass, replica_groups=RG,
                                  ins=[xn_pub.ap()], outs=[xn_ag.ap()],
                                  inc=s_cc)
            ag2_mark = s_cc.v

            SY.dma_start(out=w1m.rearrange("q (dc j) -> q dc j", dc=8),
                         in_=l2["w1T"].rearrange("(dc p) j -> p dc j", p=128),
                         inc=s_ld, n=16)
            SY.dma_start(out=w2m.rearrange("q (kc j) -> q kc j", kc=2),
                         in_=l2["w2T"].rearrange("(kc p) j -> p kc j", p=128),
                         inc=s_ld, n=16)
            SY.dma_start(out=b1row, in_=l2["b1"], inc=s_ld, n=16)
            SY.dma_start(out=b2row, in_=l2["b2"], inc=s_ld, n=16)
            SY.dma_start(out=flc, in_=l2["flc"], inc=s_ld, n=16)
            SY.dma_start(out=fl1, in_=l2["fl1"], inc=s_ld, n=16)
            SY.dma_start(out=bproj_sb, in_=l2["bproj"], inc=s_ld, n=16)
            l2ld_mark = s_ld.v
            VE.memset(tmpA[:, 0:1024], 0.0, inc=s_ve)
            SY.wait(s_ve)
            for r0 in range(0, 514, 128):
                rr = min(128, 514 - r0)
                SY.dma_start(out=g_pub.ap()[r0:r0 + rr, :],
                             in_=tmpA[0:rr, 0:1024], inc=s_st, n=16)

            pslist8 = [psA[:, 0:512], psA[:, 512:1024], psA[:, 1024:1536],
                       psA[:, 1536:2048], psB[:, :], psC[:, :], psD[:, :],
                       psE[:, :]]
            def wstream(dram, dm, k):
                GP.dma_start(
                    out=zstr[:, k * 1024:(k + 1) * 1024],
                    in_=dram[dm * 128:(dm + 1) * 128, :], inc=s_gp, n=16)
            for tb in range(NTB):
                for (wi, wd) in ((0, l2["tkT"]), (1, l2["tvT"]),
                                 (2, l2["tqT"])):
                    for dm in range(DC):
                        k = dm % 2
                        if dm == 0:
                            GP.wait(s_mm)
                            wstream(wd, 0, 0)
                        if dm < 7:
                            GP.wait(s_mm)
                            wstream(wd, dm + 1, (dm + 1) % 2)
                        TE.wait(s_gp)
                        TE.wait(s_ve)
                        if wi < 2:
                            for tc in range(4):
                                for g in range(2):
                                    TE.matmul(
                                        pslist8[tc * 2 + g],
                                        fm(bfc, dm, tb)[:, tc * 128:
                                                        (tc + 1) * 128],
                                        zstr[:, k * 1024 + g * 512:
                                             k * 1024 + (g + 1) * 512],
                                        start=(dm == 0), stop=(dm == 7),
                                        inc=s_mm)
                        else:
                            for dq in range(DC):
                                TE.matmul(
                                    pslist8[dq],
                                    zstr[:, k * 1024 + dq * 128:
                                         k * 1024 + (dq + 1) * 128],
                                    fm(bfc, dm, tb),
                                    start=(dm == 0), stop=(dm == 7),
                                    inc=s_mm)
                    VE.wait(s_mm)
                    VE.wait(s_st)
                    if wi == 0:
                        for i in range(8):
                            VE.tensor_copy(bfa[:, i * 512:(i + 1) * 512],
                                           pslist8[i], inc=s_ve)
                        SY.wait(s_ve)
                        for tc in range(4):
                            for g in range(2):
                                SY.dma_start(
                                    out=z_hbm.ap()[tb * 512 + tc * 128:
                                                   tb * 512 + (tc + 1) * 128,
                                                   g * 512:(g + 1) * 512],
                                    in_=bfa[:, (tc * 2 + g) * 512:
                                            (tc * 2 + g + 1) * 512],
                                    inc=s_st, n=16)
                    elif wi == 1:
                        for i in range(8):
                            VE.tensor_sub(bfa[:, 4096 + i * 512:
                                              4096 + (i + 1) * 512],
                                          bfa[:, i * 512:(i + 1) * 512],
                                          pslist8[i], inc=s_ve)
                        SY.wait(s_ve)
                        for tc in range(4):
                            for g in range(2):
                                SY.dma_start(
                                    out=zy_hbm.ap()[tb * 512 + tc * 128:
                                                    tb * 512 + (tc + 1) * 128,
                                                    g * 512:(g + 1) * 512],
                                    in_=bfa[:, 4096 + (tc * 2 + g) * 512:
                                            4096 + (tc * 2 + g + 1) * 512],
                                    inc=s_st, n=16)
                    else:
                        for dq in range(DC):
                            VE.tensor_copy(bfa[:, dq * 512:(dq + 1) * 512],
                                           pslist8[dq], inc=s_ve)
                        SY.wait(s_ve)
                        for dq in range(DC):
                            SY.dma_start(
                                out=q2_hbm.ap()[dq * 128:(dq + 1) * 128,
                                                tb * 512:(tb + 1) * 512],
                                in_=bfa[:, dq * 512:(dq + 1) * 512],
                                inc=s_st, n=16)
                    TE.wait(s_ve)

            if L2STOP == 1:
                l2drain()
                return
            def l2_prep():
                VE.wait(s_ld, l2ld_mark)
                VE.tensor_copy(w1b, w1m, inc=s_ve)
                VE.tensor_copy(w2b, w2m, inc=s_ve)
                for koc in range(2):
                    TE.wait(s_ve)
                    TE.transpose(psE[:, 0:1],
                                 b1row[:, koc * 128:(koc + 1) * 128],
                                 ID_F[0:1, 0:1], inc=s_mm)
                    VE.wait(s_mm)
                    VE.tensor_copy(b1c[:, koc:koc + 1], psE[:, 0:1], inc=s_ve)
                for do in range(DC):
                    TE.wait(s_ve)
                    TE.transpose(psE[:, 0:1],
                                 b2row[:, do * 128:(do + 1) * 128],
                                 ID_F[0:1, 0:1], inc=s_mm)
                    VE.wait(s_mm)
                    VE.tensor_copy(b2c[:, do:do + 1], psE[:, 0:1], inc=s_ve)
                TE.wait(s_ve)
                TE.matmul(psD[:, 0:256], ONES_ROW, b1row, start=True,
                          stop=True, inc=s_mm)
                VE.wait(s_mm)
                VE.tensor_copy(b1bc, psD[:, 0:256], inc=s_ve)
                for g in range(2):
                    TE.wait(s_ve)
                    TE.matmul(psD[:, :], ONES_ROW,
                              b2row[:, g * 512:(g + 1) * 512],
                              start=True, stop=True, inc=s_mm)
                    VE.wait(s_mm)
                    VE.tensor_copy(b2bc[g * 512:(g + 1) * 512] if False else
                                   b2bc[:, g * 512:(g + 1) * 512],
                                   psD[:, :], inc=s_ve)
                for kc in range(2):
                    for doc in range(DC):
                        TE.wait(s_ve)
                        TE.transpose(psC[:, 0:128],
                                     w2m[:, kc * 1024 + doc * 128:
                                         kc * 1024 + (doc + 1) * 128],
                                     ID_F, inc=s_mm)
                        VE.wait(s_mm)
                        VE.tensor_copy(
                            w2dom[:, doc * 256 + kc * 128:
                                  doc * 256 + (kc + 1) * 128],
                            psC[:, 0:128], inc=s_ve)
                for dmc in range(DC):
                    k = dmc % 2
                    if dmc == 0:
                        GP.wait(s_ve)
                        GP.dma_start(
                            out=zstr[:, 0:1024].rearrange(
                                "q (dz j) -> q dz j", dz=8),
                            in_=l2["tk"][:, 0:128].rearrange(
                                "(dz p) j -> p dz j", p=128),
                            inc=s_gp, n=16)
                    if dmc < 7:
                        GP.wait(s_mm)
                        GP.dma_start(
                            out=zstr[:, ((dmc + 1) % 2) * 1024:
                                     ((dmc + 1) % 2 + 1) * 1024].rearrange(
                                "q (dz j) -> q dz j", dz=8),
                            in_=l2["tk"][:, (dmc + 1) * 128:
                                         (dmc + 2) * 128].rearrange(
                                "(dz p) j -> p dz j", p=128),
                            inc=s_gp, n=16)
                    pss = psB if dmc % 2 == 0 else psC
                    TE.wait(s_gp)
                    TE.wait(s_ve)
                    for dz in range(DC):
                        TE.matmul(pss[:, 0:256],
                                  zstr[:, k * 1024 + dz * 128:
                                       k * 1024 + (dz + 1) * 128],
                                  w1b[:, dz * 256:(dz + 1) * 256],
                                  start=(dz == 0), stop=(dz == 7), inc=s_mm)
                    VE.wait(s_mm)
                    VE.tensor_copy(p1[:, dmc * 256:(dmc + 1) * 256],
                                   pss[:, 0:256], inc=s_ve)

            l2_prep()

            for s in range(4):
                fcol = flc[:, s:s + 1]
                VE.wait(s_st)
                for tc in range(16):
                    tb, tci = tc // 4, tc % 4
                    SY.wait(s_ve)
                    SY.wait(s_st)
                    SY.dma_start(out=zstr[:, 1024:2048],
                                 in_=zy_hbm.ap()[tc * 128:(tc + 1) * 128, :],
                                 inc=s_ld, n=16)
                    TE.wait(s_ve)
                    for dm in range(DC):
                        TE.matmul(psB[:, 0:256],
                                  fm(bfc, dm, tb)[:, tci * 128:
                                                  (tci + 1) * 128],
                                  p1[:, dm * 256:(dm + 1) * 256],
                                  start=(dm == 0), stop=(dm == 7), inc=s_mm)
                    VE.wait(s_mm)
                    VE.tensor_tensor(tmpB[:, 0:256], psB[:, 0:256], b1bc,
                                     ALU.add, inc=s_ve)
                    SC.wait(s_ve)
                    SC.activation(tmpB[:, 256:512], tmpB[:, 0:256], AF.Gelu,
                                  inc=s_sc)
                    VE.wait(s_sc)
                    VE.tensor_copy(hgel(tc), tmpB[:, 256:512], inc=s_ve)
                    for kc in range(2):
                        TE.wait(s_ve)
                        TE.transpose(psC[:, 0:128],
                                     tmpB[:, 256 + kc * 128:
                                          256 + (kc + 1) * 128],
                                     ID_F, inc=s_mm)
                        VE.wait(s_mm)
                        VE.tensor_copy(zstr[:, kc * 128:(kc + 1) * 128],
                                       psC[:, 0:128], inc=s_ve)
                    VE.wait(s_ld)
                    for dog in range(2):
                        pss = psD if dog == 0 else psE
                        TE.wait(s_ve)
                        for kc in range(2):
                            TE.matmul(pss[:, :],
                                      zstr[:, kc * 128:(kc + 1) * 128],
                                      w2b[:, kc * 1024 + dog * 512:
                                          kc * 1024 + (dog + 1) * 512],
                                      start=(kc == 0), stop=(kc == 1),
                                      inc=s_mm)
                        VE.wait(s_mm)
                        VE.tensor_tensor(tmpA[:, dog * 512:(dog + 1) * 512],
                                         pss[:, :],
                                         zstr[:, 1024 + dog * 512:
                                              1024 + (dog + 1) * 512],
                                         ALU.add, inc=s_ve)
                        VE.tensor_tensor(tmpA[:, dog * 512:(dog + 1) * 512],
                                         tmpA[:, dog * 512:(dog + 1) * 512],
                                         b2bc[:, dog * 512:(dog + 1) * 512],
                                         ALU.add, inc=s_ve)
                    VE.tensor_scalar(zstr[:, 1024:2048], tmpA[:, 0:1024],
                                     fcol, None, ALU.mult, inc=s_ve)
                    SY.wait(s_ve)
                    SY.dma_start(out=g_hbm.ap()[tc * 128:(tc + 1) * 128, :],
                                 in_=zstr[:, 1024:2048], inc=s_st, n=16)
                    for doc in range(DC):
                        TE.wait(s_ve)
                        TE.transpose(psC[:, 0:128],
                                     tmpA[:, doc * 128:(doc + 1) * 128],
                                     ID_F, inc=s_mm)
                        VE.wait(s_mm)
                        VE.tensor_scalar(zstr[:, doc * 128:(doc + 1) * 128],
                                         psC[:, 0:128], fcol, None, ALU.mult,
                                         inc=s_ve)
                    TE.wait(s_ve)
                    for doc in range(DC):
                        TE.matmul(psB[:, 0:256],
                                  zstr[:, doc * 128:(doc + 1) * 128],
                                  w2dom[:, doc * 256:(doc + 1) * 256],
                                  start=(doc == 0), stop=(doc == 7), inc=s_mm)
                    SC.wait(s_ve)
                    SC.activation(tmpC[:, 0:256], tmpB[:, 0:256], AF.Abs,
                                  inc=s_sc)
                    SC.activation(tmpC[:, 256:512], tmpB[:, 0:256], AF.Sign,
                                  inc=s_sc)
                    VE.wait(s_sc)
                    VE.tensor_scalar(tmpA[:, 0:256], tmpC[:, 0:256], 0.01,
                                     None, ALU.max, inc=s_ve)
                    VE.reciprocal(tmpA[:, 0:256], tmpA[:, 0:256], inc=s_ve)
                    VE.tensor_mul(tmpA[:, 256:512], hgel(tc), tmpA[:, 0:256],
                                  inc=s_ve)
                    VE.tensor_mul(tmpA[:, 256:512], tmpA[:, 256:512],
                                  tmpC[:, 256:512], inc=s_ve)
                    VE.tensor_scalar(tmpA[:, 512:768], tmpB[:, 0:256],
                                     0.3989422804014327, 0.5, ALU.mult,
                                     ALU.add, inc=s_ve)
                    VE.tensor_scalar(tmpA[:, 768:1024], tmpC[:, 0:256], 0.01,
                                     None, ALU.is_gt, inc=s_ve)
                    VE.tensor_sub(tmpA[:, 256:512], tmpA[:, 256:512],
                                  tmpA[:, 512:768], inc=s_ve)
                    VE.tensor_mul(tmpA[:, 256:512], tmpA[:, 256:512],
                                  tmpA[:, 768:1024], inc=s_ve)
                    VE.tensor_add(tmpA[:, 512:768], tmpA[:, 512:768],
                                  tmpA[:, 256:512], inc=s_ve)
                    SC.wait(s_ve)
                    SC.activation(tmpC[:, 0:256], tmpB[:, 0:256], AF.Square,
                                  inc=s_sc)
                    SC.activation(tmpC[:, 256:512], tmpC[:, 0:256], AF.Exp,
                                  scale=qcn[:, 8:9], inc=s_sc)
                    VE.wait(s_sc)
                    VE.tensor_mul(tmpA[:, 0:256], tmpB[:, 0:256],
                                  tmpC[:, 256:512], inc=s_ve)
                    VE.scalar_tensor_tensor(tmpA[:, 0:256], tmpA[:, 0:256],
                                            0.3989422804014327,
                                            tmpA[:, 512:768], ALU.mult,
                                            ALU.add, inc=s_ve)
                    VE.wait(s_mm)
                    VE.tensor_mul(dpre(tc), psB[:, 0:256], tmpA[:, 0:256],
                                  inc=s_ve)
                    TE.wait(s_ve)

                for tc in range(16):
                    k = tc % 2
                    if tc == 0:
                        SY.wait(s_ve)
                        SY.wait(s_mm)
                        SY.wait(s_st)
                        SY.dma_start(out=zstr[:, 0:1024],
                                     in_=g_hbm.ap()[0:128, :], inc=s_ld, n=16)
                    if tc < 15:
                        SY.wait(s_mm)
                        SY.dma_start(
                            out=zstr[:, ((tc + 1) % 2) * 1024:
                                     ((tc + 1) % 2) * 1024 + 1024],
                            in_=g_hbm.ap()[(tc + 1) * 128:(tc + 2) * 128, :],
                            inc=s_ld, n=16)
                    gt = zstr[:, k * 1024:(k + 1) * 1024]
                    TE.wait(s_ld)
                    for koc in range(2):
                        for dog in range(2):
                            TE.matmul(psA[:, koc * 1024 + dog * 512:
                                          koc * 1024 + (dog + 1) * 512],
                                      hgel(tc)[:, koc * 128:(koc + 1) * 128],
                                      gt[:, dog * 512:(dog + 1) * 512],
                                      start=(tc == 0), stop=(tc == 15),
                                      inc=s_mm)
                    for dog in range(2):
                        pss = psB if dog == 0 else psC
                        TE.matmul(pss[0:1, :], onebf[:, 0:1],
                                  gt[:, dog * 512:(dog + 1) * 512],
                                  start=(tc == 0), stop=(tc == 15), inc=s_mm)
                VE.wait(s_mm)
                for koc in range(2):
                    for dog in range(2):
                        VE.wait(s_st)
                        VE.tensor_copy(tmpA[:, 0:512],
                                       psA[:, koc * 1024 + dog * 512:
                                           koc * 1024 + (dog + 1) * 512],
                                       inc=s_ve)
                        SY.wait(s_ve)
                        SY.dma_start(
                            out=g_pub.ap()[koc * 128:(koc + 1) * 128,
                                           dog * 512:(dog + 1) * 512],
                            in_=tmpA[:, 0:512], inc=s_st, n=16)
                for dog in range(2):
                    pss = psB if dog == 0 else psC
                    VE.wait(s_st)
                    VE.tensor_copy(tmpB[0:1, 0:512], pss[0:1, :], inc=s_ve)
                    SY.wait(s_ve)
                    SY.dma_start(out=g_pub.ap()[512:513,
                                                dog * 512:(dog + 1) * 512],
                                 in_=tmpB[0:1, 0:512], inc=s_st, n=16)
                TE.wait(s_ve)
                for tc in range(16):
                    k = tc % 2
                    if tc == 0:
                        SY.wait(s_ve)
                        SY.wait(s_mm)
                        SY.wait(s_st)
                        SY.dma_start(out=zstr[:, 0:1024],
                                     in_=z_hbm.ap()[0:128, :], inc=s_ld, n=16)
                    if tc < 15:
                        SY.wait(s_mm)
                        SY.dma_start(
                            out=zstr[:, ((tc + 1) % 2) * 1024:
                                     ((tc + 1) % 2) * 1024 + 1024],
                            in_=z_hbm.ap()[(tc + 1) * 128:(tc + 2) * 128, :],
                            inc=s_ld, n=16)
                    zt = zstr[:, (tc % 2) * 1024:(tc % 2) * 1024 + 1024]
                    TE.wait(s_ld)
                    for dc in range(DC):
                        TE.matmul(psA[:, dc * 256:(dc + 1) * 256],
                                  zt[:, dc * 128:(dc + 1) * 128],
                                  dpre(tc), start=(tc == 0), stop=(tc == 15),
                                  inc=s_mm)
                    TE.matmul(psB[0:1, 0:256], onebf[:, 0:1], dpre(tc),
                              start=(tc == 0), stop=(tc == 15), inc=s_mm)
                VE.wait(s_mm)
                for dc in range(DC):
                    VE.wait(s_st)
                    VE.tensor_copy(tmpA[:, 0:256],
                                   psA[:, dc * 256:(dc + 1) * 256], inc=s_ve)
                    SY.wait(s_ve)
                    SY.dma_start(
                        out=g_pub.ap()[256 + 128 * (dc // 4):
                                       256 + 128 * (dc // 4) + 128,
                                       (dc % 4) * 256:(dc % 4 + 1) * 256],
                        in_=tmpA[:, 0:256], inc=s_st, n=16)
                VE.wait(s_st)
                VE.tensor_copy(tmpB[0:1, 0:256], psB[0:1, 0:256], inc=s_ve)
                SY.wait(s_ve)
                SY.dma_start(out=g_pub.ap()[513:514, 0:256],
                             in_=tmpB[0:1, 0:256], inc=s_st, n=16)
                GP.wait(s_ve, st_barrier())
                GP.collective_compute("AllReduce", ALU.add, replica_groups=RG,
                                      ins=[g_pub.ap()], outs=[g_ar.ap()],
                                      inc=s_cc)
                ar_mark = s_cc.v
                GP.wait(s_cc, ar_mark)
                for koc in range(2):
                    GP.wait(s_ve)
                    GP.dma_start(out=zstr[:, 0:1024],
                                 in_=g_ar.ap()[koc * 128:(koc + 1) * 128, :],
                                 inc=s_gp, n=16)
                    VE.wait(s_gp)
                    VE.scalar_tensor_tensor(
                        w2m[:, koc * 1024:(koc + 1) * 1024], zstr[:, 0:1024],
                        -LR, w2m[:, koc * 1024:(koc + 1) * 1024], ALU.mult,
                        ALU.add, inc=s_ve)
                for dc in range(DC):
                    GP.wait(s_ve)
                    GP.dma_start(
                        out=zstr[:, 0:256],
                        in_=g_ar.ap()[256 + 128 * (dc // 4):
                                      256 + 128 * (dc // 4) + 128,
                                      (dc % 4) * 256:(dc % 4 + 1) * 256],
                        inc=s_gp, n=16)
                    VE.wait(s_gp)
                    VE.scalar_tensor_tensor(
                        w1m[:, dc * 256:(dc + 1) * 256], zstr[:, 0:256],
                        -LR, w1m[:, dc * 256:(dc + 1) * 256], ALU.mult,
                        ALU.add, inc=s_ve)
                SY.wait(s_ve)
                SY.dma_start(out=tmpB[0:1, 0:1024], in_=g_ar.ap()[512:513, :],
                             inc=s_ld, n=16)
                VE.wait(s_ld)
                VE.scalar_tensor_tensor(b2row, tmpB[0:1, 0:1024], -LR, b2row,
                                        ALU.mult, ALU.add, inc=s_ve)
                SY.wait(s_ve)
                SY.dma_start(out=tmpB[0:1, 0:256],
                             in_=g_ar.ap()[513:514, 0:256], inc=s_ld, n=16)
                VE.wait(s_ld)
                VE.scalar_tensor_tensor(b1row, tmpB[0:1, 0:256], -LR, b1row,
                                        ALU.mult, ALU.add, inc=s_ve)
                l2_prep()
                f1col = fl1[:, s:s + 1]
                for tb in range(NTB):
                    SY.wait(s_ve)
                    SY.wait(s_mm)
                    for dq in range(DC):
                        SY.dma_start(
                            out=bfa[:, dq * 512:(dq + 1) * 512],
                            in_=q2_hbm.ap()[dq * 128:(dq + 1) * 128,
                                            tb * 512:(tb + 1) * 512],
                            inc=s_ld, n=16)
                    TE.wait(s_ld)
                    TE.wait(s_ve)
                    for koc in range(2):
                        pss = psD if koc == 0 else psE
                        for dq in range(DC):
                            TE.matmul(pss[:, :],
                                      w1b[:, dq * 256 + koc * 128:
                                          dq * 256 + (koc + 1) * 128],
                                      bfa[:, dq * 512:(dq + 1) * 512],
                                      start=(dq == 0), stop=(dq == 7),
                                      inc=s_mm)
                        SC.wait(s_mm)
                        SC.wait(s_ve)
                        SC.activation(zstr[:, koc * 512:(koc + 1) * 512],
                                      pss[:, :], AF.Gelu,
                                      bias=b1c[:, koc:koc + 1], inc=s_sc)
                    TE.wait(s_sc)
                    for do in range(DC):
                        pss = psB if do % 2 == 0 else psC
                        TE.wait(s_ve)
                        for koc in range(2):
                            TE.matmul(pss[:, :],
                                      w2b[:, koc * 1024 + do * 128:
                                          koc * 1024 + (do + 1) * 128],
                                      zstr[:, koc * 512:(koc + 1) * 512],
                                      start=(koc == 0), stop=(koc == 1),
                                      inc=s_mm)
                        VE.wait(s_mm)
                        VE.tensor_scalar(tmpB[:, 0:512], pss[:, :],
                                         b2c[:, do:do + 1], None, ALU.add,
                                         inc=s_ve)
                        VE.tensor_tensor(tmpB[:, 0:512], tmpB[:, 0:512],
                                         bfa[:, do * 512:(do + 1) * 512],
                                         ALU.add, inc=s_ve)
                        VE.tensor_scalar(tmpB[:, 0:512], tmpB[:, 0:512],
                                         f1col, None, ALU.mult, inc=s_ve)
                        VE.tensor_add(fm(x32, do, tb), fm(x32, do, tb),
                                      tmpB[:, 0:512], inc=s_ve)

            if L2STOP == 2:
                l2drain()
                return
            wq2 = bfb[:, 0:1024]
            wk2 = bfb[:, 1024:2048]
            wv2 = bfb[:, 2048:3072]
            qgn = bfb[:, 3072:3076]
            mposT = vecs[:, 256:384]
            mnegT = vecs[:, 384:512]
            wproj_sb = bfa[0:64, 8832:10880]
            vaug_p = lambda h, ck: bfa[:, h * 2080 + ck * 65:
                                       h * 2080 + ck * 65 + 65]
            vaug_n = lambda h, ck: bfa[:, 4160 + h * 2080 + ck * 65:
                                       4160 + h * 2080 + ck * 65 + 65]
            po_t2 = bfa[:, 8320:8832]
            slab2 = lambda dc: bfc[:, dc * 512:(dc + 1) * 512]
            E2 = [bfc[:, 13312 + k * 512: 13312 + (k + 1) * 512]
                  for k in range(4)]
            k2_sb = bfc[:, 5120:9216]
            q2_sb = bfc[:, 9216:13312]

            GP.wait(s_mm)
            GP.wait(s_ve)
            GP.dma_start(out=wq2.rearrange("q (dc j) -> q dc j", dc=8),
                         in_=l2["wq"].rearrange("(dc p) j -> p dc j", p=128),
                         inc=s_gp, n=16)
            GP.dma_start(out=wk2.rearrange("q (dc j) -> q dc j", dc=8),
                         in_=l2["wk"].rearrange("(dc p) j -> p dc j", p=128),
                         inc=s_gp, n=16)
            GP.dma_start(out=wv2.rearrange("q (dc j) -> q dc j", dc=8),
                         in_=l2["wv"].rearrange("(dc p) j -> p dc j", p=128),
                         inc=s_gp, n=16)
            GP.dma_start(out=wproj_sb, in_=l2["wproj"], inc=s_gp, n=16)
            l2fw_mark = s_gp.v
            SY.wait(s_ve)
            SY.dma_start(out=tmpA[:, 0:4], in_=l2["qg"], inc=s_ld, n=16)
            SC.wait(s_ld)
            SC.activation(tmpC[:, 0:4], tmpA[:, 0:4], AF.Square, inc=s_sc)
            TE.wait(s_sc)
            TE.wait(s_ve)
            TE.matmul(psB[0:1, 0:4], ONES_COL, tmpC[:, 0:4], start=True,
                      stop=True, inc=s_mm)
            VE.wait(s_mm)
            VE.reciprocal(tmpB[0:1, 0:4], psB[0:1, 0:4], inc=s_ve)
            SC.wait(s_ve)
            SC.activation(tmpB[0:1, 4:8], tmpB[0:1, 0:4], AF.Sqrt, inc=s_sc)
            TE.wait(s_sc)
            TE.matmul(psD[:, 0:4], ONES_ROW, tmpB[0:1, 4:8], start=True,
                      stop=True, inc=s_mm)
            VE.wait(s_mm)
            VE.tensor_mul(qgn, tmpA[:, 0:4], psD[:, 0:4], inc=s_ve)

            if L2STOP in (24, 25):
                if L2STOP == 24:
                    for _ in range(660):
                        TE.matmul(psB[0:1, 0:4], ONES_COL, tmpC[:, 0:4],
                                  start=True, stop=True, inc=s_mm)
                    for _ in range(600):
                        VE.tensor_copy(tmpB[:, 0:1], ONES_COL, inc=s_ve)
                l2drain()
                return
            TE.wait(s_gp, l2fw_mark)
            for seq in range(4):
                for tseg in range(8):
                    slot = 2 * seq + tseg // 4
                    col = (tseg % 4) * 512
                    SY.wait(s_cc, ag2_mark)
                    SY.wait(s_mm)
                    SY.wait(s_ve)
                    for dc in range(DC):
                        SY.dma_start(
                            out=slab2(dc),
                            in_=xn_ag.ap()[slot * DM + dc * 128:
                                           slot * DM + (dc + 1) * 128,
                                           col:col + 512], inc=s_ld, n=16)
                    if L2STOP == 21:
                        continue
                    TE.wait(s_ld)
                    TE.wait(s_ve)
                    for dc in range(DC):
                        TE.matmul(psD[:, :], wq2[:, dc * 128:(dc + 1) * 128],
                                  slab2(dc), start=(dc == 0),
                                  stop=(dc == DC - 1), inc=s_mm)
                    VE.wait(s_mm)
                    VE.wait(s_st)
                    VE.tensor_copy(E2[0], psD[:, :], inc=s_ve)
                    VE.tensor_copy(fpw[:, 0:512], psD[:, :], inc=s_ve)
                    SC.wait(s_ve)
                    SC.activation(tmpC[:, 0:512], fpw[:, 0:512], AF.Square,
                                  inc=s_sc)
                    if L2STOP in (20, 22):
                        continue
                    for h in range(2):
                        TE.wait(s_ve)
                        if L2STOP == 26:
                            TE.matmul(psB[0:1, :], half[:, h:h + 1],
                                      tmpC[:, 0:512], start=True, stop=True,
                                      inc=s_mm)
                        else:
                            TE.matmul(psB[0:1, :], qgn[:, 2 * h:2 * h + 1],
                                      E2[0], start=True, stop=True, inc=s_mm)
                        TE.wait(s_sc)
                        TE.matmul(psC[0:1, :], half[:, h:h + 1],
                                  tmpC[:, 0:512], start=True, stop=True,
                                  inc=s_mm)
                        VE.wait(s_mm)
                        VE.reciprocal(tmpB[0:1, 0:512], psC[0:1, :], inc=s_ve)
                        SC.wait(s_ve)
                        SC.activation(tmpB[0:1, 512:1024], tmpB[0:1, 0:512],
                                      AF.Sqrt, inc=s_sc)
                        VE.wait(s_sc)
                        if h == 0:
                            VE.tensor_mul(tmpA[0:1, 0:512], psB[0:1, :],
                                          tmpB[0:1, 512:1024], inc=s_ve)
                        else:
                            VE.tensor_mul(tmpB[0:1, 0:512], psB[0:1, :],
                                          tmpB[0:1, 512:1024], inc=s_ve)
                            VE.tensor_add(tmpA[0:1, 0:512], tmpA[0:1, 0:512],
                                          tmpB[0:1, 0:512], inc=s_ve)
                        TE.wait(s_ve)
                    g0 = seq * 4096 + tseg * 512
                    if L2STOP not in (21, 22, 23):
                        SY.wait(s_ve)
                        SY.dma_start(out=sim_pub.ap()[0:1, g0:g0 + 512],
                                     in_=tmpA[0:1, 0:512], inc=s_st, n=16)
            if L2STOP in (20, 21, 22, 23, 26, 27):
                l2drain()
                return
            GP.wait(s_ve, st_barrier())
            GP.collective_compute("AllReduce", ALU.add, replica_groups=RG,
                                  ins=[sim_pub.ap()], outs=[sim_ar.ap()],
                                  inc=s_cc)
            sim_mark = s_cc.v

            SY.wait(s_cc, sim_mark)
            SY.dma_start(out=tmpA[:, 0:128],
                         in_=sim_ar.ap().rearrange("o (p j) -> p (o j)",
                                                   p=128), inc=s_ld, n=16)
            VE.wait(s_ld)
            if L2STOP == 28:
                l2drain()
                return
            VE.scalar_tensor_tensor(tmpB[:, 0:128], regt[:, :], -1e30,
                                    tmpA[:, 0:128], ALU.mult, ALU.add,
                                    inc=s_ve)
            VE.reduce_max(tmpB[:, 128:129], tmpB[:, 0:128], AX, inc=s_ve)
            pmax_to_scalar(tmpB[:, 128:129], tmpB[0:1, 130:131])
            VE.scalar_tensor_tensor(tmpB[:, 0:128], regt[:, :], 1e30,
                                    tmpA[:, 0:128], ALU.mult, ALU.add,
                                    inc=s_ve)
            VE.tensor_reduce(tmpB[:, 129:130], tmpB[:, 0:128], AX, ALU.min,
                             inc=s_ve)
            TE.wait(s_ve)
            TE.transpose(psE[0:1, 0:128], tmpB[:, 129:130], ID_F, inc=s_mm)
            VE.wait(s_mm)
            VE.tensor_reduce(tmpB[0:1, 131:132], psE[0:1, 0:128], AX, ALU.min,
                             inc=s_ve)
            VE.tensor_sub(tmpB[0:1, 132:133], tmpB[0:1, 130:131],
                          tmpB[0:1, 131:132], inc=s_ve)
            VE.scalar_tensor_tensor(tmpB[0:1, 132:133], tmpB[0:1, 132:133],
                                    0.9, tmpB[0:1, 131:132], ALU.mult,
                                    ALU.add, inc=s_ve)
            thb = vecs[:, 230:231]
            bcast_col(tmpB[0:1, 132:133], thb)
            VE.tensor_scalar(tmpB[:, 0:128], tmpA[:, 0:128], thb, None,
                             ALU.is_gt, inc=s_ve)
            VE.tensor_max(tmpA[:, 128:256], tmpB[:, 0:128], regt[:, :], inc=s_ve)
            VE.tensor_scalar(tmpA[:, 256:384], tmpB[:, 0:128], -1.0, 1.0,
                             ALU.mult, ALU.add, inc=s_ve)
            VE.tensor_max(tmpA[:, 256:384], tmpA[:, 256:384], regt[:, :], inc=s_ve)
            VE.tensor_mul(tmpA[:, 384:512], tmpB[:, 0:128], regt[:, :], inc=s_ve)
            VE.tensor_sub(tmpA[:, 384:512], tmpB[:, 0:128], tmpA[:, 384:512],
                          inc=s_ve)
            VE.scalar_tensor_tensor(tmpA[:, 384:512], regt[:, :], 0.5,
                                    tmpA[:, 384:512], ALU.mult, ALU.add,
                                    inc=s_ve)
            SY.wait(s_ve)
            SY.dma_start(out=mq_hbm.ap().rearrange("o (p j) -> p (o j)",
                                                    p=128),
                         in_=tmpA[:, 384:512], inc=s_st, n=16)
            TE.wait(s_ve)
            TE.transpose(psB[:, 0:128], tmpA[:, 128:256], ID_F, inc=s_mm)
            VE.wait(s_mm)
            VE.tensor_copy(mposT, psB[:, 0:128], inc=s_ve)
            TE.wait(s_ve)
            TE.transpose(psB[:, 0:128], tmpA[:, 256:384], ID_F, inc=s_mm)
            VE.wait(s_mm)
            VE.tensor_copy(mnegT, psB[:, 0:128], inc=s_ve)

            if L2STOP == 3:
                l2drain()
                return
            VE.wait(s_st)
            for seq in range(4):
                for tseg in range(8):
                    slot = 2 * seq + tseg // 4
                    col = (tseg % 4) * 512
                    SY.wait(s_mm)
                    SY.wait(s_ve)
                    for dc in range(DC):
                        SY.dma_start(
                            out=slab2(dc),
                            in_=xn_ag.ap()[slot * DM + dc * 128:
                                           slot * DM + (dc + 1) * 128,
                                           col:col + 512], inc=s_ld, n=16)
                    TE.wait(s_ld)
                    TE.wait(s_ve)
                    for dc in range(DC):
                        TE.matmul(psD[:, :], wq2[:, dc * 128:(dc + 1) * 128],
                                  slab2(dc), start=(dc == 0),
                                  stop=(dc == DC - 1), inc=s_mm)
                    for dc in range(DC):
                        TE.matmul(psE[:, :], wk2[:, dc * 128:(dc + 1) * 128],
                                  slab2(dc), start=(dc == 0),
                                  stop=(dc == DC - 1), inc=s_mm)
                    for tk in range(4):
                        pss = psB if tk % 2 == 0 else psC
                        for dc in range(DC):
                            TE.matmul(pss[:, 0:128],
                                      slab2(dc)[:, tk * 128:(tk + 1) * 128],
                                      wv2[:, dc * 128:(dc + 1) * 128],
                                      start=(dc == 0), stop=(dc == DC - 1),
                                      inc=s_mm)
                        ck = tseg * 4 + tk
                        gck = seq * 32 + ck
                        VE.wait(s_mm)
                        for h in range(2):
                            VE.tensor_scalar(
                                vaug_p(h, ck)[:, 0:64],
                                pss[:, h * 64:h * 64 + 64],
                                mposT[:, gck:gck + 1], None, ALU.mult,
                                inc=s_ve)
                            VE.tensor_scalar(
                                vaug_n(h, ck)[:, 0:64],
                                pss[:, h * 64:h * 64 + 64],
                                mnegT[:, gck:gck + 1], None, ALU.mult,
                                inc=s_ve)
                            VE.tensor_copy(vaug_p(h, ck)[:, 64:65],
                                           mposT[:, gck:gck + 1], inc=s_ve)
                            VE.tensor_copy(vaug_n(h, ck)[:, 64:65],
                                           mnegT[:, gck:gck + 1], inc=s_ve)
                        if tk >= 1:
                            TE.wait(s_ve)
                    VE.wait(s_mm)
                    VE.tensor_copy(q2_sb[:, tseg * 512:(tseg + 1) * 512],
                                   psD[:, :], inc=s_ve)
                    VE.tensor_copy(k2_sb[:, tseg * 512:(tseg + 1) * 512],
                                   psE[:, :], inc=s_ve)
                for h in range(2):
                    VE.tensor_copy(qm[h], zc2[0:1, :], inc=s_ve)
                    VE.tensor_copy(km[h], zc2[0:1, :], inc=s_ve)
                for part in range(8):
                    for (buf, acc) in ((q2_sb, qm), (k2_sb, km)):
                        SC.wait(s_ve)
                        SC.activation(tmpC[:, 0:512],
                                      buf[:, part * 512:(part + 1) * 512],
                                      AF.Square, inc=s_sc)
                        for h in range(2):
                            TE.wait(s_sc)
                            TE.wait(s_ve)
                            TE.matmul(psD[0:1, 0:512], half[:, h:h + 1],
                                      tmpC[:, 0:512], start=True, stop=True,
                                      inc=s_mm)
                            VE.wait(s_mm)
                            VE.reduce_max(t11a2, psD[0:1, 0:512], AX,
                                          inc=s_ve)
                            VE.tensor_tensor(acc[h], acc[h], t11a2, ALU.max,
                                             inc=s_ve)
                for h in range(2):
                    VE.tensor_mul(t11a2, qm[h], km[h], inc=s_ve)
                    SC.wait(s_ve)
                    SC.activation(t11b2, t11a2, AF.Sqrt, inc=s_sc)
                    VE.wait(s_sc)
                    VE.tensor_scalar(t11a2, t11b2, -0.125, None, ALU.mult,
                                     inc=s_ve)
                    bcast_col(t11a2, negc2[h])
                for qb in range(8):
                    for h in range(2):
                        av_marks = []
                        qk_marks = []
                        sc_marks = []
                        def qk2(kb):
                            if kb >= 4:
                                TE.wait(s_sc, sc_marks[kb - 4])
                            TE.matmul(psA[:, (kb % 4) * 512:
                                          (kb % 4) * 512 + 512],
                                      k2_sb[h * 64:h * 64 + 64,
                                            kb * 128:(kb + 1) * 128],
                                      q2_sb[h * 64:h * 64 + 64,
                                            qb * 512:(qb + 1) * 512],
                                      start=True, stop=True, inc=s_mm)
                            qk_marks.append(s_mm.v)
                        for kb in range(32):
                            if kb == 0:
                                qk2(0)
                            if kb < 31:
                                qk2(kb + 1)
                            SC.wait(s_mm, qk_marks[kb])
                            if kb >= 4:
                                SC.wait(s_mm, av_marks[kb - 4])
                            SC.activation(E2[kb % 4],
                                          psA[:, (kb % 4) * 512:
                                              (kb % 4) * 512 + 512],
                                          AF.Exp, scale=qcn[:, 6:7],
                                          bias=negc2[h], inc=s_sc)
                            sc_marks.append(s_sc.v)
                            TE.wait(s_sc, sc_marks[kb])
                            TE.matmul(psB[0:65, :], vaug_p(h, kb),
                                      E2[kb % 4], start=(kb == 0),
                                      stop=(kb == 31), inc=s_mm)
                            TE.matmul(psC[0:65, :], vaug_n(h, kb),
                                      E2[kb % 4], start=(kb == 0),
                                      stop=(kb == 31), inc=s_mm)
                            av_marks.append(s_mm.v)
                        for (psx, oofs) in ((psB, 0), (psC, 1024)):
                            VE.wait(s_mm)
                            VE.reciprocal(tmpB[0:1, 0:512], psx[64:65, :],
                                          inc=s_ve)
                            TE.wait(s_ve)
                            TE.matmul(psD[0:64, :], ONES_ROW[0:1, 0:64],
                                      tmpB[0:1, 0:512], start=True, stop=True,
                                      inc=s_mm)
                            VE.wait(s_mm)
                            VE.tensor_copy(tmpA[0:64, 0:512], psD[0:64, :],
                                           inc=s_ve)
                            VE.tensor_mul(bfo[:, oofs + h * 512:
                                              oofs + (h + 1) * 512],
                                          psx[0:64, :], tmpA[0:64, 0:512],
                                          inc=s_ve)
                            TE.wait(s_ve)
                    g0 = seq * 4096 + qb * 512
                    SY.wait(s_ve)
                    SY.dma_start(out=tmpB[0:1, 512:1024],
                                 in_=mq_hbm.ap()[0:1, g0:g0 + 512],
                                 inc=s_ld, n=16)
                    TE.wait(s_ld)
                    TE.matmul(psD[0:64, :], ONES_ROW[0:1, 0:64],
                              tmpB[0:1, 512:1024], start=True, stop=True,
                              inc=s_mm)
                    VE.wait(s_mm)
                    VE.tensor_copy(tmpA[0:64, 512:1024], psD[0:64, :],
                                   inc=s_ve)
                    for h in range(2):
                        VE.tensor_sub(bfo[:, h * 512:(h + 1) * 512],
                                      bfo[:, h * 512:(h + 1) * 512],
                                      bfo[:, 1024 + h * 512:
                                          1024 + (h + 1) * 512], inc=s_ve)
                        VE.tensor_mul(bfo[:, h * 512:(h + 1) * 512],
                                      bfo[:, h * 512:(h + 1) * 512],
                                      tmpA[0:64, 512:1024], inc=s_ve)
                        VE.tensor_add(bfo[:, h * 512:(h + 1) * 512],
                                      bfo[:, h * 512:(h + 1) * 512],
                                      bfo[:, 1024 + h * 512:
                                          1024 + (h + 1) * 512], inc=s_ve)
                    rank = 2 * seq + qb // 4
                    colq = (qb % 4) * 512
                    for do in range(DC):
                        pss = psD if do % 2 == 0 else psE
                        TE.wait(s_ve)
                        for h in range(2):
                            TE.matmul(pss[:, :],
                                      wproj_sb[:, h * 1024 + do * 128:
                                               h * 1024 + (do + 1) * 128],
                                      bfo[:, h * 512:(h + 1) * 512],
                                      start=(h == 0), stop=(h == 1),
                                      inc=s_mm)
                        VE.wait(s_mm)
                        VE.wait(s_st)
                        VE.tensor_copy(po_t2, pss[:, :], inc=s_ve)
                        SY.wait(s_ve)
                        SY.dma_start(
                            out=po_hbm.ap()[rank * DM + do * 128:
                                            rank * DM + (do + 1) * 128,
                                            colq:colq + 512],
                            in_=po_t2, inc=s_st, n=16)
            GP.wait(s_ve, st_barrier())
            GP.collective_compute("ReduceScatter", ALU.add, replica_groups=RG,
                                  ins=[po_hbm.ap()], outs=[rs_out.ap()],
                                  inc=s_cc)
            rs2_mark = s_cc.v
            SY.wait(s_cc, rs2_mark)
            SY.wait(s_mm)
            for dc in range(DC):
                SY.dma_start(out=fm(bfc, dc),
                             in_=rs_out.ap()[dc * 128:(dc + 1) * 128, :],
                             inc=s_ld, n=16)
            VE.wait(s_ld)
            for dc in range(DC):
                for tb in range(NTB):
                    VE.tensor_scalar(tmpB[:, 0:512], fm(bfc, dc, tb),
                                     bproj_sb[:, dc:dc + 1], None, ALU.add,
                                     inc=s_ve)
                    VE.tensor_add(fm(x32, dc, tb), fm(x32, dc, tb),
                                  tmpB[:, 0:512], inc=s_ve)

        def emit_l3():
            _vs[0] = 0
            bout_sb = vecs[:, 96:104]
            qm3 = [vecs[0:1, 104:105], vecs[0:1, 105:106]]
            km3 = [vecs[0:1, 106:107], vecs[0:1, 107:108]]
            t11a3 = vecs[0:1, 108:109]
            t11b3 = vecs[0:1, 109:110]
            negc3 = [vecs[:, 110:111], vecs[:, 111:112]]
            zc3 = vecs[:, 112:113]
            VE.memset(zc3, 0.0, inc=s_ve)

            emit_layernorm(x32, bfc)
            ln3_ve = s_ve.v
            SY.wait(s_ve, ln3_ve)
            for dc in range(DC):
                SY.dma_start(out=xn_pub.ap()[dc * 128:(dc + 1) * 128, :],
                             in_=fm(bfc, dc), inc=s_st, n=16)
            GP.wait(s_ve, st_barrier())
            GP.collective_compute("AllGather", ALU.bypass, replica_groups=RG,
                                  ins=[xn_pub.ap()], outs=[xn_ag.ap()],
                                  inc=s_cc)
            ag3_mark = s_cc.v

            wq3 = bfb[:, 0:1024]
            wk3 = bfb[:, 1024:2048]
            wv3 = bfb[:, 2048:3072]
            wout_sb = bfb[0:64, 3072:5120]
            GP.wait(s_mm)
            GP.dma_start(out=wq3.rearrange("q (dc j) -> q dc j", dc=8),
                         in_=l3["wq"].rearrange("(dc p) j -> p dc j", p=128),
                         inc=s_gp, n=16)
            GP.dma_start(out=wk3.rearrange("q (dc j) -> q dc j", dc=8),
                         in_=l3["wk"].rearrange("(dc p) j -> p dc j", p=128),
                         inc=s_gp, n=16)
            GP.dma_start(out=wv3.rearrange("q (dc j) -> q dc j", dc=8),
                         in_=l3["wv"].rearrange("(dc p) j -> p dc j", p=128),
                         inc=s_gp, n=16)
            GP.dma_start(out=wout_sb, in_=l3["wout"], inc=s_gp, n=16)
            SY.dma_start(out=bout_sb, in_=l3["bout"], inc=s_ld, n=16)
            l3w_mark = s_gp.v

            for g in range(32):
                k = g % 2
                GP.wait(s_mm)
                GP.dma_start(out=zstr[:, k * 1024:k * 1024 + 128],
                             in_=l3["rmbd"][g * 128:(g + 1) * 128, :],
                             inc=s_gp, n=16)
                GP.dma_start(out=bfa[:, k * 512:(k + 1) * 512],
                             in_=l3["rubd"][g * 128:(g + 1) * 128, :],
                             inc=s_gp, n=16)
                TE.wait(s_gp)
                TE.wait(s_ve)
                pss = psB if k == 0 else psC
                TE.matmul(pss[:, :], zstr[:, k * 1024:k * 1024 + 128],
                          bfa[:, k * 512:(k + 1) * 512], start=True,
                          stop=True, inc=s_mm)
                VE.wait(s_mm)
                VE.wait(s_st)
                VE.tensor_copy(bfa[:, 1024 + k * 512:1024 + (k + 1) * 512],
                               pss[:, :], inc=s_ve)
                SY.wait(s_ve)
                SY.dma_start(out=gg_hbm.ap()[g * 128:(g + 1) * 128, :],
                             in_=bfa[:, 1024 + k * 512:1024 + (k + 1) * 512],
                             inc=s_st, n=16)

            for tb in range(NTB):
                for g in range(32):
                    dj = g % 8
                    k = g % 2
                    if g == 0:
                        GP.wait(s_mm)
                        if tb == 0:
                            GP.wait(s_ve, st_barrier())
                        GP.dma_start(out=bfa[:, 0:512],
                                     in_=gg_hbm.ap()[0:128, :], inc=s_gp,
                                     n=16)
                    if g < 31:
                        GP.wait(s_mm)
                        GP.dma_start(
                            out=bfa[:, ((g + 1) % 2) * 512:
                                    ((g + 1) % 2) * 512 + 512],
                            in_=gg_hbm.ap()[(g + 1) * 128:(g + 2) * 128, :],
                            inc=(s_gp if (g + 1) % 2 == 0 else s_go), n=16)
                    idx = tb * 32 + g
                    GP.wait(s_mm)
                    GP.wait(s_st)
                    GP.dma_start(
                        out=zstr[:, (idx % 2) * 1024:(idx % 2) * 1024 + 1024],
                        in_=l3["wcT"][idx * 128:(idx + 1) * 128, :],
                        inc=s_pc, n=16)
                    SY.wait(s_pc)
                    SY.dma_start(
                        out=wc_bh.ap()[idx * 128:(idx + 1) * 128, :],
                        in_=zstr[:, (idx % 2) * 1024:(idx % 2) * 1024 + 1024],
                        inc=s_st, n=16)
                    TE.wait(s_gp if k == 0 else s_go)
                    for p in range(4):
                        pss = [psB, psC, psD, psE][p]
                        TE.wait(s_sc)
                        TE.matmul(pss[:, :],
                                  bfa[:, k * 512 + p * 128:
                                      k * 512 + (p + 1) * 128],
                                  fm(bfc, dj, tb), start=True, stop=True,
                                  inc=s_mm)
                    SC.wait(s_mm)
                    SC.wait(s_st)
                    for p in range(4):
                        pss = [psB, psC, psD, psE][p]
                        SC.activation(bfa[:, 1024 + p * 512:
                                          1024 + (p + 1) * 512],
                                      pss[:, :], AF.Gelu,
                                      bias=qcn[:, 9:10], inc=s_sc)
                    SY.wait(s_sc)
                    for p in range(4):
                        hc = g * 4 + p
                        SY.dma_start(
                            out=h3_hbm.ap()[hc * 128:(hc + 1) * 128,
                                            tb * 512:(tb + 1) * 512],
                            in_=bfa[:, 1024 + p * 512:1024 + (p + 1) * 512],
                            inc=s_st, n=16)

            pslist3 = [psA[:, 0:512], psA[:, 512:1024], psA[:, 1024:1536],
                       psA[:, 1536:2048], psB[:, :], psC[:, :], psD[:, :],
                       psE[:, :]]
            def l3_load(tb, hc):
                k = hc % 2
                SY.dma_start(out=bfa[:, k * 1024:(k + 1) * 1024],
                             in_=wc_bh.ap()[hc * 128:(hc + 1) * 128, :],
                             inc=(s_ld if k == 0 else s_lo), n=16)
                SY.dma_start(out=bfa[:, 2048 + k * 512:2048 + (k + 1) * 512],
                             in_=h3_hbm.ap()[hc * 128:(hc + 1) * 128,
                                             tb * 512:(tb + 1) * 512],
                             inc=(s_gp if k == 0 else s_go), n=16)
            for tb in range(NTB):
                for hc in range(128):
                    k = hc % 2
                    if hc == 0:
                        SY.wait(s_mm)
                        if tb == 0:
                            SY.wait(s_st)
                            TE.wait(s_sc)
                        l3_load(tb, 0)
                    if hc < 127:
                        SY.wait(s_mm)
                        l3_load(tb, hc + 1)
                    TE.wait(s_ld if k == 0 else s_lo)
                    TE.wait(s_gp if k == 0 else s_go)
                    for do in range(DC):
                        TE.matmul(pslist3[do],
                                  bfa[:, k * 1024 + do * 128:
                                      k * 1024 + (do + 1) * 128],
                                  bfa[:, 2048 + k * 512:2048 + (k + 1) * 512],
                                  start=(hc == 0), stop=(hc == 127),
                                  inc=s_mm)
                VE.wait(s_mm)
                for do in range(DC):
                    resadd(pslist3[do], do, tb)
                TE.wait(s_ve)

            slab3 = lambda dc: bfc[:, dc * 512:(dc + 1) * 512]
            E3 = [bfc[:, 13312 + k * 512: 13312 + (k + 1) * 512]
                  for k in range(4)]
            k3_sb = bfc[:, 5120:9216]
            q3_sb = bfc[:, 9216:13312]
            vaug3 = lambda h, ck: bfa[:, h * 2080 + ck * 65:
                                      h * 2080 + ck * 65 + 65]
            po_t3 = bfa[:, 8320:8832]
            TE.wait(s_gp)
            TE.wait(s_ve)
            for h in range(2):
                for ck in range(32):
                    VE.tensor_copy(vaug3(h, ck)[:, 64:65], ONES_COL,
                                   inc=s_ve)
            for seq in range(4):
                for tseg in range(8):
                    slot = 2 * seq + tseg // 4
                    col = (tseg % 4) * 512
                    SY.wait(s_cc, ag3_mark)
                    SY.wait(s_mm)
                    SY.wait(s_ve)
                    for dc in range(DC):
                        SY.dma_start(
                            out=slab3(dc),
                            in_=xn_ag.ap()[slot * DM + dc * 128:
                                           slot * DM + (dc + 1) * 128,
                                           col:col + 512], inc=s_ld, n=16)
                    TE.wait(s_ld)
                    TE.wait(s_ve)
                    for dc in range(DC):
                        TE.matmul(psD[:, :], wq3[:, dc * 128:(dc + 1) * 128],
                                  slab3(dc), start=(dc == 0),
                                  stop=(dc == DC - 1), inc=s_mm)
                    for dc in range(DC):
                        TE.matmul(psE[:, :], wk3[:, dc * 128:(dc + 1) * 128],
                                  slab3(dc), start=(dc == 0),
                                  stop=(dc == DC - 1), inc=s_mm)
                    for tk in range(4):
                        pss = psB if tk % 2 == 0 else psC
                        for dc in range(DC):
                            TE.matmul(pss[:, 0:128],
                                      slab3(dc)[:, tk * 128:(tk + 1) * 128],
                                      wv3[:, dc * 128:(dc + 1) * 128],
                                      start=(dc == 0), stop=(dc == DC - 1),
                                      inc=s_mm)
                        ck = tseg * 4 + tk
                        VE.wait(s_mm)
                        for h in range(2):
                            VE.tensor_copy(vaug3(h, ck)[:, 0:64],
                                           pss[:, h * 64:h * 64 + 64],
                                           inc=s_ve)
                        if tk >= 1:
                            TE.wait(s_ve)
                    VE.wait(s_mm)
                    VE.tensor_copy(q3_sb[:, tseg * 512:(tseg + 1) * 512],
                                   psD[:, :], inc=s_ve)
                    VE.tensor_copy(k3_sb[:, tseg * 512:(tseg + 1) * 512],
                                   psE[:, :], inc=s_ve)
                for h in range(2):
                    VE.tensor_copy(qm3[h], zc3[0:1, :], inc=s_ve)
                    VE.tensor_copy(km3[h], zc3[0:1, :], inc=s_ve)
                for part in range(8):
                    for (buf, acc) in ((q3_sb, qm3), (k3_sb, km3)):
                        SC.wait(s_ve)
                        SC.activation(tmpC[:, 0:512],
                                      buf[:, part * 512:(part + 1) * 512],
                                      AF.Square, inc=s_sc)
                        for h in range(2):
                            TE.wait(s_sc)
                            TE.wait(s_ve)
                            TE.matmul(psD[0:1, 0:512], half[:, h:h + 1],
                                      tmpC[:, 0:512], start=True, stop=True,
                                      inc=s_mm)
                            VE.wait(s_mm)
                            VE.reduce_max(t11a3, psD[0:1, 0:512], AX,
                                          inc=s_ve)
                            VE.tensor_tensor(acc[h], acc[h], t11a3, ALU.max,
                                             inc=s_ve)
                for h in range(2):
                    VE.tensor_mul(t11a3, qm3[h], km3[h], inc=s_ve)
                    SC.wait(s_ve)
                    SC.activation(t11b3, t11a3, AF.Sqrt, inc=s_sc)
                    VE.wait(s_sc)
                    VE.tensor_scalar(t11a3, t11b3, -0.125, None, ALU.mult,
                                     inc=s_ve)
                    bcast_col(t11a3, negc3[h])
                for qb in range(8):
                    for h in range(2):
                        av_marks = []
                        qk_marks = []
                        sc_marks = []
                        def qk3(kb):
                            if kb >= 4:
                                TE.wait(s_sc, sc_marks[kb - 4])
                            TE.matmul(psA[:, (kb % 4) * 512:
                                          (kb % 4) * 512 + 512],
                                      k3_sb[h * 64:h * 64 + 64,
                                            kb * 128:(kb + 1) * 128],
                                      q3_sb[h * 64:h * 64 + 64,
                                            qb * 512:(qb + 1) * 512],
                                      start=True, stop=True, inc=s_mm)
                            qk_marks.append(s_mm.v)
                        for kb in range(32):
                            if kb == 0:
                                qk3(0)
                            if kb < 31:
                                qk3(kb + 1)
                            SC.wait(s_mm, qk_marks[kb])
                            if kb >= 4:
                                SC.wait(s_mm, av_marks[kb - 4])
                            SC.activation(E3[kb % 4],
                                          psA[:, (kb % 4) * 512:
                                              (kb % 4) * 512 + 512],
                                          AF.Exp, scale=qcn[:, 6:7],
                                          bias=negc3[h], inc=s_sc)
                            sc_marks.append(s_sc.v)
                            TE.wait(s_sc, sc_marks[kb])
                            TE.matmul(psB[0:65, :], vaug3(h, kb),
                                      E3[kb % 4], start=(kb == 0),
                                      stop=(kb == 31), inc=s_mm)
                            av_marks.append(s_mm.v)
                        VE.wait(s_mm)
                        VE.reciprocal(tmpB[0:1, 0:512], psB[64:65, :],
                                      inc=s_ve)
                        TE.wait(s_ve)
                        TE.matmul(psC[0:64, :], ONES_ROW[0:1, 0:64],
                                  tmpB[0:1, 0:512], start=True, stop=True,
                                  inc=s_mm)
                        VE.wait(s_mm)
                        VE.tensor_copy(tmpA[0:64, 0:512], psC[0:64, :],
                                       inc=s_ve)
                        VE.tensor_mul(bfo[:, h * 512:(h + 1) * 512],
                                      psB[0:64, :], tmpA[0:64, 0:512],
                                      inc=s_ve)
                        TE.wait(s_ve)
                    rank = 2 * seq + qb // 4
                    colq = (qb % 4) * 512
                    for do in range(DC):
                        pss = psD if do % 2 == 0 else psE
                        TE.wait(s_ve)
                        for h in range(2):
                            TE.matmul(pss[:, :],
                                      wout_sb[:, h * 1024 + do * 128:
                                              h * 1024 + (do + 1) * 128],
                                      bfo[:, h * 512:(h + 1) * 512],
                                      start=(h == 0), stop=(h == 1),
                                      inc=s_mm)
                        VE.wait(s_mm)
                        VE.wait(s_st)
                        VE.tensor_copy(po_t3, pss[:, :], inc=s_ve)
                        SY.wait(s_ve)
                        SY.dma_start(
                            out=po_hbm.ap()[rank * DM + do * 128:
                                            rank * DM + (do + 1) * 128,
                                            colq:colq + 512],
                            in_=po_t3, inc=s_st, n=16)
            GP.wait(s_ve, st_barrier())
            GP.collective_compute("ReduceScatter", ALU.add, replica_groups=RG,
                                  ins=[po_hbm.ap()], outs=[rs_out.ap()],
                                  inc=s_cc)
            rs3_mark = s_cc.v
            SY.wait(s_cc, rs3_mark)
            SY.wait(s_mm)
            for dc in range(DC):
                SY.dma_start(out=fm(bfc, dc),
                             in_=rs_out.ap()[dc * 128:(dc + 1) * 128, :],
                             inc=s_ld, n=16)
            VE.wait(s_ld)
            for dc in range(DC):
                for tb in range(NTB):
                    VE.tensor_scalar(tmpB[:, 0:512], fm(bfc, dc, tb),
                                     bout_sb[:, dc:dc + 1], None, ALU.add,
                                     inc=s_ve)
                    VE.tensor_add(fm(x32, dc, tb), fm(x32, dc, tb),
                                  tmpB[:, 0:512], inc=s_ve)


        emit_l0()
        if nlayers >= 2:
            emit_l1()
        if nlayers >= 3:
            emit_l2()
        if nlayers >= 4:
            emit_l3()

        SY.wait(s_ve)
        SY.wait(s_mm)
        SY.wait(s_sc)
        SY.wait(s_gp)
        for dc in range(DC):
            SY.dma_start(out=out_fm[dc * 128:(dc + 1) * 128, :],
                         in_=fm(x32, dc), inc=s_st, n=16)
        SY.wait(s_st)

        block = es.enter_context(nc.Block())
        block.sync(lambda e: [fn(e) for fn in Q["sy"]])
        block.tensor(lambda e: [fn(e) for fn in Q["te"]])
        block.vector(lambda e: [fn(e) for fn in Q["ve"]])
        block.scalar(lambda e: [fn(e) for fn in Q["sc"]])
        block.gpsimd(lambda e: [fn(e) for fn in Q["gp"]])
    nc._qlens = {k: len(v) for k, v in Q.items()}
    nc._semvals = {n: s.v for n, s in zip(
        ("s_ld", "s_gp", "s_cc", "s_mm", "s_ve", "s_sc", "s_st"),
        (s_ld, s_gp, s_cc, s_mm, s_ve, s_sc, s_st))}
    return nc


def stage_core(c, I, nlayers=NLAYERS):
    x = np.asarray(I["L0_attn_x"]).reshape(16384, DM)
    d = {}
    d["x0_fm"] = np.ascontiguousarray(x[c * T:(c + 1) * T].T)
    d["L0_wd"] = np.asarray(I["L0_attn_Wd"])
    d["L0_bd"] = np.asarray(I["L0_attn_bd"]).reshape(DM, 1)
    d["L0_wb"] = np.asarray(I["L0_attn_Wb"])
    d["L0_bb"] = np.asarray(I["L0_attn_bb"]).reshape(16, 1)
    d["L0_wc"] = np.asarray(I["L0_attn_Wc"])
    d["L0_bc"] = np.asarray(I["L0_attn_bc"]).reshape(16, 1)
    d["L0_wo"] = np.asarray(I["L0_attn_Wo"])
    d["L0_bo"] = np.asarray(I["L0_attn_bo"]).reshape(DM, 1)
    d["L0_at"] = np.ascontiguousarray(np.asarray(I["L0_attn_A"]).T)
    d["L0_clipx"] = np.asarray(I["L0_ffn_clip_x"]).reshape(1, 1)
    d["L0_clipd"] = np.asarray(I["L0_ffn_clip_down"]).reshape(1, 1)
    d["L0_wg"] = np.asarray(I["L0_ffn_w_gate"])
    d["L0_wu"] = np.asarray(I["L0_ffn_w_up"])
    d["L0_wdn"] = np.asarray(I["L0_ffn_w_down"])
    sel = np.zeros((16, 8), np.float32)
    if c % 2 == 1:
        sel[:, c - 1] = 1.0
    d["sel_bc"] = sel
    hs = 128 * c
    WqT = np.ascontiguousarray(np.asarray(I["L1_attn_Wq"]).T)
    WkT = np.ascontiguousarray(np.asarray(I["L1_attn_Wk"]).T)
    WvT = np.ascontiguousarray(np.asarray(I["L1_attn_Wv"]).T)
    WoT = np.ascontiguousarray(np.asarray(I["L1_attn_Wo"]).T)
    d["L1_wq"] = np.ascontiguousarray(WqT[:, hs:hs + 128])
    d["L1_wk"] = np.ascontiguousarray(WkT[:, hs:hs + 128])
    d["L1_wv"] = np.ascontiguousarray(WvT[:, hs:hs + 128])
    d["L1_bq"] = np.asarray(I["L1_attn_bq"])[hs:hs + 128].reshape(128, 1)
    d["L1_bk"] = np.asarray(I["L1_attn_bk"])[hs:hs + 128].reshape(128, 1)
    d["L1_bv"] = np.asarray(I["L1_attn_bv"])[hs:hs + 128].reshape(1, 128)
    d["L1_wo"] = np.ascontiguousarray(
        np.concatenate([WoT[hs:hs + 64, :], WoT[hs + 64:hs + 128, :]], axis=1))
    d["L1_bo"] = np.ascontiguousarray(
        np.asarray(I["L1_attn_bo"]).reshape(8, 128).T)
    d["L1_w1T"] = np.ascontiguousarray(np.asarray(I["L1_ffn_w1"]).T)
    d["L1_b1"] = np.ascontiguousarray(
        np.asarray(I["L1_ffn_b1"]).reshape(32, 128).T)
    d["L1_w2T"] = np.ascontiguousarray(np.asarray(I["L1_ffn_w2"]).T)
    d["L1_b2"] = np.ascontiguousarray(
        np.asarray(I["L1_ffn_b2"]).reshape(8, 128).T)
    d["L2_tk"] = np.asarray(I["L2_attn_t_k"])
    d["L2_tkT"] = np.ascontiguousarray(np.asarray(I["L2_attn_t_k"]).T)
    d["L2_tvT"] = np.ascontiguousarray(np.asarray(I["L2_attn_t_v"]).T)
    d["L2_tqT"] = np.ascontiguousarray(np.asarray(I["L2_attn_t_q"]).T)
    d["L2_w1T"] = np.ascontiguousarray(np.asarray(I["L2_attn_W1"]).T)
    d["L2_b1"] = np.asarray(I["L2_attn_b1"]).reshape(1, 256)
    d["L2_w2T"] = np.ascontiguousarray(np.asarray(I["L2_attn_W2"]).T)
    d["L2_b2"] = np.asarray(I["L2_attn_b2"]).reshape(1, 1024)
    flc = np.zeros((128, 4), np.float32)
    fl1 = np.zeros((128, 4), np.float32)
    flc[:, c // 2] = 2.0 / (4096.0 * 1024.0)
    fl1[:, c // 2] = 1.0
    d["L2_flc"] = flc
    d["L2_fl1"] = fl1
    Wqkv = np.asarray(I["L2_ffn_W_qkv"])
    d["L2f_wq"] = np.ascontiguousarray(Wqkv[:, hs:hs + 128])
    d["L2f_wk"] = np.ascontiguousarray(Wqkv[:, 1024 + hs:1024 + hs + 128])
    d["L2f_wv"] = np.ascontiguousarray(Wqkv[:, 2048 + hs:2048 + hs + 128])
    Wp = np.asarray(I["L2_ffn_W_proj"])
    d["L2f_wproj"] = np.ascontiguousarray(
        np.concatenate([Wp[hs:hs + 64, :], Wp[hs + 64:hs + 128, :]], axis=1))
    d["L2f_bproj"] = np.ascontiguousarray(
        np.asarray(I["L2_ffn_b_proj"]).reshape(8, 128).T)
    qg = np.asarray(I["L2_ffn_g_info"])[0].reshape(-1, 16, 64)[0]
    qgp = np.ones((128, 4), np.float32)
    for hh in range(2):
        qgp[:, 2 * hh] = 0.0
        qgp[64 * hh:64 * hh + 64, 2 * hh] = qg[2 * c + hh]
    d["L2f_qg"] = qgp
    WqT3 = np.ascontiguousarray(np.asarray(I["L3_attn_Wq"]).T)
    WkT3 = np.ascontiguousarray(np.asarray(I["L3_attn_Wk"]).T)
    WvT3 = np.ascontiguousarray(np.asarray(I["L3_attn_Wv"]).T)
    WoT3 = np.ascontiguousarray(np.asarray(I["L3_attn_Wout"]).T)
    d["L3_wq"] = np.ascontiguousarray(WqT3[:, hs:hs + 128])
    d["L3_wk"] = np.ascontiguousarray(WkT3[:, hs:hs + 128])
    d["L3_wv"] = np.ascontiguousarray(WvT3[:, hs:hs + 128])
    d["L3_wout"] = np.ascontiguousarray(
        np.concatenate([WoT3[hs:hs + 64, :], WoT3[hs + 64:hs + 128, :]],
                       axis=1))
    d["L3_bout"] = np.ascontiguousarray(
        np.asarray(I["L3_attn_bout"]).reshape(8, 128).T)
    rm = np.asarray(I["L3_ffn_rail_maker"])
    ru = np.asarray(I["L3_ffn_rail_up_proj"])
    rmbd = np.zeros((32 * 128, 128), np.float32)
    rubd = np.zeros((32 * 128, 512), np.float32)
    for g in range(32):
        h3, dj = g // 8, g % 8
        for m in range(8):
            r = 8 * dj + m
            rmbd[g * 128 + 16 * m: g * 128 + 16 * m + 16,
                 16 * m:16 * m + 16] = rm[h3, r].T
            rubd[g * 128 + 16 * m: g * 128 + 16 * m + 16,
                 64 * m:64 * m + 64] = ru[h3, r].T
    d["L3_rmbd"] = rmbd
    d["L3_rubd"] = rubd
    d["L3_wcT"] = np.ascontiguousarray(np.asarray(I["L3_ffn_w_cproj"]).T)
    return d


_NC_CACHE = {}

def kernel(**inputs):
    if NLAYERS not in _NC_CACHE:
        _NC_CACHE[NLAYERS] = build(NLAYERS)
    nc = _NC_CACHE[NLAYERS]
    in_maps = [stage_core(c, inputs, NLAYERS) for c in range(NCORES)]
    res = run_bass_kernel_spmd(nc, in_maps, core_ids=list(range(NCORES)))
    out = np.concatenate([r["out_fm"].T for r in res.results], axis=0)
    return np.ascontiguousarray(out.reshape(4, 4096, DM)).astype(np.float32)
